# version 05
# baseline (speedup 1.0000x reference)
import math
from contextlib import ExitStack

import numpy as np
import ml_dtypes

import concourse.bass as bass
import concourse.mybir as mybir
from concourse.bass_utils import run_bass_kernel_spmd

F32 = mybir.dt.float32
BF16 = mybir.dt.bfloat16
AF = mybir.ActivationFunctionType
ALU = mybir.AluOpType
AX = mybir.AxisListType
BF = ml_dtypes.bfloat16

A_MAXPH = 6
A_P5MODE = 4
A_NCH = None
A_SKIP_P2 = False
A_SKIP_P4 = False
A_SKIP_P1 = False
T = 4096
D = 1024
NH = 16
HD = 64
NE = 16
CH = 1024
NCH = T // CH
NT = CH // 128
EPS = 1e-5
BIG = 1.0e4


class Sem:
    def __init__(self, nc, name):
        self.h = nc.alloc_semaphore(name)
        self.n = 0

    def inc(self, inst, k=1):
        inst.then_inc(self.h, k)
        self.n += k
        return self.n


_WRAP = {
    "activation", "sign", "square", "sqrt", "mul", "copy", "add", "memzero",
    "memset", "tensor_copy", "tensor_tensor", "tensor_add", "tensor_sub",
    "tensor_mul", "tensor_max", "tensor_scalar", "tensor_single_scalar",
    "tensor_scalar_add", "tensor_scalar_sub", "tensor_scalar_mul",
    "tensor_scalar_max", "tensor_scalar_min", "tensor_relu", "tensor_reduce",
    "reduce_max", "reduce_sum", "scalar_tensor_tensor", "tensor_tensor_scan",
    "select", "copy_predicated", "transpose", "reciprocal", "iota",
    "affine_select",
}


class SEng:

    def __init__(self, eng, sem):
        self._e = eng
        self._s = sem

    def __getattr__(self, name):
        attr = getattr(self._e, name)
        if name in _WRAP:
            def f(*a, **k):
                self._e.wait_ge(self._s.h, self._s.n)
                inst = attr(*a, **k)
                self._s.inc(inst)
                return inst
            return f
        return attr


class B:

    def __init__(self):
        nc = bass.Bass("TRN2", target_bir_lowering=False)
        self.nc = nc
        self.ctx = ExitStack()
        self.cur = None
        self.pe = nc.tensor
        self.sp = nc.sync
        self.dma = Sem(nc, "dma")
        self.mm = Sem(nc, "mm")
        self.ea = Sem(nc, "ea")
        self.ev = Sem(nc, "ev")
        self.ep = Sem(nc, "ep")
        self.act = SEng(nc.scalar, self.ea)
        self.dve = SEng(nc.vector, self.ev)
        self.pool = SEng(nc.gpsimd, self.ep)
        self.out = Sem(nc, "outs")
        self.pdma = Sem(nc, "pdma")
        self.cdma = Sem(nc, "cdma")
        self.xdma = Sem(nc, "xdma")
        self.xdma2 = Sem(nc, "xdma2")
        self.wd = [Sem(nc, "wd0"), Sem(nc, "wd1")]
        self.hd = [Sem(nc, "hd0"), Sem(nc, "hd1")]
        self.out2 = Sem(nc, "outs2")

    def dram(self, name, shape, dtype, kind):
        return self.nc.dram_tensor(name, list(shape), dtype, kind=kind).ap()

    def sb(self, name, shape, dtype):
        return (self.cur or self.ctx).enter_context(
            self.nc.sbuf_tensor(name, list(shape), dtype))

    def ps(self, name, shape, dtype=F32):
        return (self.cur or self.ctx).enter_context(
            self.nc.psum_tensor(name, list(shape), dtype))


def _ln_stats(b, src, col):
    b.dve.tensor_reduce(b.a_s1[:, col:col + 1], src, axis=AX.X, op=ALU.add)
    b.act.activation(b.a_sqscr[:], src, AF.Square, accum_out=b.a_ssq[:, col:col + 1])


def _ln_mr(b, cols):
    dve, act, ea, ev = b.dve, b.act, b.ea, b.ev
    s = cols
    dve.tensor_scalar_mul(b.a_mcol[:, s], b.a_s1[:, s], 1.0 / D)
    dve.tensor_scalar_mul(b.a_vcol[:, s], b.a_ssq[:, s], 1.0 / D)
    dve.tensor_tensor(b.a_rcol[:, s], b.a_mcol[:, s], b.a_mcol[:, s], ALU.mult)
    dve.tensor_tensor(b.a_vcol[:, s], b.a_vcol[:, s], b.a_rcol[:, s], ALU.subtract)
    act.wait_ge(ev.h, ev.n)
    act.activation(b.a_rcol[:, s], b.a_vcol[:, s], AF.Sqrt, bias=b.a_ecol[:])
    dve.wait_ge(ea.h, ea.n)
    dve.reciprocal(b.a_rcol[:, s], b.a_rcol[:, s])


def build_stage_a(b, io):
    CH = 512
    NCH = A_NCH if A_NCH else T // CH
    NT = CH // 128
    nc = b.nc
    pe, act, dve, pool, sp = b.pe, b.act, b.dve, b.pool, b.sp
    dma, mm, ea, ev, ep = b.dma, b.mm, b.ea, b.ev, b.ep

    xa = b.sb("a_x", [128, NT, D], F32)
    xn = b.sb("a_xn", [128, NT, D], BF16)
    xnT = b.sb("a_xnT", [128, 8, CH], BF16)
    qf = b.sb("a_qf", [128, 8, CH], BF16)
    kf = b.sb("a_kf", [128, 8, 128 + CH], BF16)
    vt = b.sb("a_vt", [128, 1 + NT, D], BF16)
    msgT = b.sb("a_msgT", [128, 8, CH], BF16)
    tau = b.sb("a_tau", [128, 2048], BF16)
    wq_sb = b.sb("a_wq", [128, 8, D], BF16)
    wk_sb = b.sb("a_wk", [128, 8, D], BF16)
    wv_sb = b.sb("a_wv", [128, 8, D], BF16)
    wo_sb = b.sb("a_wo", [128, 8, D], BF16)
    mask_sb = b.sb("a_mask", [128, 2048], BF16)
    id_sb = b.sb("a_id", [128, 128], BF16)
    grow = b.sb("a_grow", [1, D], F32)
    brow = b.sb("a_brow", [1, D], F32)
    ones1 = b.sb("a_ones1", [1, 128], F32)
    gbc = b.sb("a_gbc", [128, D], F32)
    bbc = b.sb("a_bbc", [128, D], F32)
    b.a_s1 = b.sb("a_s1", [128, 16], F32)
    b.a_ssq = b.sb("a_ssq", [128, 16], F32)
    b.a_mcol = b.sb("a_mcol", [128, 16], F32)
    b.a_vcol = b.sb("a_vcol", [128, 16], F32)
    b.a_rcol = b.sb("a_rcol", [128, 16], F32)
    b.a_sqscr = b.sb("a_sqscr", [128, D], BF16)
    b.a_ecol = b.sb("a_ecol", [128, 1], F32)
    ya = b.sb("a_ya", [128, D], F32)
    yb = b.sb("a_yb", [128, D], F32)

    pp = b.ps("a_pp", [128, D])
    pp2 = b.ps("a_pp2", [128, D])
    psc = b.ps("a_psc", [128, 2048])

    pool.memset(ones1[:], 1.0)
    pool.memset(b.a_ecol[:], EPS)
    pool.memset(kf[:], 0.0)
    pool.memset(vt[:], 0.0)
    pn = ep.n
    for dst, src in [(id_sb, io["ident"]), (mask_sb, io["maskAB"])]:
        sp.dma_start(out=dst[:], in_=src).then_inc(b.cdma.h, 16)
        b.cdma.n += 16
    sp.dma_start(out=grow[:], in_=io["gamma"].unsqueeze(0)).then_inc(b.cdma.h, 16)
    sp.dma_start(out=brow[:], in_=io["beta"].unsqueeze(0)).then_inc(b.cdma.h, 16)
    b.cdma.n += 32
    for wsb, wdr in [(wq_sb, io["wqT"]), (wk_sb, io["wkT"]), (wv_sb, io["wvT"]),
                     (wo_sb, io["woT"])]:
        pool.dma_start(out=wsb[:], in_=wdr.rearrange("(kc p) n -> p kc n", p=128)
                       ).then_inc(b.pdma.h, 16)
        b.pdma.n += 16
    w_loaded = b.pdma.n

    pe.wait_ge(b.cdma.h, b.cdma.n)
    pe.wait_ge(b.pdma.h, w_loaded)
    pe.wait_ge(ep.h, pn)
    for half in range(2):
        sl = slice(half * 512, (half + 1) * 512)
        pe.matmul(pp[:, sl], ones1[:], grow[:, sl], start=True, stop=True)
        last = pe.matmul(pp2[:, sl], ones1[:], brow[:, sl], start=True, stop=True)
    gb_mm = mm.inc(last)
    dve.wait_ge(mm.h, gb_mm)
    dve.wait_ge(ep.h, pn)
    dve.tensor_copy(gbc[:], pp[:])
    dve.tensor_copy(bbc[:], pp2[:])
    gb_ev = ev.n

    dve.wait_ge(b.cdma.h, b.cdma.n)
    act.wait_ge(b.cdma.h, b.cdma.n)
    free_pp = gb_ev
    free_pp2 = gb_ev
    free_psc_ea = 0
    free_tau_mm = 0

    for c in range(NCH):
        sp.wait_ge(b.out.h, b.out.n)
        sp.dma_start(out=xa[:], in_=io["x0"][c * CH:(c + 1) * CH, :]
                     .rearrange("(i p) d -> p i d", p=128)).then_inc(b.xdma.h, 16)
        b.xdma.n += 16
        x_loaded = b.xdma.n

        if A_MAXPH < 1:
            sp.wait_ge(b.xdma.h, b.xdma.n)
            sp.dma_start(out=io["x1"][c * CH:(c + 1) * CH, :]
                         .rearrange("(i p) d -> p i d", p=128), in_=xa[:]).then_inc(b.out.h, 16)
            b.out.n += 16
            continue
        dve.wait_ge(b.xdma.h, x_loaded)
        act.wait_ge(b.xdma.h, x_loaded)
        if A_SKIP_P1:
            pool.memset(xn[:], 0.01)
            dve.wait_ge(ep.h, ep.n)
        el_dummy = 0
        if not A_SKIP_P1:
            for i in range(NT):
                _ln_stats(b, xa[:, i, :], i)
            dve.wait_ge(ea.h, ea.n)
            _ln_mr(b, slice(0, NT))
            for i in range(NT):
                dve.tensor_scalar(xn[:, i, :], xa[:, i, :], b.a_mcol[:, i:i + 1],
                                  b.a_rcol[:, i:i + 1], ALU.subtract, ALU.mult)
        ev_xn = ev.n

        if A_MAXPH < 2:
            sp.wait_ge(ev.h, ev.n)
            sp.dma_start(out=io["x1"][c * CH:(c + 1) * CH, :]
                         .rearrange("(i p) d -> p i d", p=128), in_=xa[:]).then_inc(b.out.h, 16)
            b.out.n += 16
            continue
        if A_SKIP_P2:
            pool.memset(xnT[:], 0.01)
            dve.wait_ge(ep.h, ep.n)
            dve.tensor_copy(b.a_s1[:, 15:16], b.a_s1[:, 15:16])
        else:
            ptv = [pp[:, 0:64].bitcast(BF16), pp2[:, 0:64].bitcast(BF16)]
            free_pt = [free_pp, free_pp2]
            pe.wait_ge(ev.h, ev_xn)
            for u in range(NT * 8):
                i, fc = divmod(u, 8)
                k = u % 2
                pe.wait_ge(ev.h, free_pt[k])
                t_mm = mm.inc(pe.transpose(ptv[k], xn[:, i, fc * 128:(fc + 1) * 128], id_sb[:]))
                dve.wait_ge(mm.h, t_mm)
                dve.tensor_copy(xnT[:, fc, i * 128:(i + 1) * 128], ptv[k])
                free_pt[k] = ev.n
            free_pp, free_pp2 = free_pt

        if c > 0:
            dve.tensor_copy(kf[:, :, 0:128], kf[:, :, CH:CH + 128])
            dve.tensor_copy(vt[:, 0, :], vt[:, NT, :])
        else:
            dve.wait_ge(ep.h, pn)
        ev_carry = ev.n
        pe.wait_ge(ev.h, ev_carry)

        if A_MAXPH < 4:
            sp.wait_ge(ev.h, ev.n)
            sp.dma_start(out=io["x1"][c * CH:(c + 1) * CH, :]
                         .rearrange("(i p) d -> p i d", p=128), in_=xa[:]).then_inc(b.out.h, 16)
            b.out.n += 16
            continue
        if A_SKIP_P4:
            pool.memset(qf[:], 0.01)
            pool.memset(kf[:], 0.01)
            pool.memset(vt[:], 0.01)
            pe.wait_ge(ep.h, ep.n)
        pbuf = [pp, pp2]
        pfree = [free_pp, free_pp2]
        u = 0
        for wsb, kind in ([] if A_SKIP_P4 else [(wq_sb, "q"), (wk_sb, "k"), (wv_sb, "v")]):
            nm = NT if kind == "v" else 8
            for m in range(nm):
                k = u % 2
                u += 1
                ppu = pbuf[k]
                pe.wait_ge(ev.h, pfree[k])
                if kind == "v":
                    for kc in range(8):
                        for half in range(2):
                            sl = slice(half * 512, (half + 1) * 512)
                            last = pe.matmul(ppu[:, sl], xnT[:, kc, m * 128:(m + 1) * 128],
                                             wv_sb[:, kc, sl], start=(kc == 0), stop=(kc == 7))
                else:
                    for kc in range(8):
                        last = pe.matmul(ppu[:, 0:CH], wsb[:, kc, m * 128:(m + 1) * 128],
                                         xnT[:, kc, :], start=(kc == 0), stop=(kc == 7))
                m_mm = mm.inc(last)
                dve.wait_ge(mm.h, m_mm)
                if kind == "q":
                    dve.tensor_copy(qf[:, m, :], ppu[:, 0:CH])
                elif kind == "k":
                    dve.tensor_copy(kf[:, m, 128:128 + CH], ppu[:, 0:CH])
                else:
                    dve.tensor_copy(vt[:, 1 + m, :], ppu[:])
                pfree[k] = ev.n
        free_pp, free_pp2 = pfree
        pe.wait_ge(ev.h, ev.n)

        if A_MAXPH < 5:
            sp.wait_ge(ev.h, ev.n)
            sp.dma_start(out=io["x1"][c * CH:(c + 1) * CH, :]
                         .rearrange("(i p) d -> p i d", p=128), in_=xa[:]).then_inc(b.out.h, 16)
            b.out.n += 16
            continue
        for qt in range(NT):
            for half in range(2):
                if A_P5MODE == 0:
                    pe.wait_ge(ev.h, ev.n)
                else:
                    pe.wait_ge(ea.h, free_psc_ea)
                for hh in [0, 2, 4, 6, 1, 3, 5, 7]:
                    h = half * 8 + hh
                    fc, r0 = h // 2, 64 * (h % 2)
                    for piece in range(2):
                        kb = qt + piece
                        c0 = hh * 256 + piece * 128
                        last = pe.matmul(psc[:, c0:c0 + 128],
                                         kf[r0:r0 + 64, fc, kb * 128:(kb + 1) * 128],
                                         qf[r0:r0 + 64, fc, qt * 128:(qt + 1) * 128],
                                         start=True, stop=True)
                sc_mm = mm.inc(last)
                act.wait_ge(mm.h, sc_mm)
                act.wait_ge(mm.h, free_tau_mm)
                if A_P5MODE == 0:
                    dve.tensor_copy(tau[:], psc[:])
                    free_psc_ea = 0
                    continue
                act.activation(tau[:], psc[:], AF.Sigmoid, scale=0.125)
                free_psc_ea = ea.n
                if A_P5MODE < 2:
                    continue
                dve.wait_ge(ea.h, free_psc_ea)
                dve.tensor_tensor(tau[:], tau[:], mask_sb[:], ALU.mult)
                ev_tau = ev.n
                if A_P5MODE < 3:
                    free_tau_mm = mm.n
                    continue
                pe.wait_ge(ev.h, ev_tau)
                pe.wait_ge(ev.h, free_pp)
                for hh in range(8):
                    h = half * 8 + hh
                    for piece in range(2):
                        kb = qt + piece
                        last = pe.matmul(pp[0:64, hh * 128:(hh + 1) * 128],
                                         vt[:, kb, h * 64:(h + 1) * 64],
                                         tau[:, hh * 256 + piece * 128:hh * 256 + piece * 128 + 128],
                                         start=(piece == 0), stop=(piece == 1))
                free_tau_mm = mm.inc(last)
                if A_P5MODE < 4:
                    continue
                dve.wait_ge(mm.h, free_tau_mm)
                for hh in range(8):
                    h = half * 8 + hh
                    fc, r0 = h // 2, 64 * (h % 2)
                    dve.tensor_copy(msgT[r0:r0 + 64, fc, qt * 128:(qt + 1) * 128],
                                    pp[0:64, hh * 128:(hh + 1) * 128])
                free_pp = ev.n

        if A_MAXPH < 6:
            sp.wait_ge(ev.h, ev.n)
            sp.dma_start(out=io["x1"][c * CH:(c + 1) * CH, :]
                         .rearrange("(i p) d -> p i d", p=128), in_=xa[:]).then_inc(b.out.h, 16)
            b.out.n += 16
            continue
        ev_msg = ev.n
        pe.wait_ge(ev.h, ev_msg)
        for i in range(NT):
            pe.wait_ge(ev.h, free_pp2)
            for kc in range(8):
                for half in range(2):
                    sl = slice(half * 512, (half + 1) * 512)
                    last = pe.matmul(pp2[:, sl], msgT[:, kc, i * 128:(i + 1) * 128],
                                     wo_sb[:, kc, sl], start=(kc == 0), stop=(kc == 7))
            o_mm = mm.inc(last)
            dve.wait_ge(mm.h, o_mm)
            dve.scalar_tensor_tensor(ya[:], pp2[:], 1.0, xn[:, i, :], ALU.mult, ALU.add)
            free_pp2 = ev.n
            act.wait_ge(ev.h, free_pp2)
            _ln_stats(b, ya[:], 8)
            dve.wait_ge(ea.h, ea.n)
            _ln_mr(b, slice(8, 9))
            dve.tensor_scalar(yb[:], ya[:], b.a_mcol[:, 8:9], b.a_rcol[:, 8:9],
                              ALU.subtract, ALU.mult)
            dve.scalar_tensor_tensor(yb[:], yb[:], 1.0, gbc[:], ALU.mult, ALU.mult)
            dve.tensor_tensor(ya[:], yb[:], bbc[:], ALU.add)
            ev_ao = ev.n
            act.wait_ge(ev.h, ev_ao)
            _ln_stats(b, ya[:], 9)
            dve.wait_ge(ea.h, ea.n)
            _ln_mr(b, slice(9, 10))
            dve.tensor_scalar(yb[:], ya[:], b.a_mcol[:, 9:10], b.a_rcol[:, 9:10],
                              ALU.subtract, ALU.mult)
            dve.tensor_tensor(xa[:, i, :], xa[:, i, :], yb[:], ALU.add)

        sp.wait_ge(ev.h, ev.n)
        sp.dma_start(out=io["x1"][c * CH:(c + 1) * CH, :]
                     .rearrange("(i p) d -> p i d", p=128), in_=xa[:]).then_inc(b.out.h, 16)
        b.out.n += 16

    sp.wait_ge(b.out.h, b.out.n)


def hostprep_a(inputs):
    wq = np.ascontiguousarray(np.asarray(inputs["L0_attn_wq"]).T)
    wkv = np.asarray(inputs["L0_attn_wkv"]).reshape(NH, 2 * HD, D)
    wkT = np.ascontiguousarray(wkv[:, :HD].reshape(D, D).T)
    wvT = np.ascontiguousarray(wkv[:, HD:].reshape(D, D).T)
    woT = np.ascontiguousarray(np.asarray(inputs["L0_attn_wo"]).T)
    ki = np.arange(128)[:, None]
    qi = np.arange(128)[None, :]
    mA = ((ki - qi) >= 112).astype(BF)
    mB = (((ki - qi) >= -16) & ((ki - qi) <= -1)).astype(BF)
    maskAB = np.ascontiguousarray(np.concatenate([np.concatenate([mA, mB], 1)] * 8, 1))
    return dict(wqT=wq, wkT=wkT, wvT=wvT, woT=woT,
                gamma=np.asarray(inputs["L0_attn_gamma"]),
                beta=np.asarray(inputs["L0_attn_beta"]),
                maskAB=maskAB, ident=np.eye(128, dtype=BF))


A_IO = [("x0", [T, D], F32), ("wqT", [D, D], F32), ("wkT", [D, D], F32),
        ("wvT", [D, D], F32), ("woT", [D, D], F32), ("gamma", [D], F32),
        ("beta", [D], F32), ("maskAB", [128, 2048], BF16), ("ident", [128, 128], BF16)]


def build_a_program():
    b = B()
    io = {n: b.dram(n, s, d, "ExternalInput") for n, s, d in A_IO}
    io["x1"] = b.dram("x1", [T, D], F32, "ExternalOutput")
    with b.ctx:
        blk = b.ctx.enter_context(b.nc.Block())

        @blk.gpsimd
        def _(g):
            build_stage_a(b, io)
    return b.nc




def _free_as_ev(b, fr):
    sm, cnt = fr
    if sm is b.ev:
        return cnt
    b.dve.wait_ge(sm.h, cnt)
    return b.ev.n


def build_stage_b(b, io):
    CH = 1024
    NCH = T // CH
    NT = CH // 128
    nc = b.nc
    pe, act, dve, pool, sp = b.pe, b.act, b.dve, b.pool, b.sp
    dma, mm, ea, ev, ep = b.dma, b.mm, b.ea, b.ev, b.ep

    xt = b.sb("b_xt", [128, 2, D], F32)
    xn = b.sb("b_xn", [128, NT, D], BF16)
    xnT = b.sb("b_xnT", [128, 8, CH], BF16)
    acc = b.sb("b_acc", [128, NT, D], F32)
    h_sb = b.sb("b_h", [128, 8, CH], BF16)
    h1s = b.sb("b_h1s", [128, CH], BF16)
    wsl = [[b.sb(f"b_w{m}_{p}", [128, 8, D], BF16) for m in range(3)] for p in range(2)]
    gw_sb = b.sb("b_gw", [128, 8, 16], BF16)
    logits = b.sb("b_logits", [128, NT, 16], F32)
    g_all = b.sb("b_g", [128, NT, 16], F32)
    bwork = b.sb("b_bwork", [128, 16], F32)
    selm = b.sb("b_selm", [128, 16], F32)
    t1 = b.sb("b_t1", [128, 16], F32)
    t2 = b.sb("b_t2", [128, 16], F32)
    lmrow = b.sb("b_lm", [128, 16], F32)
    den = b.sb("b_den", [128, 1], F32)
    ebias_bc = b.sb("b_ebias", [128, 16], F32)
    iota_bc = b.sb("b_iota", [128, 16], F32)
    iotamB_bc = b.sb("b_iotamB", [128, 16], F32)
    gones = b.sb("b_gones", [128, 1], F32)
    ebrow = b.sb("b_ebrow", [1, 16], F32)
    ones1 = b.sb("b_ones1", [1, 128], F32)
    id_sb = b.sb("b_id", [128, 128], BF16)
    b.a_s1 = b.sb("b_s1", [128, 16], F32)
    b.a_ssq = b.sb("b_ssq", [128, 16], F32)
    b.a_mcol = b.sb("b_mcol", [128, 16], F32)
    b.a_vcol = b.sb("b_vcol", [128, 16], F32)
    b.a_rcol = b.sb("b_rcol", [128, 16], F32)
    b.a_sqscr = b.sb("b_sqscr", [128, D], BF16)
    b.a_ecol = b.sb("b_ecol", [128, 1], F32)
    ya = b.sb("b_ya", [128, 2, D], F32)

    pp = b.ps("b_pp", [128, D])
    pp2 = b.ps("b_pp2", [128, D])
    pp3 = b.ps("b_pp3", [128, D])
    pr = b.ps("b_pr", [128, 16])
    pr2 = b.ps("b_pr2", [128, 16])

    pool.memset(ones1[:], 1.0)
    pool.memset(b.a_ecol[:], EPS)
    pool.memset(gones[:], 1.0)
    pn = ep.n
    for dst, src in [(id_sb[:], io["ident"]), (iota_bc[:], io["iota16"]),
                     (iotamB_bc[:], io["iotamB16"]),
                     (ebrow[:], io["expert_bias"].unsqueeze(0))]:
        sp.dma_start(out=dst, in_=src).then_inc(b.cdma.h, 16)
        b.cdma.n += 16
    pool.dma_start(out=gw_sb[:], in_=io["gate_wT"].rearrange("(kc p) n -> p kc n", p=128)
                   ).then_inc(b.pdma.h, 16)
    b.pdma.n += 16
    gw_loaded = b.pdma.n
    pe.wait_ge(b.cdma.h, b.cdma.n)
    pe.wait_ge(ep.h, pn)
    last = pe.matmul(pr[:], ones1[:], ebrow[:], start=True, stop=True)
    eb_mm = mm.inc(last)
    dve.wait_ge(mm.h, eb_mm)
    dve.tensor_copy(ebias_bc[:], pr[:])
    dve.wait_ge(b.cdma.h, b.cdma.n)
    ev_const = ev.n

    worder = [("s", None)] + [("e", e) for e in range(NE)]
    loaded = {}
    compute_done = {}

    def issue_weight_load(idx):
        kind, e = worder[idx]
        par = idx % 2
        ws = b.wd[par]
        if idx >= 2:
            pool.wait_ge(mm.h, compute_done[idx - 2])
        for j, nm in enumerate(["w1", "w3", "w2"]):
            src = io["s" + nm + "T"] if kind == "s" else io[nm + "T"][e]
            pool.dma_start(out=wsl[par][j][:],
                           in_=src.rearrange("(kc p) n -> p kc n", p=128)
                           ).then_inc(ws.h, 16)
            ws.n += 16
        loaded[idx] = ws.n

    free_pp = ev_const
    free_pp2 = ev_const
    free_pr_ev = [ev.n, ev.n]

    for c in range(NCH):
        pool.wait_ge(ev.h, ev.n)
        pool.wait_ge(ea.h, ea.n)
        pool.memset(acc[:], 0.0)
        acc_zero_ep = ep.n

        xt_free = [b.ev.n, b.ev.n]
        for i in range(NT):
            k = i % 2
            xs = b.xdma if k == 0 else b.xdma2
            sp.wait_ge(ev.h, xt_free[k])
            sp.dma_start(out=xt[:, k, :], in_=io["x1"][c * CH + i * 128:c * CH + (i + 1) * 128, :]
                         ).then_inc(xs.h, 16)
            xs.n += 16
            dve.wait_ge(xs.h, xs.n)
            act.wait_ge(xs.h, xs.n)
            _ln_stats(b, xt[:, k, :], i % 8)
            dve.wait_ge(ea.h, ea.n)
            _ln_mr(b, slice(i % 8, i % 8 + 1))
            dve.tensor_scalar(xn[:, i, :], xt[:, k, :], b.a_mcol[:, i % 8:i % 8 + 1],
                              b.a_rcol[:, i % 8:i % 8 + 1], ALU.subtract, ALU.mult)
            xt_free[k] = ev.n

        ptv = [pp[:, 0:64].bitcast(BF16), pp2[:, 0:64].bitcast(BF16)]
        free_pt = [free_pp, free_pp2]
        pe.wait_ge(ev.h, ev.n)
        for u in range(NT * 8):
            i, fc = divmod(u, 8)
            k = u % 2
            pe.wait_ge(ev.h, free_pt[k])
            t_mm = mm.inc(pe.transpose(ptv[k], xn[:, i, fc * 128:(fc + 1) * 128], id_sb[:]))
            dve.wait_ge(mm.h, t_mm)
            dve.tensor_copy(xnT[:, fc, i * 128:(i + 1) * 128], ptv[k])
            free_pt[k] = ev.n
        free_pp, free_pp2 = free_pt

        issue_weight_load(0)
        issue_weight_load(1)

        pe.wait_ge(ev.h, ev.n)
        pe.wait_ge(b.pdma.h, gw_loaded)
        for i in range(NT):
            pru = pr if i % 2 == 0 else pr2
            pe.wait_ge(ev.h, free_pr_ev[i % 2])
            for kc in range(8):
                last = pe.matmul(pru[:], xnT[:, kc, i * 128:(i + 1) * 128],
                                 gw_sb[:, kc, :], start=(kc == 0), stop=(kc == 7))
            r_mm = mm.inc(last)
            dve.wait_ge(mm.h, r_mm)
            dve.tensor_copy(logits[:, i, :], pru[:])
            free_pr_ev[i % 2] = ev.n
            dve.tensor_tensor(bwork[:], logits[:, i, :], ebias_bc[:], ALU.add)
            dve.memset(selm[:], 0.0)
            for kk in range(4):
                dve.tensor_reduce(den[:], bwork[:], axis=AX.X, op=ALU.max)
                dve.tensor_scalar(t1[:], bwork[:], den[:], None, ALU.is_equal)
                dve.tensor_tensor(t2[:], t1[:], iotamB_bc[:], ALU.mult)
                dve.tensor_scalar_add(t2[:], t2[:], BIG)
                dve.tensor_reduce(den[:], t2[:], axis=AX.X, op=ALU.min)
                dve.tensor_scalar(t1[:], iota_bc[:], den[:], None, ALU.is_equal)
                dve.tensor_tensor(selm[:], selm[:], t1[:], ALU.add)
                dve.scalar_tensor_tensor(bwork[:], t1[:], -BIG, bwork[:], ALU.mult, ALU.add)
            dve.tensor_scalar_add(t1[:], logits[:, i, :], -BIG)
            dve.scalar_tensor_tensor(lmrow[:], selm[:], BIG, t1[:], ALU.mult, ALU.add)
            dve.tensor_reduce(den[:], lmrow[:], axis=AX.X, op=ALU.max)
            dve.tensor_scalar_mul(den[:], den[:], -1.0)
            act.wait_ge(ev.h, ev.n)
            act.activation(g_all[:, i, :], lmrow[:], AF.Exp, bias=den[:],
                           accum_out=b.a_vcol[:, 15:16])
            dve.wait_ge(ea.h, ea.n)
            dve.reciprocal(den[:], b.a_vcol[:, 15:16])
            dve.tensor_scalar(g_all[:, i, :], g_all[:, i, :], den[:], None, ALU.mult)

        dve.wait_ge(ep.h, acc_zero_ep)
        R = [pp, pp2, pp3]
        buf_free = [(ev, free_pp), (ev, free_pp2), (ev, ev.n)]
        for idx in range(len(worder)):
            kind, e = worder[idx]
            par = idx % 2
            pe.wait_ge(b.wd[par].h, loaded[idx])
            w1s, w3s, w2s = wsl[par]
            for m in range(8):
                ja = (2 * m) % 3
                jb = (2 * m + 1) % 3
                sm, cnt = buf_free[ja]
                pe.wait_ge(sm.h, cnt)
                for kc in range(8):
                    for half in range(2):
                        sl = slice(half * 512, (half + 1) * 512)
                        last = pe.matmul(R[ja][:, sl], w1s[:, kc, m * 128:(m + 1) * 128],
                                         xnT[:, kc, sl], start=(kc == 0), stop=(kc == 7))
                h1_mm = mm.inc(last)
                sm, cnt = buf_free[jb]
                pe.wait_ge(sm.h, cnt)
                for kc in range(8):
                    for half in range(2):
                        sl = slice(half * 512, (half + 1) * 512)
                        last = pe.matmul(R[jb][:, sl], w3s[:, kc, m * 128:(m + 1) * 128],
                                         xnT[:, kc, sl], start=(kc == 0), stop=(kc == 7))
                h3_mm = mm.inc(last)
                act.wait_ge(mm.h, h1_mm)
                act.wait_ge(ev.h, ev.n)
                act.activation(h1s[:], R[ja][:], AF.Silu)
                buf_free[ja] = (ea, ea.n)
                dve.wait_ge(mm.h, h3_mm)
                dve.wait_ge(ea.h, ea.n)
                dve.scalar_tensor_tensor(h_sb[:, m, :], R[jb][:], 1.0, h1s[:],
                                         ALU.mult, ALU.mult)
                buf_free[jb] = (ev, ev.n)
            ev_h = ev.n
            pe.wait_ge(ev.h, ev_h)
            for i in range(NT):
                k = i % 2
                sm, cnt = buf_free[k]
                pe.wait_ge(sm.h, cnt)
                for ic in range(8):
                    for half in range(2):
                        sl = slice(half * 512, (half + 1) * 512)
                        last = pe.matmul(R[k][:, sl], h_sb[:, ic, i * 128:(i + 1) * 128],
                                         w2s[:, ic, sl], start=(ic == 0), stop=(ic == 7))
                d_mm = mm.inc(last)
                dve.wait_ge(mm.h, d_mm)
                gcol = gones[:, 0:1] if kind == "s" else g_all[:, i, e:e + 1]
                dve.scalar_tensor_tensor(acc[:, i, :], R[k][:], gcol, acc[:, i, :],
                                         ALU.mult, ALU.add)
                buf_free[k] = (ev, ev.n)
            compute_done[idx] = mm.n
            if idx + 2 < len(worder):
                issue_weight_load(idx + 2)
        free_pp = _free_as_ev(b, buf_free[0])
        free_pp2 = _free_as_ev(b, buf_free[1])

        act.wait_ge(ev.h, ev.n)
        xt_free = [b.ev.n, b.ev.n]
        ya_free = [b.out.n, b.out2.n]
        for i in range(NT):
            k = i % 2
            xs = b.xdma if k == 0 else b.xdma2
            sp.wait_ge(ev.h, xt_free[k])
            sp.dma_start(out=xt[:, k, :], in_=io["x1"][c * CH + i * 128:c * CH + (i + 1) * 128, :]
                         ).then_inc(xs.h, 16)
            xs.n += 16
            _ln_stats(b, acc[:, i, :], 8 + (i % 2))
            dve.wait_ge(ea.h, ea.n)
            _ln_mr(b, slice(8 + (i % 2), 9 + (i % 2)))
            dve.wait_ge((b.out if k == 0 else b.out2).h, ya_free[k])
            dve.wait_ge(xs.h, xs.n)
            dve.tensor_scalar(ya[:, k, :], acc[:, i, :], b.a_mcol[:, 8 + i % 2:9 + i % 2],
                              b.a_rcol[:, 8 + i % 2:9 + i % 2], ALU.subtract, ALU.mult)
            dve.tensor_tensor(ya[:, k, :], ya[:, k, :], xt[:, k, :], ALU.add)
            xt_free[k] = ev.n
            os_ = b.out if k == 0 else b.out2
            sp.wait_ge(ev.h, ev.n)
            sp.dma_start(out=io["x2"][c * CH + i * 128:c * CH + (i + 1) * 128, :],
                         in_=ya[:, k, :]).then_inc(os_.h, 16)
            os_.n += 16
            ya_free[k] = os_.n

    sp.wait_ge(b.out.h, b.out.n)
    sp.wait_ge(b.out2.h, b.out2.n)


def hostprep_b(inputs):
    iota = np.tile(np.arange(16, dtype=np.float32), (128, 1))
    return dict(
        gate_wT=np.ascontiguousarray(np.asarray(inputs["L0_ffn_gate_w"]).T),
        expert_bias=np.asarray(inputs["L0_ffn_expert_bias"]),
        w1T=np.ascontiguousarray(np.asarray(inputs["L0_ffn_w1"]).transpose(0, 2, 1)),
        w2T=np.ascontiguousarray(np.asarray(inputs["L0_ffn_w2"]).transpose(0, 2, 1)),
        w3T=np.ascontiguousarray(np.asarray(inputs["L0_ffn_w3"]).transpose(0, 2, 1)),
        sw1T=np.ascontiguousarray(np.asarray(inputs["L0_ffn_sw1"]).T),
        sw2T=np.ascontiguousarray(np.asarray(inputs["L0_ffn_sw2"]).T),
        sw3T=np.ascontiguousarray(np.asarray(inputs["L0_ffn_sw3"]).T),
        iota16=iota, iotamB16=np.ascontiguousarray(iota - BIG),
        ident=np.eye(128, dtype=BF))


B_IO = [("gate_wT", [D, 16], F32), ("expert_bias", [16], F32),
        ("w1T", [NE, D, D], F32), ("w2T", [NE, D, D], F32), ("w3T", [NE, D, D], F32),
        ("sw1T", [D, D], F32), ("sw2T", [D, D], F32), ("sw3T", [D, D], F32),
        ("iota16", [128, 16], F32), ("iotamB16", [128, 16], F32),
        ("ident", [128, 128], BF16)]


def build_b_program():
    b = B()
    io = {n: b.dram(n, s, d, "ExternalInput") for n, s, d in B_IO}
    io["x1"] = b.dram("x1", [T, D], F32, "ExternalInput")
    io["x2"] = b.dram("x2", [T, D], F32, "ExternalOutput")
    with b.ctx:
        blk = b.ctx.enter_context(b.nc.Block())

        @blk.gpsimd
        def _(g):
            build_stage_b(b, io)
    return b.nc




def build_stage_c(b, io, n_cores=8):
    CH = 1024
    NCH = T // CH
    NT = CH // 128
    nc = b.nc
    pe, act, dve, pool, sp = b.pe, b.act, b.dve, b.pool, b.sp
    dma, mm, ea, ev, ep = b.dma, b.mm, b.ea, b.ev, b.ep

    qT_h = io["qT_h"]
    kT_h = io["kT_h"]
    v_h = io["v_h"]
    cc_in = io["cc_in"]
    cc_out = io["cc_out"]

    energy = b.sb("c_energy", [16, T], F32)
    mask_all = b.sb("c_mask", [16, T], BF16)
    ones1c = b.sb("c_ones1c", [1, 128], F32)
    mk16 = b.sb("c_mk16", [16, 1], F32)
    qstab = b.sb("c_qstab", [16, T], BF16)
    thr = b.sb("c_thr", [1, 1], F32)
    thr16 = b.sb("c_thr16", [16, 1], F32)
    srow = b.sb("c_srow", [16, 1], F32)
    s11 = b.sb("c_s11", [1, 1], F32)
    ones2 = b.sb("c_ones2", [128, 16], F32)
    selmat = b.sb("c_selmat", [128, 8, 16], BF16)
    eh64 = b.sb("c_eh64", [16, NH * 64], BF16)
    ctx_fm = b.sb("c_ctx", [128, 8, T], BF16)
    id_sb = b.sb("c_id", [128, 128], BF16)
    b.a_s1 = b.sb("c_s1", [128, 16], F32)
    b.a_ssq = b.sb("c_ssq", [128, 16], F32)
    b.a_mcol = b.sb("c_mcol", [128, 16], F32)
    b.a_vcol = b.sb("c_vcol", [128, 16], F32)
    b.a_rcol = b.sb("c_rcol", [128, 16], F32)
    b.a_sqscr = b.sb("c_sqscr", [128, D], BF16)
    b.a_ecol = b.sb("c_ecol", [128, 1], F32)

    pool.memset(ones2[:], 1.0)
    pool.memset(ones1c[:], 1.0)
    pool.memset(b.a_ecol[:], EPS)
    pool.memset(mk16[:], 0.0)
    pn0 = ep.n
    for dst, src in [(id_sb[:], io["ident"]), (selmat[:], io["selmat"]),
                     (eh64[:], io["eh64"])]:
        sp.dma_start(out=dst, in_=src).then_inc(b.cdma.h, 16)
        b.cdma.n += 16
    c_const = b.cdma.n

    with ExitStack() as c1ctx:
        sb1 = lambda n, sh, dt: c1ctx.enter_context(nc.sbuf_tensor(n, sh, dt))
        xt = sb1("c_xt", [128, 2, D], F32)
        xn = sb1("c_xn", [128, NT, D], BF16)
        xnT = sb1("c_xnT", [128, 8, CH], BF16)
        wq_sb = sb1("c_wq", [128, 8, D], BF16)
        wk_sb = sb1("c_wk", [128, 8, D], BF16)
        wv_sb = sb1("c_wv", [128, 8, D], BF16)
        bq_sb = sb1("c_bq", [128, 8], F32)
        bk_sb = sb1("c_bk", [128, 8], F32)
        bvrow = sb1("c_bvrow", [1, D], F32)
        bv_bc = sb1("c_bvbc", [128, D], F32)
        stage = sb1("c_stage", [128, 2, D], BF16)
        sqs = sb1("c_sqs", [128, CH], BF16)
        ek_scr = sb1("c_ekscr", [16, CH], BF16)
        ekcol = sb1("c_ekcol", [16, 1], F32)

        ones1 = sb1("c_ones1", [1, 128], F32)
        with ExitStack() as p1ctx:
            pp = p1ctx.enter_context(nc.psum_tensor("c_pp", [128, D], F32))
            pp2 = p1ctx.enter_context(nc.psum_tensor("c_pp2", [128, D], F32))
            pes = p1ctx.enter_context(nc.psum_tensor("c_pes", [16, CH], F32))

            pool.memset(ones1[:], 1.0)
            pn = ep.n
            for dst, src in [(wq_sb[:], io["wqT1"]), (wk_sb[:], io["wkT1"]),
                             (wv_sb[:], io["wvT1"])]:
                pool.dma_start(out=dst, in_=src.rearrange("(kc p) n -> p kc n", p=128)
                               ).then_inc(b.pdma.h, 16)
                b.pdma.n += 16
            for dst, src in [(bq_sb[:], io["bq1"]), (bk_sb[:], io["bk1"]),
                             (bvrow[:], io["bv1"].unsqueeze(0))]:
                sp.dma_start(out=dst, in_=src).then_inc(b.cdma.h, 16)
                b.cdma.n += 16
            w1_loaded = b.pdma.n
            pe.wait_ge(b.cdma.h, b.cdma.n)
            pe.wait_ge(ep.h, pn)
            for half in range(2):
                sl = slice(half * 512, (half + 1) * 512)
                last = pe.matmul(pp[:, sl], ones1[:], bvrow[:, sl], start=True, stop=True)
            bv_mm = mm.inc(last)
            dve.wait_ge(mm.h, bv_mm)
            dve.tensor_copy(bv_bc[:], pp[:])
            free_pp = ev.n
            free_pp2 = ev.n
            pe.wait_ge(b.pdma.h, w1_loaded)

            for c in range(NCH):
                xt_free = [ev.n, ev.n]
                for i in range(NT):
                    k = i % 2
                    xs = b.xdma if k == 0 else b.xdma2
                    sp.wait_ge(ev.h, xt_free[k])
                    sp.dma_start(out=xt[:, k, :],
                                 in_=io["x2"][c * CH + i * 128:c * CH + (i + 1) * 128, :]
                                 ).then_inc(xs.h, 16)
                    xs.n += 16
                    dve.wait_ge(xs.h, xs.n)
                    act.wait_ge(xs.h, xs.n)
                    _ln_stats(b, xt[:, k, :], i)
                    dve.wait_ge(ea.h, ea.n)
                    _ln_mr(b, slice(i, i + 1))
                    dve.tensor_scalar(xn[:, i, :], xt[:, k, :], b.a_mcol[:, i:i + 1],
                                      b.a_rcol[:, i:i + 1], ALU.subtract, ALU.mult)
                    xt_free[k] = ev.n
                ptv = [pp[:, 0:64].bitcast(BF16), pp2[:, 0:64].bitcast(BF16)]
                free_pt = [free_pp, free_pp2]
                pe.wait_ge(ev.h, ev.n)
                for u in range(NT * 8):
                    i, fc = divmod(u, 8)
                    k = u % 2
                    pe.wait_ge(ev.h, free_pt[k])
                    t_mm = mm.inc(pe.transpose(ptv[k], xn[:, i, fc * 128:(fc + 1) * 128],
                                               id_sb[:]))
                    dve.wait_ge(mm.h, t_mm)
                    dve.tensor_copy(xnT[:, fc, i * 128:(i + 1) * 128], ptv[k])
                    free_pt[k] = ev.n
                free_pp, free_pp2 = free_pt
                pe.wait_ge(ev.h, ev.n)

                for wsb, bcol, isq, hdr in [(wq_sb, bq_sb, True, qT_h),
                                            (wk_sb, bk_sb, False, kT_h)]:
                    st_free = [b.out.n, b.out2.n]
                    for m in range(8):
                        k = m % 2
                        ppu = pp if k == 0 else pp2
                        pe.wait_ge(ev.h, free_pp if k == 0 else free_pp2)
                        for kc in range(8):
                            for half in range(2):
                                sl = slice(half * 512, (half + 1) * 512)
                                last = pe.matmul(ppu[:, sl],
                                                 wsb[:, kc, m * 128:(m + 1) * 128],
                                                 xnT[:, kc, sl],
                                                 start=(kc == 0), stop=(kc == 7))
                        q_mm = mm.inc(last)
                        dve.wait_ge(mm.h, q_mm)
                        os_ = b.out if k == 0 else b.out2
                        dve.wait_ge(os_.h, st_free[k])
                        dve.tensor_scalar(stage[:, k, :], ppu[:], bcol[:, m:m + 1], None,
                                          ALU.add)
                        if k == 0:
                            free_pp = ev.n
                        else:
                            free_pp2 = ev.n
                        act.wait_ge(ev.h, ev.n)
                        act.activation(sqs[:], stage[:, k, :], AF.Square)
                        pe.wait_ge(ea.h, ea.n)
                        for half in range(2):
                            sl = slice(half * 512, (half + 1) * 512)
                            last = pe.matmul(pes[:, sl], selmat[:, m, :], sqs[:, sl],
                                             start=(m == 0), stop=(m == 7))
                        es_mm = mm.inc(last)
                        sp.wait_ge(ev.h, ev.n)
                        sp.dma_start(out=hdr[m * 128:(m + 1) * 128, c * CH:(c + 1) * CH],
                                     in_=stage[:, k, :]).then_inc(os_.h, 16)
                        os_.n += 16
                        st_free[k] = os_.n
                    act.wait_ge(mm.h, mm.n)
                    act.wait_ge(ep.h, pn0)
                    if isq:
                        act.activation(energy[:, c * CH:(c + 1) * CH], pes[:], AF.Sqrt)
                    else:
                        act.activation(ek_scr[:], pes[:], AF.Sqrt)
                        dve.wait_ge(ea.h, ea.n)
                        dve.tensor_reduce(ekcol[:], ek_scr[:], axis=AX.X, op=ALU.max)
                        dve.tensor_tensor(mk16[:], mk16[:], ekcol[:], ALU.max)
                    pe.wait_ge(ea.h, ea.n)
                st_free = [b.out.n, b.out2.n]
                for i in range(NT):
                    k = i % 2
                    ppu = pp if k == 0 else pp2
                    pe.wait_ge(ev.h, free_pp if k == 0 else free_pp2)
                    for kc in range(8):
                        for half in range(2):
                            sl = slice(half * 512, (half + 1) * 512)
                            last = pe.matmul(ppu[:, sl], xnT[:, kc, i * 128:(i + 1) * 128],
                                             wv_sb[:, kc, sl], start=(kc == 0), stop=(kc == 7))
                    v_mm = mm.inc(last)
                    dve.wait_ge(mm.h, v_mm)
                    os_ = b.out if k == 0 else b.out2
                    dve.wait_ge(os_.h, st_free[k])
                    dve.scalar_tensor_tensor(stage[:, k, :], ppu[:], 1.0, bv_bc[:],
                                             ALU.mult, ALU.add)
                    if k == 0:
                        free_pp = ev.n
                    else:
                        free_pp2 = ev.n
                    sp.wait_ge(ev.h, ev.n)
                    sp.dma_start(out=v_h[c * CH + i * 128:c * CH + (i + 1) * 128, :],
                                 in_=stage[:, k, :]).then_inc(os_.h, 16)
                    os_.n += 16
                    st_free[k] = os_.n

    dve.wait_ge(ea.h, ea.n)
    dve.wait_ge(b.cdma.h, b.cdma.n)
    dve.tensor_reduce(srow[:], energy[:], axis=AX.X, op=ALU.add)
    with ExitStack() as ptx:
        p11 = ptx.enter_context(nc.psum_tensor("c_p11", [128, 16], F32))
        pe.wait_ge(ev.h, ev.n)
        pe.wait_ge(ep.h, pn0)
        l_mm = mm.inc(pe.matmul(p11[0:1, 0:1], srow[:], ones2[0:16, 0:1],
                                start=True, stop=True))
        dve.wait_ge(mm.h, l_mm)
        dve.tensor_copy(s11[:], p11[0:1, 0:1])
        pool.wait_ge(ev.h, ev.n)
        pool.dma_start(out=cc_in, in_=s11[:]).then_inc(b.pdma.h, 16)
        b.pdma.n += 16
        pool.wait_ge(b.pdma.h, b.pdma.n)
        pool.collective_compute("AllReduce", ALU.add,
                                replica_groups=[list(range(n_cores))],
                                ins=[cc_in.opt()], outs=[cc_out.opt()]).then_inc(b.ep.h, 1)
        b.ep.n += 1
        pool.wait_ge(b.ep.h, b.ep.n)
        pool.dma_start(out=thr[:], in_=cc_out).then_inc(b.pdma.h, 16)
        b.pdma.n += 16
        dve.wait_ge(b.pdma.h, b.pdma.n)
        dve.tensor_scalar_mul(thr[:], thr[:], 0.1 / (n_cores * T * NH))
        ev_thr = ev.n
        pe.wait_ge(ev.h, ev_thr)
        t_mm = mm.inc(pe.matmul(p11[0:16, 0:1], ones2[0:1, :], thr[:], start=True, stop=True))
        dve.wait_ge(mm.h, t_mm)
        dve.tensor_copy(thr16[:], p11[0:16, 0:1])
        dve.tensor_scalar(mask_all[:], energy[:], thr16[:], None, ALU.is_gt)
        dve.tensor_scalar_mul(ekcol2 := b.a_vcol[0:16, 14:15], mk16[:], -1.0)
        dve.tensor_tensor(energy[:], energy[:], mask_all[:], ALU.mult)
        dve.tensor_scalar(qstab[:], energy[:], b.a_vcol[0:16, 14:15], None, ALU.mult)
        ev_mask = ev.n

    KTG = 2
    with ExitStack() as c2ctx:
        sb2 = lambda n, sh, dt: c2ctx.enter_context(nc.sbuf_tensor(n, sh, dt))
        qh = [sb2(f"c_qh{p}", [64, T], BF16) for p in range(2)]
        kh = [sb2(f"c_kh{p}", [65, T], BF16) for p in range(2)]
        vh = [sb2(f"c_vh{p}", [128, 32, 65], BF16) for p in range(2)]
        qm = [sb2(f"c_qm{p}", [65, 512], BF16) for p in range(2)]
        phat = [sb2(f"c_ph{p}", [128, KTG * 512], BF16) for p in range(2)]
        rrow = sb2("c_rrow", [128, 512], F32)
        recipb = sb2("c_recipb", [64, 512], BF16)
        onesc = sb2("c_onesc", [128, 128], F32)
        with ExitStack() as p2ctx:
            psc2 = [p2ctx.enter_context(nc.psum_tensor(f"c_psc{p}", [128, KTG * 512], F32))
                    for p in range(2)]
            pctx = [p2ctx.enter_context(nc.psum_tensor(f"c_pctx{p}", [65, 512], F32))
                    for p in range(2)]
            pmb = p2ctx.enter_context(nc.psum_tensor("c_pmb", [128, 512], F32))

            pool.wait_ge(ev.h, ev.n)
            pool.wait_ge(ea.h, ea.n)
            pool.wait_ge(mm.h, mm.n)
            pool.memset(onesc[:], 1.0)
            pool.memset(vh[0][:], 1.0)
            pool.memset(vh[1][:], 1.0)
            pool.memset(kh[0][:], 1.0)
            pool.memset(kh[1][:], 1.0)
            pn2 = ep.n
            hd_sem = [b.hd[0], b.hd[1]]
            h_loaded = {}
            h_done = {}

            def load_head(h):
                par = h % 2
                ws = hd_sem[par]
                if h >= 2:
                    sp.wait_ge(mm.h, h_done[h - 2])
                sp.dma_start(out=qh[par][:], in_=qT_h[h * 64:(h + 1) * 64, :]
                             ).then_inc(ws.h, 16)
                sp.dma_start(out=kh[par][0:64, :], in_=kT_h[h * 64:(h + 1) * 64, :]
                             ).then_inc(ws.h, 16)
                sp.dma_start(out=vh[par][:, :, 0:64],
                             in_=v_h[:, h * 64:(h + 1) * 64]
                             .rearrange("(kt p) d -> p kt d", p=128)).then_inc(ws.h, 16)
                ws.n += 48
                h_loaded[h] = ws.n

            sp.wait_ge(b.out.h, b.out.n)
            sp.wait_ge(b.out2.h, b.out2.n)
            sp.wait_ge(ep.h, pn2)
            load_head(0)
            load_head(1)
            free_phat = [mm.n, mm.n]
            free_psc_ea = [ea.n, ea.n]
            free_pctx = [ev.n, ev.n]
            qmf = [mm.n, mm.n]
            free_pmb_ev = ev.n
            ngrp = 32 // KTG
            pending = None
            act.wait_ge(ev.h, ev_mask)

            def emit_normalize(ph, pqs, pqt, pctx_mm):
                nonlocal free_pmb_ev
                dve.wait_ge(mm.h, pctx_mm)
                dve.reciprocal(rrow[64:65, :], pctx[pqs][64:65, :])
                ev_rec = ev.n
                pe.wait_ge(ev.h, ev_rec)
                pe.wait_ge(ev.h, free_pmb_ev)
                r_mm = mm.inc(pe.matmul(pmb[0:64, :], onesc[64:65, 0:64],
                                        rrow[64:65, :], start=True, stop=True))
                dve.wait_ge(mm.h, r_mm)
                dve.tensor_copy(recipb[:], pmb[0:64, :])
                free_pmb_ev = ev.n
                pfc, pr0 = ph // 2, 64 * (ph % 2)
                dve.scalar_tensor_tensor(
                    ctx_fm[pr0:pr0 + 64, pfc, pqt * 512:(pqt + 1) * 512],
                    pctx[pqs][0:64, :], 1.0, recipb[:], ALU.mult, ALU.mult)
                free_pctx[pqs] = ev.n

            for h in range(NH):
                par = h % 2
                pe.wait_ge(b.hd[par].h, h_loaded[h])
                pe.wait_ge(ev.h, ev_mask)
                pe.wait_ge(ep.h, pn2)
                for qt in range(8):
                    qs = qt % 2
                    pe.wait_ge(ev.h, free_pmb_ev)
                    pe.matmul(pmb[0:64, :], eh64[:, h * 64:(h + 1) * 64],
                              mask_all[:, qt * 512:(qt + 1) * 512],
                              start=True, stop=True)
                    mb_mm = mm.inc(pe.matmul(pmb[64:65, :], id_sb[0:16, h:h + 1],
                                             qstab[:, qt * 512:(qt + 1) * 512],
                                             start=True, stop=True))
                    dve.wait_ge(mm.h, mb_mm)
                    dve.wait_ge(mm.h, qmf[qs])
                    dve.tensor_tensor(qm[qs][0:64, :], qh[par][:, qt * 512:(qt + 1) * 512],
                                      pmb[0:64, :], ALU.mult)
                    dve.tensor_copy(qm[qs][64:65, :], pmb[64:65, :])
                    free_pmb_ev = ev.n
                    ev_qm = ev.n

                    def emit_sc(g):
                        sl_par = g % 2
                        for j, kt in enumerate(range(g * KTG, (g + 1) * KTG)):
                            last = pe.matmul(psc2[sl_par][:, j * 512:(j + 1) * 512],
                                             kh[par][:, kt * 128:(kt + 1) * 128],
                                             qm[qs][:], start=True, stop=True)
                        return mm.inc(last)

                    sc_mm = [0] * ngrp
                    pe.wait_ge(ev.h, ev_qm)
                    pe.wait_ge(ea.h, free_psc_ea[0])
                    sc_mm[0] = emit_sc(0)
                    if pending is not None:
                        emit_normalize(*pending)
                        pending = None
                    for g in range(ngrp):
                        sl_par = g % 2
                        if g + 1 < ngrp:
                            pe.wait_ge(ea.h, free_psc_ea[(g + 1) % 2])
                            sc_mm[g + 1] = emit_sc(g + 1)
                        act.wait_ge(mm.h, sc_mm[g])
                        act.wait_ge(mm.h, free_phat[sl_par])
                        act.activation(phat[sl_par][:], psc2[sl_par][:],
                                       AF.Exp, scale=0.125)
                        free_psc_ea[sl_par] = ea.n
                        pe.wait_ge(ea.h, free_psc_ea[sl_par])
                        if g == 0:
                            pe.wait_ge(ev.h, free_pctx[qs])
                        for j, kt in enumerate(range(g * KTG, (g + 1) * KTG)):
                            last = pe.matmul(pctx[qs][:], vh[par][:, kt, :],
                                             phat[sl_par][:, j * 512:(j + 1) * 512],
                                             start=(kt == 0), stop=(kt == 31))
                        free_phat[sl_par] = mm.inc(last)
                    qmf[qs] = sc_mm[ngrp - 1]
                    pending = (h, qs, qt, free_phat[(ngrp - 1) % 2])
                h_done[h] = mm.n
                if h + 2 < NH:
                    load_head(h + 2)
            emit_normalize(*pending)
            pending = None

    with ExitStack() as c3ctx:
        sb3 = lambda n, sh, dt: c3ctx.enter_context(nc.sbuf_tensor(n, sh, dt))
        wo_sb = sb3("c_wo", [128, 8, D], BF16)
        borow = sb3("c_borow", [1, D], F32)
        bo_bc = sb3("c_bobc", [128, D], F32)
        xt = sb3("c_xt3", [128, 2, D], F32)
        ya = sb3("c_ya3", [128, 2, D], F32)
        yb = sb3("c_yb3", [128, D], F32)
        ones1b = sb3("c_ones1b", [1, 128], F32)
        with ExitStack() as p3ctx:
            pp = p3ctx.enter_context(nc.psum_tensor("c3_pp", [128, D], F32))
            pp2 = p3ctx.enter_context(nc.psum_tensor("c3_pp2", [128, D], F32))
            pool.wait_ge(ev.h, ev.n)
            pool.wait_ge(ea.h, ea.n)
            pool.wait_ge(mm.h, mm.n)
            pool.memset(ones1b[:], 1.0)
            pn3 = ep.n
            pool.dma_start(out=wo_sb[:], in_=io["woT1"].rearrange("(kc p) n -> p kc n", p=128)
                           ).then_inc(b.pdma.h, 16)
            b.pdma.n += 16
            sp.wait_ge(mm.h, mm.n)
            sp.wait_ge(ev.h, ev.n)
            sp.wait_ge(ea.h, ea.n)
            sp.dma_start(out=borow[:], in_=io["bo1"].unsqueeze(0)).then_inc(b.cdma.h, 16)
            b.cdma.n += 16
            pe.wait_ge(b.cdma.h, b.cdma.n)
            pe.wait_ge(ep.h, pn3)
            for half in range(2):
                sl = slice(half * 512, (half + 1) * 512)
                last = pe.matmul(pp[:, sl], ones1b[:], borow[:, sl], start=True, stop=True)
            bo_mm = mm.inc(last)
            dve.wait_ge(mm.h, bo_mm)
            dve.tensor_copy(bo_bc[:], pp[:])
            free_pp = ev.n
            free_pp2 = ev.n
            pe.wait_ge(b.pdma.h, b.pdma.n)
            pe.wait_ge(ev.h, ev.n)
            xt_free = [ev.n, ev.n]
            ya_free = [b.out.n, b.out2.n]
            for ti in range(T // 128):
                k = ti % 2
                xs = b.xdma if k == 0 else b.xdma2
                sp.wait_ge(ev.h, xt_free[k])
                sp.dma_start(out=xt[:, k, :], in_=io["x2"][ti * 128:(ti + 1) * 128, :]
                             ).then_inc(xs.h, 16)
                xs.n += 16
                ppu = pp if k == 0 else pp2
                pe.wait_ge(ev.h, free_pp if k == 0 else free_pp2)
                for kc in range(8):
                    for half in range(2):
                        sl = slice(half * 512, (half + 1) * 512)
                        last = pe.matmul(ppu[:, sl], ctx_fm[:, kc, ti * 128:(ti + 1) * 128],
                                         wo_sb[:, kc, sl], start=(kc == 0), stop=(kc == 7))
                o_mm = mm.inc(last)
                dve.wait_ge(mm.h, o_mm)
                dve.wait_ge((b.out if k == 0 else b.out2).h, ya_free[k])
                dve.scalar_tensor_tensor(ya[:, k, :], ppu[:], 1.0, bo_bc[:],
                                         ALU.mult, ALU.add)
                if k == 0:
                    free_pp = ev.n
                else:
                    free_pp2 = ev.n
                act.wait_ge(ev.h, ev.n)
                _ln_stats(b, ya[:, k, :], 8 + k)
                dve.wait_ge(ea.h, ea.n)
                _ln_mr(b, slice(8 + k, 9 + k))
                os_ = b.out if k == 0 else b.out2
                dve.wait_ge(xs.h, xs.n)
                dve.tensor_scalar(yb[:], ya[:, k, :], b.a_mcol[:, 8 + k:9 + k],
                                  b.a_rcol[:, 8 + k:9 + k], ALU.subtract, ALU.mult)
                dve.tensor_tensor(ya[:, k, :], yb[:], xt[:, k, :], ALU.add)
                xt_free[k] = ev.n
                sp.wait_ge(ev.h, ev.n)
                sp.dma_start(out=io["x3"][ti * 128:(ti + 1) * 128, :],
                             in_=ya[:, k, :]).then_inc(os_.h, 16)
                os_.n += 16
                ya_free[k] = os_.n

    sp.wait_ge(b.out.h, b.out.n)
    sp.wait_ge(b.out2.h, b.out2.n)


def hostprep_c(inputs):
    wqkv = np.asarray(inputs["L1_attn_w_qkv"])
    bqkv = np.asarray(inputs["L1_attn_b_qkv"])
    selmat = np.zeros((128, 8, 16), np.float32)
    for m in range(8):
        selmat[0:64, m, 2 * m] = 1.0
        selmat[64:128, m, 2 * m + 1] = 1.0
    eh64 = np.zeros((16, NH * 64), np.float32)
    for h in range(NH):
        eh64[h, h * 64:(h + 1) * 64] = 1.0
    return dict(
        wqT1=np.ascontiguousarray(wqkv[0:D].T),
        wkT1=np.ascontiguousarray(wqkv[D:2 * D].T),
        wvT1=np.ascontiguousarray(wqkv[2 * D:].T),
        bq1=np.ascontiguousarray(bqkv[0:D].reshape(8, 128).T),
        bk1=np.ascontiguousarray(bqkv[D:2 * D].reshape(8, 128).T),
        bv1=np.ascontiguousarray(bqkv[2 * D:]),
        woT1=np.ascontiguousarray(np.asarray(inputs["L1_attn_w_out"]).T),
        bo1=np.asarray(inputs["L1_attn_b_out"]),
        selmat=selmat.astype(BF), eh64=eh64.astype(BF),
        ident=np.eye(128, dtype=BF))


C_IO = [("wqT1", [D, D], F32), ("wkT1", [D, D], F32), ("wvT1", [D, D], F32),
        ("bq1", [128, 8], F32), ("bk1", [128, 8], F32), ("bv1", [D], F32),
        ("woT1", [D, D], F32), ("bo1", [D], F32),
        ("selmat", [128, 8, 16], BF16),
        ("eh64", [16, NH * 64], BF16), ("ident", [128, 128], BF16)]


def build_c_program(n_cores=8):
    b = B()
    io = {n: b.dram(n, s, d, "ExternalInput") for n, s, d in C_IO}
    io["x2"] = b.dram("x2", [T, D], F32, "ExternalInput")
    io["x3"] = b.dram("x3", [T, D], F32, "ExternalOutput")
    io["qT_h"] = b.dram("qT_h", [D, T], BF16, "Internal")
    io["kT_h"] = b.dram("kT_h", [D, T], BF16, "Internal")
    io["v_h"] = b.dram("v_h", [T, D], BF16, "Internal")
    io["cc_in"] = b.dram("cc_in", [1, 1], F32, "Internal")
    io["cc_out"] = b.dram("cc_out", [1, 1], F32, "Internal")
    with b.ctx:
        blk = b.ctx.enter_context(b.nc.Block())

        @blk.gpsimd
        def _(g):
            build_stage_c(b, io, n_cores=n_cores)
    return b.nc




def build_stage_d(b, io):
    nc = b.nc
    pe, act, dve, pool, sp = b.pe, b.act, b.dve, b.pool, b.sp
    dma, mm, ea, ev, ep = b.dma, b.mm, b.ea, b.ev, b.ep

    swT = b.sb("d_swT", [128, 8, D], BF16)
    id_sb = b.sb("d_id", [128, 128], BF16)
    xt = b.sb("d_xt", [128, 2, D], F32)
    xn = b.sb("d_xn", [128, D], BF16)
    xnT = b.sb("d_xnT", [128, 8, 128], BF16)
    ya = b.sb("d_ya", [128, 2, D], F32)
    yb = b.sb("d_yb", [128, D], F32)
    b.a_s1 = b.sb("d_s1", [128, 16], F32)
    b.a_ssq = b.sb("d_ssq", [128, 16], F32)
    b.a_mcol = b.sb("d_mcol", [128, 16], F32)
    b.a_vcol = b.sb("d_vcol", [128, 16], F32)
    b.a_rcol = b.sb("d_rcol", [128, 16], F32)
    b.a_sqscr = b.sb("d_sqscr", [128, D], BF16)
    b.a_ecol = b.sb("d_ecol", [128, 1], F32)

    with ExitStack() as pctx:
        pp = pctx.enter_context(nc.psum_tensor("d_pp", [128, D], F32))
        pp2 = pctx.enter_context(nc.psum_tensor("d_pp2", [128, D], F32))

        pool.wait_ge(ev.h, ev.n)
        pool.wait_ge(ea.h, ea.n)
        pool.wait_ge(mm.h, mm.n)
        pool.memset(b.a_ecol[:], EPS)
        pool.dma_start(out=swT[:], in_=io["wT_d"].rearrange("(kc p) n -> p kc n", p=128)
                       ).then_inc(b.pdma.h, 16)
        b.pdma.n += 16
        sp.wait_ge(mm.h, mm.n)
        sp.wait_ge(ev.h, ev.n)
        sp.wait_ge(ea.h, ea.n)
        sp.dma_start(out=id_sb[:], in_=io["ident"]).then_inc(b.cdma.h, 16)
        b.cdma.n += 16
        act.wait_ge(b.pdma.h, b.pdma.n)
        act.activation(swT[:], swT[:], AF.Sign)
        sw_ea = ea.n
        pe.wait_ge(ea.h, sw_ea)
        pe.wait_ge(b.cdma.h, b.cdma.n)

        free_pp = ev.n
        free_pp2 = ev.n
        xt_free = [ev.n, ev.n]
        ya_free = [b.out.n, b.out2.n]
        for ti in range(T // 128):
            k = ti % 2
            xs = b.xdma if k == 0 else b.xdma2
            os_ = b.out if k == 0 else b.out2
            sp.wait_ge(ev.h, xt_free[k])
            sp.dma_start(out=xt[:, k, :], in_=io["x3"][ti * 128:(ti + 1) * 128, :]
                         ).then_inc(xs.h, 16)
            xs.n += 16
            dve.wait_ge(xs.h, xs.n)
            act.wait_ge(xs.h, xs.n)
            _ln_stats(b, xt[:, k, :], k)
            dve.wait_ge(ea.h, ea.n)
            _ln_mr(b, slice(k, k + 1))
            dve.wait_ge(mm.h, mm.n)
            dve.tensor_scalar(xn[:], xt[:, k, :], b.a_mcol[:, k:k + 1],
                              b.a_rcol[:, k:k + 1], ALU.subtract, ALU.mult)
            ev_xn = ev.n
            ptv = [pp[:, 0:64].bitcast(BF16), pp2[:, 0:64].bitcast(BF16)]
            free_pt = [free_pp, free_pp2]
            pe.wait_ge(ev.h, ev_xn)
            for fc in range(8):
                kk = fc % 2
                pe.wait_ge(ev.h, free_pt[kk])
                t_mm = mm.inc(pe.transpose(ptv[kk], xn[:, fc * 128:(fc + 1) * 128], id_sb[:]))
                dve.wait_ge(mm.h, t_mm)
                dve.tensor_copy(xnT[:, fc, :], ptv[kk])
                free_pt[kk] = ev.n
            free_pp, free_pp2 = free_pt
            ppu = pp if k == 0 else pp2
            pe.wait_ge(ev.h, ev.n)
            for kc in range(8):
                for half in range(2):
                    sl = slice(half * 512, (half + 1) * 512)
                    last = pe.matmul(ppu[:, sl], xnT[:, kc, :], swT[:, kc, sl],
                                     start=(kc == 0), stop=(kc == 7))
            y_mm = mm.inc(last)
            dve.wait_ge(mm.h, y_mm)
            dve.wait_ge(os_.h, ya_free[k])
            dve.tensor_copy(ya[:, k, :], ppu[:])
            if k == 0:
                free_pp = ev.n
            else:
                free_pp2 = ev.n
            act.wait_ge(ev.h, ev.n)
            _ln_stats(b, ya[:, k, :], 8 + k)
            dve.wait_ge(ea.h, ea.n)
            _ln_mr(b, slice(8 + k, 9 + k))
            dve.tensor_scalar(yb[:], ya[:, k, :], b.a_mcol[:, 8 + k:9 + k],
                              b.a_rcol[:, 8 + k:9 + k], ALU.subtract, ALU.mult)
            dve.tensor_tensor(ya[:, k, :], yb[:], xt[:, k, :], ALU.add)
            xt_free[k] = ev.n
            sp.wait_ge(ev.h, ev.n)
            sp.dma_start(out=io["out"][ti * 128:(ti + 1) * 128, :],
                         in_=ya[:, k, :]).then_inc(os_.h, 16)
            os_.n += 16
            ya_free[k] = os_.n

    sp.wait_ge(b.out.h, b.out.n)
    sp.wait_ge(b.out2.h, b.out2.n)


def hostprep_d(inputs):
    return dict(wT_d=np.ascontiguousarray(np.asarray(inputs["L1_ffn_weight"]).T),
                ident=np.eye(128, dtype=BF))


D_IO = [("wT_d", [D, D], F32), ("ident", [128, 128], BF16)]


def build_d_program():
    b = B()
    io = {n: b.dram(n, s, d, "ExternalInput") for n, s, d in D_IO}
    io["x3"] = b.dram("x3", [T, D], F32, "ExternalInput")
    io["out"] = b.dram("out", [T, D], F32, "ExternalOutput")
    with b.ctx:
        blk = b.ctx.enter_context(b.nc.Block())

        @blk.gpsimd
        def _(g):
            build_stage_d(b, io)
    return b.nc


_PROG_CACHE = {}


def _get_prog(name, builder):
    if name not in _PROG_CACHE:
        _PROG_CACHE[name] = builder()
    return _PROG_CACHE[name]


def kernel(**inputs):
    return kernel_fused(**inputs)


def kernel_unfused(**inputs):
    x = np.asarray(inputs["L0_attn_x"])
    ncores = 8
    hpa = hostprep_a(inputs)
    hpb = hostprep_b(inputs)
    hpc = hostprep_c(inputs)
    hpd = hostprep_d(inputs)

    nc_a = _get_prog("a", build_a_program)
    in_maps = [dict(hpa, x0=np.ascontiguousarray(x[c])) for c in range(ncores)]
    res = run_bass_kernel_spmd(nc_a, in_maps, core_ids=list(range(ncores)))
    x1 = [np.asarray(res.results[c]["x1"]) for c in range(ncores)]

    nc_b = _get_prog("b", build_b_program)
    in_maps = [dict(hpb, x1=x1[c]) for c in range(ncores)]
    res = run_bass_kernel_spmd(nc_b, in_maps, core_ids=list(range(ncores)))
    x2 = [np.asarray(res.results[c]["x2"]) for c in range(ncores)]

    nc_c = _get_prog("c", build_c_program)
    in_maps = [dict(hpc, x2=x2[c]) for c in range(ncores)]
    res = run_bass_kernel_spmd(nc_c, in_maps, core_ids=list(range(ncores)))
    x3 = [np.asarray(res.results[c]["x3"]) for c in range(ncores)]

    nc_d = _get_prog("d", build_d_program)
    in_maps = [dict(hpd, x3=x3[c]) for c in range(ncores)]
    res = run_bass_kernel_spmd(nc_d, in_maps, core_ids=list(range(ncores)))
    out = np.stack([np.asarray(res.results[c]["out"]) for c in range(ncores)])
    return out



def _stage_barrier(b):
    sems = [b.mm, b.ea, b.ev, b.ep, b.out, b.out2, b.cdma, b.pdma,
            b.xdma, b.xdma2, b.wd[0], b.wd[1], b.hd[0], b.hd[1]]
    engs = [b.pe, b.nc.scalar, b.nc.vector, b.nc.gpsimd, b.sp]
    for e in engs:
        for sm in sems:
            if sm.n:
                e.wait_ge(sm.h, sm.n)


def build_fused_program(n_cores=8):
    b = B()
    io = {}
    for n, sh, dt in A_IO + B_IO + C_IO + D_IO:
        if n not in io:
            io[n] = b.dram(n, sh, dt, "ExternalInput")
    io["out"] = b.dram("out", [T, D], F32, "ExternalOutput")
    for n in ["x1", "x2", "x3"]:
        io[n] = b.dram(n, [T, D], F32, "Internal")
    io["qT_h"] = b.dram("qT_h", [D, T], BF16, "Internal")
    io["kT_h"] = b.dram("kT_h", [D, T], BF16, "Internal")
    io["v_h"] = b.dram("v_h", [T, D], BF16, "Internal")
    io["cc_in"] = b.dram("cc_in", [1, 1], F32, "Internal")
    io["cc_out"] = b.dram("cc_out", [1, 1], F32, "Internal")
    with b.ctx:
        blk = b.ctx.enter_context(b.nc.Block())

        @blk.gpsimd
        def _(g):
            for fn in [build_stage_a, build_stage_b,
                       lambda bb, ii: build_stage_c(bb, ii, n_cores=n_cores),
                       build_stage_d]:
                with ExitStack() as sc:
                    b.cur = sc
                    fn(b, io)
                b.cur = None
                _stage_barrier(b)
    return b.nc


def kernel_fused(**inputs):
    x = np.asarray(inputs["L0_attn_x"])
    ncores = 8
    hp = {}
    hp.update(hostprep_a(inputs))
    hp.update(hostprep_b(inputs))
    hp.update(hostprep_c(inputs))
    hp.update(hostprep_d(inputs))
    nc = _get_prog("fused", build_fused_program)
    in_maps = [dict(hp, x0=np.ascontiguousarray(x[c])) for c in range(ncores)]
    res = run_bass_kernel_spmd(nc, in_maps, core_ids=list(range(ncores)))
    return np.stack([np.asarray(res.results[c]["out"]) for c in range(ncores)])


# version 06
# speedup vs baseline: 1.0157x; 1.0157x over previous
import math
from contextlib import ExitStack

import numpy as np
import ml_dtypes

import concourse.bass as bass
import concourse.mybir as mybir
from concourse.bass_utils import run_bass_kernel_spmd

F32 = mybir.dt.float32
BF16 = mybir.dt.bfloat16
AF = mybir.ActivationFunctionType
ALU = mybir.AluOpType
AX = mybir.AxisListType
BF = ml_dtypes.bfloat16

A_MAXPH = 6
A_P5MODE = 4
A_NCH = None
A_SKIP_P2 = False
A_SKIP_P4 = False
A_SKIP_P1 = False
T = 4096
D = 1024
NH = 16
HD = 64
NE = 16
CH = 1024
NCH = T // CH
NT = CH // 128
EPS = 1e-5
BIG = 1.0e4


class Sem:
    def __init__(self, nc, name):
        self.h = nc.alloc_semaphore(name)
        self.n = 0

    def inc(self, inst, k=1):
        inst.then_inc(self.h, k)
        self.n += k
        return self.n


_WRAP = {
    "activation", "sign", "square", "sqrt", "mul", "copy", "add", "memzero",
    "memset", "tensor_copy", "tensor_tensor", "tensor_add", "tensor_sub",
    "tensor_mul", "tensor_max", "tensor_scalar", "tensor_single_scalar",
    "tensor_scalar_add", "tensor_scalar_sub", "tensor_scalar_mul",
    "tensor_scalar_max", "tensor_scalar_min", "tensor_relu", "tensor_reduce",
    "reduce_max", "reduce_sum", "scalar_tensor_tensor", "tensor_tensor_scan",
    "select", "copy_predicated", "transpose", "reciprocal", "iota",
    "affine_select",
}


class SEng:

    def __init__(self, eng, sem):
        self._e = eng
        self._s = sem

    def __getattr__(self, name):
        attr = getattr(self._e, name)
        if name in _WRAP:
            def f(*a, **k):
                self._e.wait_ge(self._s.h, self._s.n)
                inst = attr(*a, **k)
                self._s.inc(inst)
                return inst
            return f
        return attr


class B:

    def __init__(self):
        nc = bass.Bass("TRN2", target_bir_lowering=False)
        self.nc = nc
        self.ctx = ExitStack()
        self.cur = None
        self.pe = nc.tensor
        self.sp = nc.sync
        self.dma = Sem(nc, "dma")
        self.mm = Sem(nc, "mm")
        self.ea = Sem(nc, "ea")
        self.ev = Sem(nc, "ev")
        self.ep = Sem(nc, "ep")
        self.act = SEng(nc.scalar, self.ea)
        self.dve = SEng(nc.vector, self.ev)
        self.pool = SEng(nc.gpsimd, self.ep)
        self.out = Sem(nc, "outs")
        self.pdma = Sem(nc, "pdma")
        self.cdma = Sem(nc, "cdma")
        self.xdma = Sem(nc, "xdma")
        self.xdma2 = Sem(nc, "xdma2")
        self.wd = [Sem(nc, "wd0"), Sem(nc, "wd1")]
        self.hd = [Sem(nc, "hd0"), Sem(nc, "hd1")]
        self.out2 = Sem(nc, "outs2")

    def dram(self, name, shape, dtype, kind):
        return self.nc.dram_tensor(name, list(shape), dtype, kind=kind).ap()

    def sb(self, name, shape, dtype):
        return (self.cur or self.ctx).enter_context(
            self.nc.sbuf_tensor(name, list(shape), dtype))

    def ps(self, name, shape, dtype=F32):
        return (self.cur or self.ctx).enter_context(
            self.nc.psum_tensor(name, list(shape), dtype))


def _ln_stats(b, src, col):
    b.dve.tensor_reduce(b.a_s1[:, col:col + 1], src, axis=AX.X, op=ALU.add)
    b.act.activation(b.a_sqscr[:], src, AF.Square, accum_out=b.a_ssq[:, col:col + 1])


def _ln_mr(b, cols):
    dve, act, ea, ev = b.dve, b.act, b.ea, b.ev
    s = cols
    dve.tensor_scalar_mul(b.a_mcol[:, s], b.a_s1[:, s], 1.0 / D)
    dve.tensor_scalar_mul(b.a_vcol[:, s], b.a_ssq[:, s], 1.0 / D)
    dve.tensor_tensor(b.a_rcol[:, s], b.a_mcol[:, s], b.a_mcol[:, s], ALU.mult)
    dve.tensor_tensor(b.a_vcol[:, s], b.a_vcol[:, s], b.a_rcol[:, s], ALU.subtract)
    act.wait_ge(ev.h, ev.n)
    act.activation(b.a_rcol[:, s], b.a_vcol[:, s], AF.Sqrt, bias=b.a_ecol[:])
    dve.wait_ge(ea.h, ea.n)
    dve.reciprocal(b.a_rcol[:, s], b.a_rcol[:, s])


def build_stage_a(b, io):
    CH = 512
    NCH = A_NCH if A_NCH else T // CH
    NT = CH // 128
    nc = b.nc
    pe, act, dve, pool, sp = b.pe, b.act, b.dve, b.pool, b.sp
    dma, mm, ea, ev, ep = b.dma, b.mm, b.ea, b.ev, b.ep

    xa = b.sb("a_x", [128, NT, D], F32)
    xn = b.sb("a_xn", [128, NT, D], BF16)
    xnT = b.sb("a_xnT", [128, 8, CH], BF16)
    qf = b.sb("a_qf", [128, 8, CH], BF16)
    kf = b.sb("a_kf", [128, 8, 128 + CH], BF16)
    vt = b.sb("a_vt", [128, 1 + NT, D], BF16)
    msgT = b.sb("a_msgT", [128, 8, CH], BF16)
    tau = b.sb("a_tau", [128, 2048], BF16)
    wq_sb = b.sb("a_wq", [128, 8, D], BF16)
    wk_sb = b.sb("a_wk", [128, 8, D], BF16)
    wv_sb = b.sb("a_wv", [128, 8, D], BF16)
    wo_sb = b.sb("a_wo", [128, 8, D], BF16)
    mask_sb = b.sb("a_mask", [128, 2048], BF16)
    id_sb = b.sb("a_id", [128, 128], BF16)
    grow = b.sb("a_grow", [1, D], F32)
    brow = b.sb("a_brow", [1, D], F32)
    ones1 = b.sb("a_ones1", [1, 128], F32)
    gbc = b.sb("a_gbc", [128, D], F32)
    bbc = b.sb("a_bbc", [128, D], F32)
    b.a_s1 = b.sb("a_s1", [128, 16], F32)
    b.a_ssq = b.sb("a_ssq", [128, 16], F32)
    b.a_mcol = b.sb("a_mcol", [128, 16], F32)
    b.a_vcol = b.sb("a_vcol", [128, 16], F32)
    b.a_rcol = b.sb("a_rcol", [128, 16], F32)
    b.a_sqscr = b.sb("a_sqscr", [128, D], BF16)
    b.a_ecol = b.sb("a_ecol", [128, 1], F32)
    ya = b.sb("a_ya", [128, D], F32)
    yb = b.sb("a_yb", [128, D], F32)

    pp = b.ps("a_pp", [128, D])
    pp2 = b.ps("a_pp2", [128, D])
    psc = b.ps("a_psc", [128, 2048])

    pool.memset(ones1[:], 1.0)
    pool.memset(b.a_ecol[:], EPS)
    pool.memset(kf[:], 0.0)
    pool.memset(vt[:], 0.0)
    pn = ep.n
    for dst, src in [(id_sb, io["ident"]), (mask_sb, io["maskAB"])]:
        sp.dma_start(out=dst[:], in_=src).then_inc(b.cdma.h, 16)
        b.cdma.n += 16
    sp.dma_start(out=grow[:], in_=io["gamma"].unsqueeze(0)).then_inc(b.cdma.h, 16)
    sp.dma_start(out=brow[:], in_=io["beta"].unsqueeze(0)).then_inc(b.cdma.h, 16)
    b.cdma.n += 32
    for wsb, wdr in [(wq_sb, io["wqT"]), (wk_sb, io["wkT"]), (wv_sb, io["wvT"]),
                     (wo_sb, io["woT"])]:
        pool.dma_start(out=wsb[:], in_=wdr.rearrange("(kc p) n -> p kc n", p=128)
                       ).then_inc(b.pdma.h, 16)
        b.pdma.n += 16
    w_loaded = b.pdma.n

    pe.wait_ge(b.cdma.h, b.cdma.n)
    pe.wait_ge(b.pdma.h, w_loaded)
    pe.wait_ge(ep.h, pn)
    for half in range(2):
        sl = slice(half * 512, (half + 1) * 512)
        pe.matmul(pp[:, sl], ones1[:], grow[:, sl], start=True, stop=True)
        last = pe.matmul(pp2[:, sl], ones1[:], brow[:, sl], start=True, stop=True)
    gb_mm = mm.inc(last)
    dve.wait_ge(mm.h, gb_mm)
    dve.wait_ge(ep.h, pn)
    dve.tensor_copy(gbc[:], pp[:])
    dve.tensor_copy(bbc[:], pp2[:])
    gb_ev = ev.n

    dve.wait_ge(b.cdma.h, b.cdma.n)
    act.wait_ge(b.cdma.h, b.cdma.n)
    free_pp = gb_ev
    free_pp2 = gb_ev
    free_psc_ea = 0
    free_tau_mm = 0

    for c in range(NCH):
        sp.wait_ge(b.out.h, b.out.n)
        sp.dma_start(out=xa[:], in_=io["x0"][c * CH:(c + 1) * CH, :]
                     .rearrange("(i p) d -> p i d", p=128)).then_inc(b.xdma.h, 16)
        b.xdma.n += 16
        x_loaded = b.xdma.n

        if A_MAXPH < 1:
            sp.wait_ge(b.xdma.h, b.xdma.n)
            sp.dma_start(out=io["x1"][c * CH:(c + 1) * CH, :]
                         .rearrange("(i p) d -> p i d", p=128), in_=xa[:]).then_inc(b.out.h, 16)
            b.out.n += 16
            continue
        dve.wait_ge(b.xdma.h, x_loaded)
        act.wait_ge(b.xdma.h, x_loaded)
        if A_SKIP_P1:
            pool.memset(xn[:], 0.01)
            dve.wait_ge(ep.h, ep.n)
        el_dummy = 0
        if not A_SKIP_P1:
            for i in range(NT):
                _ln_stats(b, xa[:, i, :], i)
            dve.wait_ge(ea.h, ea.n)
            _ln_mr(b, slice(0, NT))
            for i in range(NT):
                dve.tensor_scalar(xn[:, i, :], xa[:, i, :], b.a_mcol[:, i:i + 1],
                                  b.a_rcol[:, i:i + 1], ALU.subtract, ALU.mult)
        ev_xn = ev.n

        if A_MAXPH < 2:
            sp.wait_ge(ev.h, ev.n)
            sp.dma_start(out=io["x1"][c * CH:(c + 1) * CH, :]
                         .rearrange("(i p) d -> p i d", p=128), in_=xa[:]).then_inc(b.out.h, 16)
            b.out.n += 16
            continue
        if A_SKIP_P2:
            pool.memset(xnT[:], 0.01)
            dve.wait_ge(ep.h, ep.n)
            dve.tensor_copy(b.a_s1[:, 15:16], b.a_s1[:, 15:16])
        else:
            ptv = [pp[:, 0:64].bitcast(BF16), pp2[:, 0:64].bitcast(BF16)]
            free_pt = [free_pp, free_pp2]
            pe.wait_ge(ev.h, ev_xn)
            for u in range(NT * 8):
                i, fc = divmod(u, 8)
                k = u % 2
                pe.wait_ge(ev.h, free_pt[k])
                t_mm = mm.inc(pe.transpose(ptv[k], xn[:, i, fc * 128:(fc + 1) * 128], id_sb[:]))
                dve.wait_ge(mm.h, t_mm)
                dve.tensor_copy(xnT[:, fc, i * 128:(i + 1) * 128], ptv[k])
                free_pt[k] = ev.n
            free_pp, free_pp2 = free_pt

        if c > 0:
            dve.tensor_copy(kf[:, :, 0:128], kf[:, :, CH:CH + 128])
            dve.tensor_copy(vt[:, 0, :], vt[:, NT, :])
        else:
            dve.wait_ge(ep.h, pn)
        ev_carry = ev.n
        pe.wait_ge(ev.h, ev_carry)

        if A_MAXPH < 4:
            sp.wait_ge(ev.h, ev.n)
            sp.dma_start(out=io["x1"][c * CH:(c + 1) * CH, :]
                         .rearrange("(i p) d -> p i d", p=128), in_=xa[:]).then_inc(b.out.h, 16)
            b.out.n += 16
            continue
        if A_SKIP_P4:
            pool.memset(qf[:], 0.01)
            pool.memset(kf[:], 0.01)
            pool.memset(vt[:], 0.01)
            pe.wait_ge(ep.h, ep.n)
        pbuf = [pp, pp2]
        pfree = [free_pp, free_pp2]
        u = 0
        for wsb, kind in ([] if A_SKIP_P4 else [(wq_sb, "q"), (wk_sb, "k"), (wv_sb, "v")]):
            nm = NT if kind == "v" else 8
            for m in range(nm):
                k = u % 2
                u += 1
                ppu = pbuf[k]
                pe.wait_ge(ev.h, pfree[k])
                if kind == "v":
                    for kc in range(8):
                        for half in range(2):
                            sl = slice(half * 512, (half + 1) * 512)
                            last = pe.matmul(ppu[:, sl], xnT[:, kc, m * 128:(m + 1) * 128],
                                             wv_sb[:, kc, sl], start=(kc == 0), stop=(kc == 7))
                else:
                    for kc in range(8):
                        last = pe.matmul(ppu[:, 0:CH], wsb[:, kc, m * 128:(m + 1) * 128],
                                         xnT[:, kc, :], start=(kc == 0), stop=(kc == 7))
                m_mm = mm.inc(last)
                dve.wait_ge(mm.h, m_mm)
                if kind == "q":
                    dve.tensor_copy(qf[:, m, :], ppu[:, 0:CH])
                elif kind == "k":
                    dve.tensor_copy(kf[:, m, 128:128 + CH], ppu[:, 0:CH])
                else:
                    dve.tensor_copy(vt[:, 1 + m, :], ppu[:])
                pfree[k] = ev.n
        free_pp, free_pp2 = pfree
        pe.wait_ge(ev.h, ev.n)

        if A_MAXPH < 5:
            sp.wait_ge(ev.h, ev.n)
            sp.dma_start(out=io["x1"][c * CH:(c + 1) * CH, :]
                         .rearrange("(i p) d -> p i d", p=128), in_=xa[:]).then_inc(b.out.h, 16)
            b.out.n += 16
            continue
        for qt in range(NT):
            for half in range(2):
                if A_P5MODE == 0:
                    pe.wait_ge(ev.h, ev.n)
                else:
                    pe.wait_ge(ea.h, free_psc_ea)
                for hh in [0, 2, 4, 6, 1, 3, 5, 7]:
                    h = half * 8 + hh
                    fc, r0 = h // 2, 64 * (h % 2)
                    for piece in range(2):
                        kb = qt + piece
                        c0 = hh * 256 + piece * 128
                        last = pe.matmul(psc[:, c0:c0 + 128],
                                         kf[r0:r0 + 64, fc, kb * 128:(kb + 1) * 128],
                                         qf[r0:r0 + 64, fc, qt * 128:(qt + 1) * 128],
                                         start=True, stop=True)
                sc_mm = mm.inc(last)
                act.wait_ge(mm.h, sc_mm)
                act.wait_ge(mm.h, free_tau_mm)
                if A_P5MODE == 0:
                    dve.tensor_copy(tau[:], psc[:])
                    free_psc_ea = 0
                    continue
                act.activation(tau[:], psc[:], AF.Sigmoid, scale=0.125)
                free_psc_ea = ea.n
                if A_P5MODE < 2:
                    continue
                dve.wait_ge(ea.h, free_psc_ea)
                dve.tensor_tensor(tau[:], tau[:], mask_sb[:], ALU.mult)
                ev_tau = ev.n
                if A_P5MODE < 3:
                    free_tau_mm = mm.n
                    continue
                pe.wait_ge(ev.h, ev_tau)
                pe.wait_ge(ev.h, free_pp)
                for hh in range(8):
                    h = half * 8 + hh
                    for piece in range(2):
                        kb = qt + piece
                        last = pe.matmul(pp[0:64, hh * 128:(hh + 1) * 128],
                                         vt[:, kb, h * 64:(h + 1) * 64],
                                         tau[:, hh * 256 + piece * 128:hh * 256 + piece * 128 + 128],
                                         start=(piece == 0), stop=(piece == 1))
                free_tau_mm = mm.inc(last)
                if A_P5MODE < 4:
                    continue
                dve.wait_ge(mm.h, free_tau_mm)
                for hh in range(8):
                    h = half * 8 + hh
                    fc, r0 = h // 2, 64 * (h % 2)
                    dve.tensor_copy(msgT[r0:r0 + 64, fc, qt * 128:(qt + 1) * 128],
                                    pp[0:64, hh * 128:(hh + 1) * 128])
                free_pp = ev.n

        if A_MAXPH < 6:
            sp.wait_ge(ev.h, ev.n)
            sp.dma_start(out=io["x1"][c * CH:(c + 1) * CH, :]
                         .rearrange("(i p) d -> p i d", p=128), in_=xa[:]).then_inc(b.out.h, 16)
            b.out.n += 16
            continue
        ev_msg = ev.n
        pe.wait_ge(ev.h, ev_msg)
        for i in range(NT):
            pe.wait_ge(ev.h, free_pp2)
            for kc in range(8):
                for half in range(2):
                    sl = slice(half * 512, (half + 1) * 512)
                    last = pe.matmul(pp2[:, sl], msgT[:, kc, i * 128:(i + 1) * 128],
                                     wo_sb[:, kc, sl], start=(kc == 0), stop=(kc == 7))
            o_mm = mm.inc(last)
            dve.wait_ge(mm.h, o_mm)
            dve.scalar_tensor_tensor(ya[:], pp2[:], 1.0, xn[:, i, :], ALU.mult, ALU.add)
            free_pp2 = ev.n
            act.wait_ge(ev.h, free_pp2)
            _ln_stats(b, ya[:], 8)
            dve.wait_ge(ea.h, ea.n)
            _ln_mr(b, slice(8, 9))
            dve.tensor_scalar(yb[:], ya[:], b.a_mcol[:, 8:9], b.a_rcol[:, 8:9],
                              ALU.subtract, ALU.mult)
            dve.scalar_tensor_tensor(yb[:], yb[:], 1.0, gbc[:], ALU.mult, ALU.mult)
            dve.tensor_tensor(ya[:], yb[:], bbc[:], ALU.add)
            ev_ao = ev.n
            act.wait_ge(ev.h, ev_ao)
            _ln_stats(b, ya[:], 9)
            dve.wait_ge(ea.h, ea.n)
            _ln_mr(b, slice(9, 10))
            dve.tensor_scalar(yb[:], ya[:], b.a_mcol[:, 9:10], b.a_rcol[:, 9:10],
                              ALU.subtract, ALU.mult)
            dve.tensor_tensor(xa[:, i, :], xa[:, i, :], yb[:], ALU.add)

        sp.wait_ge(ev.h, ev.n)
        sp.dma_start(out=io["x1"][c * CH:(c + 1) * CH, :]
                     .rearrange("(i p) d -> p i d", p=128), in_=xa[:]).then_inc(b.out.h, 16)
        b.out.n += 16

    sp.wait_ge(b.out.h, b.out.n)


def hostprep_a(inputs):
    wq = np.ascontiguousarray(np.asarray(inputs["L0_attn_wq"]).T)
    wkv = np.asarray(inputs["L0_attn_wkv"]).reshape(NH, 2 * HD, D)
    wkT = np.ascontiguousarray(wkv[:, :HD].reshape(D, D).T)
    wvT = np.ascontiguousarray(wkv[:, HD:].reshape(D, D).T)
    woT = np.ascontiguousarray(np.asarray(inputs["L0_attn_wo"]).T)
    ki = np.arange(128)[:, None]
    qi = np.arange(128)[None, :]
    mA = ((ki - qi) >= 112).astype(BF)
    mB = (((ki - qi) >= -16) & ((ki - qi) <= -1)).astype(BF)
    maskAB = np.ascontiguousarray(np.concatenate([np.concatenate([mA, mB], 1)] * 8, 1))
    return dict(wqT=wq, wkT=wkT, wvT=wvT, woT=woT,
                gamma=np.asarray(inputs["L0_attn_gamma"]),
                beta=np.asarray(inputs["L0_attn_beta"]),
                maskAB=maskAB, ident=np.eye(128, dtype=BF))


A_IO = [("x0", [T, D], F32), ("wqT", [D, D], F32), ("wkT", [D, D], F32),
        ("wvT", [D, D], F32), ("woT", [D, D], F32), ("gamma", [D], F32),
        ("beta", [D], F32), ("maskAB", [128, 2048], BF16), ("ident", [128, 128], BF16)]


def build_a_program():
    b = B()
    io = {n: b.dram(n, s, d, "ExternalInput") for n, s, d in A_IO}
    io["x1"] = b.dram("x1", [T, D], F32, "ExternalOutput")
    with b.ctx:
        blk = b.ctx.enter_context(b.nc.Block())

        @blk.gpsimd
        def _(g):
            build_stage_a(b, io)
    return b.nc




def _free_as_ev(b, fr):
    sm, cnt = fr
    if sm is b.ev:
        return cnt
    b.dve.wait_ge(sm.h, cnt)
    return b.ev.n


def build_stage_b(b, io):
    CH = 1024
    NCH = T // CH
    NT = CH // 128
    nc = b.nc
    pe, act, dve, pool, sp = b.pe, b.act, b.dve, b.pool, b.sp
    dma, mm, ea, ev, ep = b.dma, b.mm, b.ea, b.ev, b.ep

    xt = b.sb("b_xt", [128, 2, D], F32)
    xn = b.sb("b_xn", [128, NT, D], BF16)
    xnT = b.sb("b_xnT", [128, 8, CH], BF16)
    acc = b.sb("b_acc", [128, NT, D], F32)
    h_sb = b.sb("b_h", [128, 8, CH], BF16)
    h1s = b.sb("b_h1s", [128, CH], BF16)
    wsl = [[b.sb(f"b_w{m}_{p}", [128, 8, D], BF16) for m in range(3)] for p in range(2)]
    gw_sb = b.sb("b_gw", [128, 8, 16], BF16)
    logits = b.sb("b_logits", [128, NT, 16], F32)
    g_all = b.sb("b_g", [128, NT, 16], F32)
    bwork = b.sb("b_bwork", [128, 16], F32)
    selm = b.sb("b_selm", [128, 16], F32)
    t1 = b.sb("b_t1", [128, 16], F32)
    t2 = b.sb("b_t2", [128, 16], F32)
    lmrow = b.sb("b_lm", [128, 16], F32)
    den = b.sb("b_den", [128, 1], F32)
    ebias_bc = b.sb("b_ebias", [128, 16], F32)
    iota_bc = b.sb("b_iota", [128, 16], F32)
    iotamB_bc = b.sb("b_iotamB", [128, 16], F32)
    gones = b.sb("b_gones", [128, 1], F32)
    ebrow = b.sb("b_ebrow", [1, 16], F32)
    ones1 = b.sb("b_ones1", [1, 128], F32)
    id_sb = b.sb("b_id", [128, 128], BF16)
    b.a_s1 = b.sb("b_s1", [128, 16], F32)
    b.a_ssq = b.sb("b_ssq", [128, 16], F32)
    b.a_mcol = b.sb("b_mcol", [128, 16], F32)
    b.a_vcol = b.sb("b_vcol", [128, 16], F32)
    b.a_rcol = b.sb("b_rcol", [128, 16], F32)
    b.a_sqscr = b.sb("b_sqscr", [128, D], BF16)
    b.a_ecol = b.sb("b_ecol", [128, 1], F32)
    ya = b.sb("b_ya", [128, 2, D], F32)

    pp = b.ps("b_pp", [128, D])
    pp2 = b.ps("b_pp2", [128, D])
    pp3 = b.ps("b_pp3", [128, D])
    pr = b.ps("b_pr", [128, 16])
    pr2 = b.ps("b_pr2", [128, 16])

    pool.memset(ones1[:], 1.0)
    pool.memset(b.a_ecol[:], EPS)
    pool.memset(gones[:], 1.0)
    pn = ep.n
    for dst, src in [(id_sb[:], io["ident"]), (iota_bc[:], io["iota16"]),
                     (iotamB_bc[:], io["iotamB16"]),
                     (ebrow[:], io["expert_bias"].unsqueeze(0))]:
        sp.dma_start(out=dst, in_=src).then_inc(b.cdma.h, 16)
        b.cdma.n += 16
    pool.dma_start(out=gw_sb[:], in_=io["gate_wT"].rearrange("(kc p) n -> p kc n", p=128)
                   ).then_inc(b.pdma.h, 16)
    b.pdma.n += 16
    gw_loaded = b.pdma.n
    pe.wait_ge(b.cdma.h, b.cdma.n)
    pe.wait_ge(ep.h, pn)
    last = pe.matmul(pr[:], ones1[:], ebrow[:], start=True, stop=True)
    eb_mm = mm.inc(last)
    dve.wait_ge(mm.h, eb_mm)
    dve.tensor_copy(ebias_bc[:], pr[:])
    dve.wait_ge(b.cdma.h, b.cdma.n)
    ev_const = ev.n

    worder = [("s", None)] + [("e", e) for e in range(NE)]
    loaded = {}
    compute_done = {}

    def issue_weight_load(idx):
        kind, e = worder[idx]
        par = idx % 2
        ws = b.wd[par]
        if idx >= 2:
            pool.wait_ge(mm.h, compute_done[idx - 2])
        for j, nm in enumerate(["w1", "w3", "w2"]):
            src = io["s" + nm + "T"] if kind == "s" else io[nm + "T"][e]
            pool.dma_start(out=wsl[par][j][:],
                           in_=src.rearrange("(kc p) n -> p kc n", p=128)
                           ).then_inc(ws.h, 16)
            ws.n += 16
        loaded[idx] = ws.n

    free_pp = ev_const
    free_pp2 = ev_const
    free_pr_ev = [ev.n, ev.n]

    for c in range(NCH):
        pool.wait_ge(ev.h, ev.n)
        pool.wait_ge(ea.h, ea.n)
        pool.memset(acc[:], 0.0)
        acc_zero_ep = ep.n

        xt_free = [b.ev.n, b.ev.n]
        for i in range(NT):
            k = i % 2
            xs = b.xdma if k == 0 else b.xdma2
            sp.wait_ge(ev.h, xt_free[k])
            sp.dma_start(out=xt[:, k, :], in_=io["x1"][c * CH + i * 128:c * CH + (i + 1) * 128, :]
                         ).then_inc(xs.h, 16)
            xs.n += 16
            dve.wait_ge(xs.h, xs.n)
            act.wait_ge(xs.h, xs.n)
            _ln_stats(b, xt[:, k, :], i % 8)
            dve.wait_ge(ea.h, ea.n)
            _ln_mr(b, slice(i % 8, i % 8 + 1))
            dve.tensor_scalar(xn[:, i, :], xt[:, k, :], b.a_mcol[:, i % 8:i % 8 + 1],
                              b.a_rcol[:, i % 8:i % 8 + 1], ALU.subtract, ALU.mult)
            xt_free[k] = ev.n

        ptv = [pp[:, 0:64].bitcast(BF16), pp2[:, 0:64].bitcast(BF16)]
        free_pt = [free_pp, free_pp2]
        pe.wait_ge(ev.h, ev.n)
        for u in range(NT * 8):
            i, fc = divmod(u, 8)
            k = u % 2
            pe.wait_ge(ev.h, free_pt[k])
            t_mm = mm.inc(pe.transpose(ptv[k], xn[:, i, fc * 128:(fc + 1) * 128], id_sb[:]))
            dve.wait_ge(mm.h, t_mm)
            dve.tensor_copy(xnT[:, fc, i * 128:(i + 1) * 128], ptv[k])
            free_pt[k] = ev.n
        free_pp, free_pp2 = free_pt

        issue_weight_load(0)
        issue_weight_load(1)

        pe.wait_ge(ev.h, ev.n)
        pe.wait_ge(b.pdma.h, gw_loaded)
        for i in range(NT):
            pru = pr if i % 2 == 0 else pr2
            pe.wait_ge(ev.h, free_pr_ev[i % 2])
            for kc in range(8):
                last = pe.matmul(pru[:], xnT[:, kc, i * 128:(i + 1) * 128],
                                 gw_sb[:, kc, :], start=(kc == 0), stop=(kc == 7))
            r_mm = mm.inc(last)
            dve.wait_ge(mm.h, r_mm)
            dve.tensor_copy(logits[:, i, :], pru[:])
            free_pr_ev[i % 2] = ev.n
            dve.tensor_tensor(bwork[:], logits[:, i, :], ebias_bc[:], ALU.add)
            dve.memset(selm[:], 0.0)
            for kk in range(4):
                dve.tensor_reduce(den[:], bwork[:], axis=AX.X, op=ALU.max)
                dve.tensor_scalar(t1[:], bwork[:], den[:], None, ALU.is_equal)
                dve.tensor_tensor(t2[:], t1[:], iotamB_bc[:], ALU.mult)
                dve.tensor_scalar_add(t2[:], t2[:], BIG)
                dve.tensor_reduce(den[:], t2[:], axis=AX.X, op=ALU.min)
                dve.tensor_scalar(t1[:], iota_bc[:], den[:], None, ALU.is_equal)
                dve.tensor_tensor(selm[:], selm[:], t1[:], ALU.add)
                dve.scalar_tensor_tensor(bwork[:], t1[:], -BIG, bwork[:], ALU.mult, ALU.add)
            dve.tensor_scalar_add(t1[:], logits[:, i, :], -BIG)
            dve.scalar_tensor_tensor(lmrow[:], selm[:], BIG, t1[:], ALU.mult, ALU.add)
            dve.tensor_reduce(den[:], lmrow[:], axis=AX.X, op=ALU.max)
            dve.tensor_scalar_mul(den[:], den[:], -1.0)
            act.wait_ge(ev.h, ev.n)
            act.activation(g_all[:, i, :], lmrow[:], AF.Exp, bias=den[:],
                           accum_out=b.a_vcol[:, 15:16])
            dve.wait_ge(ea.h, ea.n)
            dve.reciprocal(den[:], b.a_vcol[:, 15:16])
            dve.tensor_scalar(g_all[:, i, :], g_all[:, i, :], den[:], None, ALU.mult)

        dve.wait_ge(ep.h, acc_zero_ep)
        R = [pp, pp2, pp3]
        buf_free = [(ev, free_pp), (ev, free_pp2), (ev, ev.n)]
        for idx in range(len(worder)):
            kind, e = worder[idx]
            par = idx % 2
            pe.wait_ge(b.wd[par].h, loaded[idx])
            w1s, w3s, w2s = wsl[par]
            for m in range(8):
                ja = (2 * m) % 3
                jb = (2 * m + 1) % 3
                sm, cnt = buf_free[ja]
                pe.wait_ge(sm.h, cnt)
                for kc in range(8):
                    for half in range(2):
                        sl = slice(half * 512, (half + 1) * 512)
                        last = pe.matmul(R[ja][:, sl], w1s[:, kc, m * 128:(m + 1) * 128],
                                         xnT[:, kc, sl], start=(kc == 0), stop=(kc == 7))
                h1_mm = mm.inc(last)
                sm, cnt = buf_free[jb]
                pe.wait_ge(sm.h, cnt)
                for kc in range(8):
                    for half in range(2):
                        sl = slice(half * 512, (half + 1) * 512)
                        last = pe.matmul(R[jb][:, sl], w3s[:, kc, m * 128:(m + 1) * 128],
                                         xnT[:, kc, sl], start=(kc == 0), stop=(kc == 7))
                h3_mm = mm.inc(last)
                act.wait_ge(mm.h, h1_mm)
                act.wait_ge(ev.h, ev.n)
                act.activation(h1s[:], R[ja][:], AF.Silu)
                buf_free[ja] = (ea, ea.n)
                dve.wait_ge(mm.h, h3_mm)
                dve.wait_ge(ea.h, ea.n)
                dve.scalar_tensor_tensor(h_sb[:, m, :], R[jb][:], 1.0, h1s[:],
                                         ALU.mult, ALU.mult)
                buf_free[jb] = (ev, ev.n)
            ev_h = ev.n
            pe.wait_ge(ev.h, ev_h)
            for i in range(NT):
                k = i % 2
                sm, cnt = buf_free[k]
                pe.wait_ge(sm.h, cnt)
                for ic in range(8):
                    for half in range(2):
                        sl = slice(half * 512, (half + 1) * 512)
                        last = pe.matmul(R[k][:, sl], h_sb[:, ic, i * 128:(i + 1) * 128],
                                         w2s[:, ic, sl], start=(ic == 0), stop=(ic == 7))
                d_mm = mm.inc(last)
                dve.wait_ge(mm.h, d_mm)
                gcol = gones[:, 0:1] if kind == "s" else g_all[:, i, e:e + 1]
                dve.scalar_tensor_tensor(acc[:, i, :], R[k][:], gcol, acc[:, i, :],
                                         ALU.mult, ALU.add)
                buf_free[k] = (ev, ev.n)
            compute_done[idx] = mm.n
            if idx + 2 < len(worder):
                issue_weight_load(idx + 2)
        free_pp = _free_as_ev(b, buf_free[0])
        free_pp2 = _free_as_ev(b, buf_free[1])

        act.wait_ge(ev.h, ev.n)
        xt_free = [b.ev.n, b.ev.n]
        ya_free = [b.out.n, b.out2.n]
        for i in range(NT):
            k = i % 2
            xs = b.xdma if k == 0 else b.xdma2
            sp.wait_ge(ev.h, xt_free[k])
            sp.dma_start(out=xt[:, k, :], in_=io["x1"][c * CH + i * 128:c * CH + (i + 1) * 128, :]
                         ).then_inc(xs.h, 16)
            xs.n += 16
            _ln_stats(b, acc[:, i, :], 8 + (i % 2))
            dve.wait_ge(ea.h, ea.n)
            _ln_mr(b, slice(8 + (i % 2), 9 + (i % 2)))
            dve.wait_ge((b.out if k == 0 else b.out2).h, ya_free[k])
            dve.wait_ge(xs.h, xs.n)
            dve.tensor_scalar(ya[:, k, :], acc[:, i, :], b.a_mcol[:, 8 + i % 2:9 + i % 2],
                              b.a_rcol[:, 8 + i % 2:9 + i % 2], ALU.subtract, ALU.mult)
            dve.tensor_tensor(ya[:, k, :], ya[:, k, :], xt[:, k, :], ALU.add)
            xt_free[k] = ev.n
            os_ = b.out if k == 0 else b.out2
            sp.wait_ge(ev.h, ev.n)
            sp.dma_start(out=io["x2"][c * CH + i * 128:c * CH + (i + 1) * 128, :],
                         in_=ya[:, k, :]).then_inc(os_.h, 16)
            os_.n += 16
            ya_free[k] = os_.n

    sp.wait_ge(b.out.h, b.out.n)
    sp.wait_ge(b.out2.h, b.out2.n)


def hostprep_b(inputs):
    iota = np.tile(np.arange(16, dtype=np.float32), (128, 1))
    return dict(
        gate_wT=np.ascontiguousarray(np.asarray(inputs["L0_ffn_gate_w"]).T),
        expert_bias=np.asarray(inputs["L0_ffn_expert_bias"]),
        w1T=np.ascontiguousarray(np.asarray(inputs["L0_ffn_w1"]).transpose(0, 2, 1)),
        w2T=np.ascontiguousarray(np.asarray(inputs["L0_ffn_w2"]).transpose(0, 2, 1)),
        w3T=np.ascontiguousarray(np.asarray(inputs["L0_ffn_w3"]).transpose(0, 2, 1)),
        sw1T=np.ascontiguousarray(np.asarray(inputs["L0_ffn_sw1"]).T),
        sw2T=np.ascontiguousarray(np.asarray(inputs["L0_ffn_sw2"]).T),
        sw3T=np.ascontiguousarray(np.asarray(inputs["L0_ffn_sw3"]).T),
        iota16=iota, iotamB16=np.ascontiguousarray(iota - BIG),
        ident=np.eye(128, dtype=BF))


B_IO = [("gate_wT", [D, 16], F32), ("expert_bias", [16], F32),
        ("w1T", [NE, D, D], F32), ("w2T", [NE, D, D], F32), ("w3T", [NE, D, D], F32),
        ("sw1T", [D, D], F32), ("sw2T", [D, D], F32), ("sw3T", [D, D], F32),
        ("iota16", [128, 16], F32), ("iotamB16", [128, 16], F32),
        ("ident", [128, 128], BF16)]


def build_b_program():
    b = B()
    io = {n: b.dram(n, s, d, "ExternalInput") for n, s, d in B_IO}
    io["x1"] = b.dram("x1", [T, D], F32, "ExternalInput")
    io["x2"] = b.dram("x2", [T, D], F32, "ExternalOutput")
    with b.ctx:
        blk = b.ctx.enter_context(b.nc.Block())

        @blk.gpsimd
        def _(g):
            build_stage_b(b, io)
    return b.nc




def build_stage_c(b, io, n_cores=8):
    CH = 1024
    NCH = T // CH
    NT = CH // 128
    nc = b.nc
    pe, act, dve, pool, sp = b.pe, b.act, b.dve, b.pool, b.sp
    dma, mm, ea, ev, ep = b.dma, b.mm, b.ea, b.ev, b.ep

    qT_h = io["qT_h"]
    kT_h = io["kT_h"]
    v_h = io["v_h"]
    cc_in = io["cc_in"]
    cc_out = io["cc_out"]

    energy = b.sb("c_energy", [16, T], F32)
    mask_all = b.sb("c_mask", [16, T], BF16)
    ones1c = b.sb("c_ones1c", [1, 128], F32)
    mk16 = b.sb("c_mk16", [16, 1], F32)
    qstab = b.sb("c_qstab", [16, T], BF16)
    thr = b.sb("c_thr", [1, 1], F32)
    thr16 = b.sb("c_thr16", [16, 1], F32)
    srow = b.sb("c_srow", [16, 1], F32)
    s11 = b.sb("c_s11", [1, 1], F32)
    ones2 = b.sb("c_ones2", [128, 16], F32)
    selmat = b.sb("c_selmat", [128, 8, 16], BF16)
    eh64 = b.sb("c_eh64", [16, NH * 64], BF16)
    ctx_fm = b.sb("c_ctx", [128, 8, T], BF16)
    id_sb = b.sb("c_id", [128, 128], BF16)
    b.a_s1 = b.sb("c_s1", [128, 16], F32)
    b.a_ssq = b.sb("c_ssq", [128, 16], F32)
    b.a_mcol = b.sb("c_mcol", [128, 16], F32)
    b.a_vcol = b.sb("c_vcol", [128, 16], F32)
    b.a_rcol = b.sb("c_rcol", [128, 16], F32)
    b.a_sqscr = b.sb("c_sqscr", [128, D], BF16)
    b.a_ecol = b.sb("c_ecol", [128, 1], F32)

    pool.memset(ones2[:], 1.0)
    pool.memset(ones1c[:], 1.0)
    pool.memset(b.a_ecol[:], EPS)
    pool.memset(mk16[:], 0.0)
    pn0 = ep.n
    for dst, src in [(id_sb[:], io["ident"]), (selmat[:], io["selmat"]),
                     (eh64[:], io["eh64"])]:
        sp.dma_start(out=dst, in_=src).then_inc(b.cdma.h, 16)
        b.cdma.n += 16
    c_const = b.cdma.n

    with ExitStack() as c1ctx:
        sb1 = lambda n, sh, dt: c1ctx.enter_context(nc.sbuf_tensor(n, sh, dt))
        xt = sb1("c_xt", [128, 2, D], F32)
        xn = sb1("c_xn", [128, NT, D], BF16)
        xnT = sb1("c_xnT", [128, 8, CH], BF16)
        wq_sb = sb1("c_wq", [128, 8, D], BF16)
        wk_sb = sb1("c_wk", [128, 8, D], BF16)
        wv_sb = sb1("c_wv", [128, 8, D], BF16)
        bq_sb = sb1("c_bq", [128, 8], F32)
        bk_sb = sb1("c_bk", [128, 8], F32)
        bvrow = sb1("c_bvrow", [1, D], F32)
        bv_bc = sb1("c_bvbc", [128, D], F32)
        stage = sb1("c_stage", [128, 2, D], BF16)
        sqs = sb1("c_sqs", [128, CH], BF16)
        ek_scr = sb1("c_ekscr", [16, CH], BF16)
        ekcol = sb1("c_ekcol", [16, 1], F32)

        ones1 = sb1("c_ones1", [1, 128], F32)
        with ExitStack() as p1ctx:
            pp = p1ctx.enter_context(nc.psum_tensor("c_pp", [128, D], F32))
            pp2 = p1ctx.enter_context(nc.psum_tensor("c_pp2", [128, D], F32))
            pes = p1ctx.enter_context(nc.psum_tensor("c_pes", [16, CH], F32))

            pool.memset(ones1[:], 1.0)
            pn = ep.n
            for dst, src in [(wq_sb[:], io["wqT1"]), (wk_sb[:], io["wkT1"]),
                             (wv_sb[:], io["wvT1"])]:
                pool.dma_start(out=dst, in_=src.rearrange("(kc p) n -> p kc n", p=128)
                               ).then_inc(b.pdma.h, 16)
                b.pdma.n += 16
            for dst, src in [(bq_sb[:], io["bq1"]), (bk_sb[:], io["bk1"]),
                             (bvrow[:], io["bv1"].unsqueeze(0))]:
                sp.dma_start(out=dst, in_=src).then_inc(b.cdma.h, 16)
                b.cdma.n += 16
            w1_loaded = b.pdma.n
            pe.wait_ge(b.cdma.h, b.cdma.n)
            pe.wait_ge(ep.h, pn)
            for half in range(2):
                sl = slice(half * 512, (half + 1) * 512)
                last = pe.matmul(pp[:, sl], ones1[:], bvrow[:, sl], start=True, stop=True)
            bv_mm = mm.inc(last)
            dve.wait_ge(mm.h, bv_mm)
            dve.tensor_copy(bv_bc[:], pp[:])
            free_pp = ev.n
            free_pp2 = ev.n
            pe.wait_ge(b.pdma.h, w1_loaded)

            for c in range(NCH):
                xt_free = [ev.n, ev.n]
                for i in range(NT):
                    k = i % 2
                    xs = b.xdma if k == 0 else b.xdma2
                    sp.wait_ge(ev.h, xt_free[k])
                    sp.dma_start(out=xt[:, k, :],
                                 in_=io["x2"][c * CH + i * 128:c * CH + (i + 1) * 128, :]
                                 ).then_inc(xs.h, 16)
                    xs.n += 16
                    dve.wait_ge(xs.h, xs.n)
                    act.wait_ge(xs.h, xs.n)
                    _ln_stats(b, xt[:, k, :], i)
                    dve.wait_ge(ea.h, ea.n)
                    _ln_mr(b, slice(i, i + 1))
                    dve.tensor_scalar(xn[:, i, :], xt[:, k, :], b.a_mcol[:, i:i + 1],
                                      b.a_rcol[:, i:i + 1], ALU.subtract, ALU.mult)
                    xt_free[k] = ev.n
                ptv = [pp[:, 0:64].bitcast(BF16), pp2[:, 0:64].bitcast(BF16)]
                free_pt = [free_pp, free_pp2]
                pe.wait_ge(ev.h, ev.n)
                for u in range(NT * 8):
                    i, fc = divmod(u, 8)
                    k = u % 2
                    pe.wait_ge(ev.h, free_pt[k])
                    t_mm = mm.inc(pe.transpose(ptv[k], xn[:, i, fc * 128:(fc + 1) * 128],
                                               id_sb[:]))
                    dve.wait_ge(mm.h, t_mm)
                    dve.tensor_copy(xnT[:, fc, i * 128:(i + 1) * 128], ptv[k])
                    free_pt[k] = ev.n
                free_pp, free_pp2 = free_pt
                pe.wait_ge(ev.h, ev.n)

                for wsb, bcol, isq, hdr in [(wq_sb, bq_sb, True, qT_h),
                                            (wk_sb, bk_sb, False, kT_h)]:
                    st_free = [b.out.n, b.out2.n]
                    for m in range(8):
                        k = m % 2
                        ppu = pp if k == 0 else pp2
                        pe.wait_ge(ev.h, free_pp if k == 0 else free_pp2)
                        for kc in range(8):
                            for half in range(2):
                                sl = slice(half * 512, (half + 1) * 512)
                                last = pe.matmul(ppu[:, sl],
                                                 wsb[:, kc, m * 128:(m + 1) * 128],
                                                 xnT[:, kc, sl],
                                                 start=(kc == 0), stop=(kc == 7))
                        q_mm = mm.inc(last)
                        dve.wait_ge(mm.h, q_mm)
                        os_ = b.out if k == 0 else b.out2
                        dve.wait_ge(os_.h, st_free[k])
                        dve.tensor_scalar(stage[:, k, :], ppu[:], bcol[:, m:m + 1], None,
                                          ALU.add)
                        if k == 0:
                            free_pp = ev.n
                        else:
                            free_pp2 = ev.n
                        act.wait_ge(ev.h, ev.n)
                        act.activation(sqs[:], stage[:, k, :], AF.Square)
                        pe.wait_ge(ea.h, ea.n)
                        for half in range(2):
                            sl = slice(half * 512, (half + 1) * 512)
                            last = pe.matmul(pes[:, sl], selmat[:, m, :], sqs[:, sl],
                                             start=(m == 0), stop=(m == 7))
                        es_mm = mm.inc(last)
                        sp.wait_ge(ev.h, ev.n)
                        sp.dma_start(out=hdr[m * 128:(m + 1) * 128, c * CH:(c + 1) * CH],
                                     in_=stage[:, k, :]).then_inc(os_.h, 16)
                        os_.n += 16
                        st_free[k] = os_.n
                    act.wait_ge(mm.h, mm.n)
                    act.wait_ge(ep.h, pn0)
                    if isq:
                        act.activation(energy[:, c * CH:(c + 1) * CH], pes[:], AF.Sqrt)
                    else:
                        act.activation(ek_scr[:], pes[:], AF.Sqrt)
                        dve.wait_ge(ea.h, ea.n)
                        dve.tensor_reduce(ekcol[:], ek_scr[:], axis=AX.X, op=ALU.max)
                        dve.tensor_tensor(mk16[:], mk16[:], ekcol[:], ALU.max)
                    pe.wait_ge(ea.h, ea.n)
                st_free = [b.out.n, b.out2.n]
                for i in range(NT):
                    k = i % 2
                    ppu = pp if k == 0 else pp2
                    pe.wait_ge(ev.h, free_pp if k == 0 else free_pp2)
                    for kc in range(8):
                        for half in range(2):
                            sl = slice(half * 512, (half + 1) * 512)
                            last = pe.matmul(ppu[:, sl], xnT[:, kc, i * 128:(i + 1) * 128],
                                             wv_sb[:, kc, sl], start=(kc == 0), stop=(kc == 7))
                    v_mm = mm.inc(last)
                    dve.wait_ge(mm.h, v_mm)
                    os_ = b.out if k == 0 else b.out2
                    dve.wait_ge(os_.h, st_free[k])
                    dve.scalar_tensor_tensor(stage[:, k, :], ppu[:], 1.0, bv_bc[:],
                                             ALU.mult, ALU.add)
                    if k == 0:
                        free_pp = ev.n
                    else:
                        free_pp2 = ev.n
                    sp.wait_ge(ev.h, ev.n)
                    sp.dma_start(out=v_h[c * CH + i * 128:c * CH + (i + 1) * 128, :],
                                 in_=stage[:, k, :]).then_inc(os_.h, 16)
                    os_.n += 16
                    st_free[k] = os_.n

    dve.wait_ge(ea.h, ea.n)
    dve.wait_ge(b.cdma.h, b.cdma.n)
    dve.tensor_reduce(srow[:], energy[:], axis=AX.X, op=ALU.add)
    with ExitStack() as ptx:
        p11 = ptx.enter_context(nc.psum_tensor("c_p11", [128, 16], F32))
        pe.wait_ge(ev.h, ev.n)
        pe.wait_ge(ep.h, pn0)
        l_mm = mm.inc(pe.matmul(p11[0:1, 0:1], srow[:], ones2[0:16, 0:1],
                                start=True, stop=True))
        dve.wait_ge(mm.h, l_mm)
        dve.tensor_copy(s11[:], p11[0:1, 0:1])
        pool.wait_ge(ev.h, ev.n)
        pool.dma_start(out=cc_in, in_=s11[:]).then_inc(b.pdma.h, 16)
        b.pdma.n += 16
        pool.wait_ge(b.pdma.h, b.pdma.n)
        pool.collective_compute("AllReduce", ALU.add,
                                replica_groups=[list(range(n_cores))],
                                ins=[cc_in.opt()], outs=[cc_out.opt()]).then_inc(b.ep.h, 1)
        b.ep.n += 1
        pool.wait_ge(b.ep.h, b.ep.n)
        pool.dma_start(out=thr[:], in_=cc_out).then_inc(b.pdma.h, 16)
        b.pdma.n += 16
        dve.wait_ge(b.pdma.h, b.pdma.n)
        dve.tensor_scalar_mul(thr[:], thr[:], 0.1 / (n_cores * T * NH))
        ev_thr = ev.n
        pe.wait_ge(ev.h, ev_thr)
        t_mm = mm.inc(pe.matmul(p11[0:16, 0:1], ones2[0:1, :], thr[:], start=True, stop=True))
        dve.wait_ge(mm.h, t_mm)
        dve.tensor_copy(thr16[:], p11[0:16, 0:1])
        dve.tensor_scalar(mask_all[:], energy[:], thr16[:], None, ALU.is_gt)
        dve.tensor_scalar_mul(ekcol2 := b.a_vcol[0:16, 14:15], mk16[:], -1.0)
        dve.tensor_tensor(energy[:], energy[:], mask_all[:], ALU.mult)
        dve.tensor_scalar(qstab[:], energy[:], b.a_vcol[0:16, 14:15], None, ALU.mult)
        ev_mask = ev.n

    KTG = 2
    with ExitStack() as c2ctx:
        sb2 = lambda n, sh, dt: c2ctx.enter_context(nc.sbuf_tensor(n, sh, dt))
        qh = [sb2(f"c_qh{p}", [64, T], BF16) for p in range(2)]
        kh = [sb2(f"c_kh{p}", [65, T], BF16) for p in range(2)]
        vh = [sb2(f"c_vh{p}", [128, 32, 65], BF16) for p in range(2)]
        qm = [sb2(f"c_qm{p}", [65, 512], BF16) for p in range(2)]
        phat = [sb2(f"c_ph{p}", [128, KTG * 512], BF16) for p in range(2)]
        rrow = sb2("c_rrow", [128, 512], F32)
        recipb = sb2("c_recipb", [64, 512], BF16)
        onesc = sb2("c_onesc", [128, 128], F32)
        with ExitStack() as p2ctx:
            psc2 = [p2ctx.enter_context(nc.psum_tensor(f"c_psc{p}", [128, KTG * 512], F32))
                    for p in range(2)]
            pctx = [p2ctx.enter_context(nc.psum_tensor(f"c_pctx{p}", [65, 512], F32))
                    for p in range(2)]
            pmb = [p2ctx.enter_context(nc.psum_tensor(f"c_pmb{p}", [128, 512], F32))
                   for p in range(2)]

            pool.wait_ge(ev.h, ev.n)
            pool.wait_ge(ea.h, ea.n)
            pool.wait_ge(mm.h, mm.n)
            pool.memset(onesc[:], 1.0)
            pool.memset(vh[0][:], 1.0)
            pool.memset(vh[1][:], 1.0)
            pool.memset(kh[0][:], 1.0)
            pool.memset(kh[1][:], 1.0)
            pn2 = ep.n
            hd_sem = [b.hd[0], b.hd[1]]
            h_loaded = {}
            h_done = {}

            def load_head(h):
                par = h % 2
                ws = hd_sem[par]
                if h >= 2:
                    sp.wait_ge(mm.h, h_done[h - 2])
                sp.dma_start(out=qh[par][:], in_=qT_h[h * 64:(h + 1) * 64, :]
                             ).then_inc(ws.h, 16)
                sp.dma_start(out=kh[par][0:64, :], in_=kT_h[h * 64:(h + 1) * 64, :]
                             ).then_inc(ws.h, 16)
                sp.dma_start(out=vh[par][:, :, 0:64],
                             in_=v_h[:, h * 64:(h + 1) * 64]
                             .rearrange("(kt p) d -> p kt d", p=128)).then_inc(ws.h, 16)
                ws.n += 48
                h_loaded[h] = ws.n

            sp.wait_ge(b.out.h, b.out.n)
            sp.wait_ge(b.out2.h, b.out2.n)
            sp.wait_ge(ep.h, pn2)
            load_head(0)
            load_head(1)
            free_phat = [mm.n, mm.n]
            free_psc_ea = [ea.n, ea.n]
            free_pctx = [ev.n, ev.n]
            qmf = [mm.n, mm.n]
            free_pmb = [ev.n, ev.n]
            ev_qm_s = [ev.n, ev.n]
            ngrp = 32 // KTG
            pending = None
            act.wait_ge(ev.h, ev_mask)
            pairs = [(h, qt) for h in range(NH) for qt in range(8)]

            def emit_prologue(idx):
                ph, pqt = pairs[idx]
                ppar = ph % 2
                qs = pqt % 2
                if pqt == 0:
                    pe.wait_ge(b.hd[ppar].h, h_loaded[ph])
                    pe.wait_ge(ev.h, ev_mask)
                    pe.wait_ge(ep.h, pn2)
                pe.wait_ge(ev.h, free_pmb[qs])
                pe.matmul(pmb[qs][0:64, :], eh64[:, ph * 64:(ph + 1) * 64],
                          mask_all[:, pqt * 512:(pqt + 1) * 512],
                          start=True, stop=True)
                mb_mm = mm.inc(pe.matmul(pmb[qs][64:65, :], id_sb[0:16, ph:ph + 1],
                                         qstab[:, pqt * 512:(pqt + 1) * 512],
                                         start=True, stop=True))
                dve.wait_ge(mm.h, mb_mm)
                dve.wait_ge(mm.h, qmf[qs])
                dve.tensor_tensor(qm[qs][0:64, :],
                                  qh[ppar][:, pqt * 512:(pqt + 1) * 512],
                                  pmb[qs][0:64, :], ALU.mult)
                dve.tensor_copy(qm[qs][64:65, :], pmb[qs][64:65, :])
                free_pmb[qs] = ev.n
                ev_qm_s[qs] = ev.n

            def emit_normalize(ph, pqt, pctx_mm):
                pqs = pqt % 2
                dve.wait_ge(mm.h, pctx_mm)
                dve.reciprocal(rrow[64:65, :], pctx[pqs][64:65, :])
                ev_rec = ev.n
                pe.wait_ge(ev.h, ev_rec)
                pe.wait_ge(ev.h, free_pmb[pqs])
                r_mm = mm.inc(pe.matmul(pmb[pqs][0:64, :], onesc[64:65, 0:64],
                                        rrow[64:65, :], start=True, stop=True))
                dve.wait_ge(mm.h, r_mm)
                dve.tensor_copy(recipb[:], pmb[pqs][0:64, :])
                free_pmb[pqs] = ev.n
                pfc, pr0 = ph // 2, 64 * (ph % 2)
                dve.scalar_tensor_tensor(
                    ctx_fm[pr0:pr0 + 64, pfc, pqt * 512:(pqt + 1) * 512],
                    pctx[pqs][0:64, :], 1.0, recipb[:], ALU.mult, ALU.mult)
                free_pctx[pqs] = ev.n

            emit_prologue(0)
            for idx, (h, qt) in enumerate(pairs):
                par = h % 2
                qs = qt % 2

                def emit_sc(g):
                    sl_par = g % 2
                    for j, kt in enumerate(range(g * KTG, (g + 1) * KTG)):
                        last = pe.matmul(psc2[sl_par][:, j * 512:(j + 1) * 512],
                                         kh[par][:, kt * 128:(kt + 1) * 128],
                                         qm[qs][:], start=True, stop=True)
                    return mm.inc(last)

                sc_mm = [0] * ngrp
                pe.wait_ge(ev.h, ev_qm_s[qs])
                pe.wait_ge(ea.h, free_psc_ea[0])
                sc_mm[0] = emit_sc(0)
                if pending is not None:
                    emit_normalize(*pending)
                    pending = None
                for g in range(ngrp):
                    sl_par = g % 2
                    if g + 1 < ngrp:
                        pe.wait_ge(ea.h, free_psc_ea[(g + 1) % 2])
                        sc_mm[g + 1] = emit_sc(g + 1)
                    if g == 2 and idx + 1 < len(pairs):
                        emit_prologue(idx + 1)
                    act.wait_ge(mm.h, sc_mm[g])
                    act.wait_ge(mm.h, free_phat[sl_par])
                    act.activation(phat[sl_par][:], psc2[sl_par][:],
                                   AF.Exp, scale=0.125)
                    free_psc_ea[sl_par] = ea.n
                    pe.wait_ge(ea.h, free_psc_ea[sl_par])
                    if g == 0:
                        pe.wait_ge(ev.h, free_pctx[qs])
                    for j, kt in enumerate(range(g * KTG, (g + 1) * KTG)):
                        last = pe.matmul(pctx[qs][:], vh[par][:, kt, :],
                                         phat[sl_par][:, j * 512:(j + 1) * 512],
                                         start=(kt == 0), stop=(kt == 31))
                    free_phat[sl_par] = mm.inc(last)
                qmf[qs] = sc_mm[ngrp - 1]
                pending = (h, qt, free_phat[(ngrp - 1) % 2])
                if qt == 7:
                    h_done[h] = mm.n
                    if h + 2 < NH:
                        load_head(h + 2)
            emit_normalize(*pending)
            pending = None

    with ExitStack() as c3ctx:
        sb3 = lambda n, sh, dt: c3ctx.enter_context(nc.sbuf_tensor(n, sh, dt))
        wo_sb = sb3("c_wo", [128, 8, D], BF16)
        borow = sb3("c_borow", [1, D], F32)
        bo_bc = sb3("c_bobc", [128, D], F32)
        xt = sb3("c_xt3", [128, 2, D], F32)
        ya = sb3("c_ya3", [128, 2, D], F32)
        yb = sb3("c_yb3", [128, D], F32)
        ones1b = sb3("c_ones1b", [1, 128], F32)
        with ExitStack() as p3ctx:
            pp = p3ctx.enter_context(nc.psum_tensor("c3_pp", [128, D], F32))
            pp2 = p3ctx.enter_context(nc.psum_tensor("c3_pp2", [128, D], F32))
            pool.wait_ge(ev.h, ev.n)
            pool.wait_ge(ea.h, ea.n)
            pool.wait_ge(mm.h, mm.n)
            pool.memset(ones1b[:], 1.0)
            pn3 = ep.n
            pool.dma_start(out=wo_sb[:], in_=io["woT1"].rearrange("(kc p) n -> p kc n", p=128)
                           ).then_inc(b.pdma.h, 16)
            b.pdma.n += 16
            sp.wait_ge(mm.h, mm.n)
            sp.wait_ge(ev.h, ev.n)
            sp.wait_ge(ea.h, ea.n)
            sp.dma_start(out=borow[:], in_=io["bo1"].unsqueeze(0)).then_inc(b.cdma.h, 16)
            b.cdma.n += 16
            pe.wait_ge(b.cdma.h, b.cdma.n)
            pe.wait_ge(ep.h, pn3)
            for half in range(2):
                sl = slice(half * 512, (half + 1) * 512)
                last = pe.matmul(pp[:, sl], ones1b[:], borow[:, sl], start=True, stop=True)
            bo_mm = mm.inc(last)
            dve.wait_ge(mm.h, bo_mm)
            dve.tensor_copy(bo_bc[:], pp[:])
            free_pp = ev.n
            free_pp2 = ev.n
            pe.wait_ge(b.pdma.h, b.pdma.n)
            pe.wait_ge(ev.h, ev.n)
            xt_free = [ev.n, ev.n]
            ya_free = [b.out.n, b.out2.n]
            for ti in range(T // 128):
                k = ti % 2
                xs = b.xdma if k == 0 else b.xdma2
                sp.wait_ge(ev.h, xt_free[k])
                sp.dma_start(out=xt[:, k, :], in_=io["x2"][ti * 128:(ti + 1) * 128, :]
                             ).then_inc(xs.h, 16)
                xs.n += 16
                ppu = pp if k == 0 else pp2
                pe.wait_ge(ev.h, free_pp if k == 0 else free_pp2)
                for kc in range(8):
                    for half in range(2):
                        sl = slice(half * 512, (half + 1) * 512)
                        last = pe.matmul(ppu[:, sl], ctx_fm[:, kc, ti * 128:(ti + 1) * 128],
                                         wo_sb[:, kc, sl], start=(kc == 0), stop=(kc == 7))
                o_mm = mm.inc(last)
                dve.wait_ge(mm.h, o_mm)
                dve.wait_ge((b.out if k == 0 else b.out2).h, ya_free[k])
                dve.scalar_tensor_tensor(ya[:, k, :], ppu[:], 1.0, bo_bc[:],
                                         ALU.mult, ALU.add)
                if k == 0:
                    free_pp = ev.n
                else:
                    free_pp2 = ev.n
                act.wait_ge(ev.h, ev.n)
                _ln_stats(b, ya[:, k, :], 8 + k)
                dve.wait_ge(ea.h, ea.n)
                _ln_mr(b, slice(8 + k, 9 + k))
                os_ = b.out if k == 0 else b.out2
                dve.wait_ge(xs.h, xs.n)
                dve.tensor_scalar(yb[:], ya[:, k, :], b.a_mcol[:, 8 + k:9 + k],
                                  b.a_rcol[:, 8 + k:9 + k], ALU.subtract, ALU.mult)
                dve.tensor_tensor(ya[:, k, :], yb[:], xt[:, k, :], ALU.add)
                xt_free[k] = ev.n
                sp.wait_ge(ev.h, ev.n)
                sp.dma_start(out=io["x3"][ti * 128:(ti + 1) * 128, :],
                             in_=ya[:, k, :]).then_inc(os_.h, 16)
                os_.n += 16
                ya_free[k] = os_.n

    sp.wait_ge(b.out.h, b.out.n)
    sp.wait_ge(b.out2.h, b.out2.n)


def hostprep_c(inputs):
    wqkv = np.asarray(inputs["L1_attn_w_qkv"])
    bqkv = np.asarray(inputs["L1_attn_b_qkv"])
    selmat = np.zeros((128, 8, 16), np.float32)
    for m in range(8):
        selmat[0:64, m, 2 * m] = 1.0
        selmat[64:128, m, 2 * m + 1] = 1.0
    eh64 = np.zeros((16, NH * 64), np.float32)
    for h in range(NH):
        eh64[h, h * 64:(h + 1) * 64] = 1.0
    return dict(
        wqT1=np.ascontiguousarray(wqkv[0:D].T),
        wkT1=np.ascontiguousarray(wqkv[D:2 * D].T),
        wvT1=np.ascontiguousarray(wqkv[2 * D:].T),
        bq1=np.ascontiguousarray(bqkv[0:D].reshape(8, 128).T),
        bk1=np.ascontiguousarray(bqkv[D:2 * D].reshape(8, 128).T),
        bv1=np.ascontiguousarray(bqkv[2 * D:]),
        woT1=np.ascontiguousarray(np.asarray(inputs["L1_attn_w_out"]).T),
        bo1=np.asarray(inputs["L1_attn_b_out"]),
        selmat=selmat.astype(BF), eh64=eh64.astype(BF),
        ident=np.eye(128, dtype=BF))


C_IO = [("wqT1", [D, D], F32), ("wkT1", [D, D], F32), ("wvT1", [D, D], F32),
        ("bq1", [128, 8], F32), ("bk1", [128, 8], F32), ("bv1", [D], F32),
        ("woT1", [D, D], F32), ("bo1", [D], F32),
        ("selmat", [128, 8, 16], BF16),
        ("eh64", [16, NH * 64], BF16), ("ident", [128, 128], BF16)]


def build_c_program(n_cores=8):
    b = B()
    io = {n: b.dram(n, s, d, "ExternalInput") for n, s, d in C_IO}
    io["x2"] = b.dram("x2", [T, D], F32, "ExternalInput")
    io["x3"] = b.dram("x3", [T, D], F32, "ExternalOutput")
    io["qT_h"] = b.dram("qT_h", [D, T], BF16, "Internal")
    io["kT_h"] = b.dram("kT_h", [D, T], BF16, "Internal")
    io["v_h"] = b.dram("v_h", [T, D], BF16, "Internal")
    io["cc_in"] = b.dram("cc_in", [1, 1], F32, "Internal")
    io["cc_out"] = b.dram("cc_out", [1, 1], F32, "Internal")
    with b.ctx:
        blk = b.ctx.enter_context(b.nc.Block())

        @blk.gpsimd
        def _(g):
            build_stage_c(b, io, n_cores=n_cores)
    return b.nc




def build_stage_d(b, io):
    nc = b.nc
    pe, act, dve, pool, sp = b.pe, b.act, b.dve, b.pool, b.sp
    dma, mm, ea, ev, ep = b.dma, b.mm, b.ea, b.ev, b.ep

    swT = b.sb("d_swT", [128, 8, D], BF16)
    id_sb = b.sb("d_id", [128, 128], BF16)
    xt = b.sb("d_xt", [128, 2, D], F32)
    xn = b.sb("d_xn", [128, D], BF16)
    xnT = b.sb("d_xnT", [128, 8, 128], BF16)
    ya = b.sb("d_ya", [128, 2, D], F32)
    yb = b.sb("d_yb", [128, D], F32)
    b.a_s1 = b.sb("d_s1", [128, 16], F32)
    b.a_ssq = b.sb("d_ssq", [128, 16], F32)
    b.a_mcol = b.sb("d_mcol", [128, 16], F32)
    b.a_vcol = b.sb("d_vcol", [128, 16], F32)
    b.a_rcol = b.sb("d_rcol", [128, 16], F32)
    b.a_sqscr = b.sb("d_sqscr", [128, D], BF16)
    b.a_ecol = b.sb("d_ecol", [128, 1], F32)

    with ExitStack() as pctx:
        pp = pctx.enter_context(nc.psum_tensor("d_pp", [128, D], F32))
        pp2 = pctx.enter_context(nc.psum_tensor("d_pp2", [128, D], F32))

        pool.wait_ge(ev.h, ev.n)
        pool.wait_ge(ea.h, ea.n)
        pool.wait_ge(mm.h, mm.n)
        pool.memset(b.a_ecol[:], EPS)
        pool.dma_start(out=swT[:], in_=io["wT_d"].rearrange("(kc p) n -> p kc n", p=128)
                       ).then_inc(b.pdma.h, 16)
        b.pdma.n += 16
        sp.wait_ge(mm.h, mm.n)
        sp.wait_ge(ev.h, ev.n)
        sp.wait_ge(ea.h, ea.n)
        sp.dma_start(out=id_sb[:], in_=io["ident"]).then_inc(b.cdma.h, 16)
        b.cdma.n += 16
        act.wait_ge(b.pdma.h, b.pdma.n)
        act.activation(swT[:], swT[:], AF.Sign)
        sw_ea = ea.n
        pe.wait_ge(ea.h, sw_ea)
        pe.wait_ge(b.cdma.h, b.cdma.n)

        free_pp = ev.n
        free_pp2 = ev.n
        xt_free = [ev.n, ev.n]
        ya_free = [b.out.n, b.out2.n]
        for ti in range(T // 128):
            k = ti % 2
            xs = b.xdma if k == 0 else b.xdma2
            os_ = b.out if k == 0 else b.out2
            sp.wait_ge(ev.h, xt_free[k])
            sp.dma_start(out=xt[:, k, :], in_=io["x3"][ti * 128:(ti + 1) * 128, :]
                         ).then_inc(xs.h, 16)
            xs.n += 16
            dve.wait_ge(xs.h, xs.n)
            act.wait_ge(xs.h, xs.n)
            _ln_stats(b, xt[:, k, :], k)
            dve.wait_ge(ea.h, ea.n)
            _ln_mr(b, slice(k, k + 1))
            dve.wait_ge(mm.h, mm.n)
            dve.tensor_scalar(xn[:], xt[:, k, :], b.a_mcol[:, k:k + 1],
                              b.a_rcol[:, k:k + 1], ALU.subtract, ALU.mult)
            ev_xn = ev.n
            ptv = [pp[:, 0:64].bitcast(BF16), pp2[:, 0:64].bitcast(BF16)]
            free_pt = [free_pp, free_pp2]
            pe.wait_ge(ev.h, ev_xn)
            for fc in range(8):
                kk = fc % 2
                pe.wait_ge(ev.h, free_pt[kk])
                t_mm = mm.inc(pe.transpose(ptv[kk], xn[:, fc * 128:(fc + 1) * 128], id_sb[:]))
                dve.wait_ge(mm.h, t_mm)
                dve.tensor_copy(xnT[:, fc, :], ptv[kk])
                free_pt[kk] = ev.n
            free_pp, free_pp2 = free_pt
            ppu = pp if k == 0 else pp2
            pe.wait_ge(ev.h, ev.n)
            for kc in range(8):
                for half in range(2):
                    sl = slice(half * 512, (half + 1) * 512)
                    last = pe.matmul(ppu[:, sl], xnT[:, kc, :], swT[:, kc, sl],
                                     start=(kc == 0), stop=(kc == 7))
            y_mm = mm.inc(last)
            dve.wait_ge(mm.h, y_mm)
            dve.wait_ge(os_.h, ya_free[k])
            dve.tensor_copy(ya[:, k, :], ppu[:])
            if k == 0:
                free_pp = ev.n
            else:
                free_pp2 = ev.n
            act.wait_ge(ev.h, ev.n)
            _ln_stats(b, ya[:, k, :], 8 + k)
            dve.wait_ge(ea.h, ea.n)
            _ln_mr(b, slice(8 + k, 9 + k))
            dve.tensor_scalar(yb[:], ya[:, k, :], b.a_mcol[:, 8 + k:9 + k],
                              b.a_rcol[:, 8 + k:9 + k], ALU.subtract, ALU.mult)
            dve.tensor_tensor(ya[:, k, :], yb[:], xt[:, k, :], ALU.add)
            xt_free[k] = ev.n
            sp.wait_ge(ev.h, ev.n)
            sp.dma_start(out=io["out"][ti * 128:(ti + 1) * 128, :],
                         in_=ya[:, k, :]).then_inc(os_.h, 16)
            os_.n += 16
            ya_free[k] = os_.n

    sp.wait_ge(b.out.h, b.out.n)
    sp.wait_ge(b.out2.h, b.out2.n)


def hostprep_d(inputs):
    return dict(wT_d=np.ascontiguousarray(np.asarray(inputs["L1_ffn_weight"]).T),
                ident=np.eye(128, dtype=BF))


D_IO = [("wT_d", [D, D], F32), ("ident", [128, 128], BF16)]


def build_d_program():
    b = B()
    io = {n: b.dram(n, s, d, "ExternalInput") for n, s, d in D_IO}
    io["x3"] = b.dram("x3", [T, D], F32, "ExternalInput")
    io["out"] = b.dram("out", [T, D], F32, "ExternalOutput")
    with b.ctx:
        blk = b.ctx.enter_context(b.nc.Block())

        @blk.gpsimd
        def _(g):
            build_stage_d(b, io)
    return b.nc


_PROG_CACHE = {}


def _get_prog(name, builder):
    if name not in _PROG_CACHE:
        _PROG_CACHE[name] = builder()
    return _PROG_CACHE[name]


def kernel(**inputs):
    return kernel_fused(**inputs)


def kernel_unfused(**inputs):
    x = np.asarray(inputs["L0_attn_x"])
    ncores = 8
    hpa = hostprep_a(inputs)
    hpb = hostprep_b(inputs)
    hpc = hostprep_c(inputs)
    hpd = hostprep_d(inputs)

    nc_a = _get_prog("a", build_a_program)
    in_maps = [dict(hpa, x0=np.ascontiguousarray(x[c])) for c in range(ncores)]
    res = run_bass_kernel_spmd(nc_a, in_maps, core_ids=list(range(ncores)))
    x1 = [np.asarray(res.results[c]["x1"]) for c in range(ncores)]

    nc_b = _get_prog("b", build_b_program)
    in_maps = [dict(hpb, x1=x1[c]) for c in range(ncores)]
    res = run_bass_kernel_spmd(nc_b, in_maps, core_ids=list(range(ncores)))
    x2 = [np.asarray(res.results[c]["x2"]) for c in range(ncores)]

    nc_c = _get_prog("c", build_c_program)
    in_maps = [dict(hpc, x2=x2[c]) for c in range(ncores)]
    res = run_bass_kernel_spmd(nc_c, in_maps, core_ids=list(range(ncores)))
    x3 = [np.asarray(res.results[c]["x3"]) for c in range(ncores)]

    nc_d = _get_prog("d", build_d_program)
    in_maps = [dict(hpd, x3=x3[c]) for c in range(ncores)]
    res = run_bass_kernel_spmd(nc_d, in_maps, core_ids=list(range(ncores)))
    out = np.stack([np.asarray(res.results[c]["out"]) for c in range(ncores)])
    return out



def _stage_barrier(b):
    sems = [b.mm, b.ea, b.ev, b.ep, b.out, b.out2, b.cdma, b.pdma,
            b.xdma, b.xdma2, b.wd[0], b.wd[1], b.hd[0], b.hd[1]]
    engs = [b.pe, b.nc.scalar, b.nc.vector, b.nc.gpsimd, b.sp]
    for e in engs:
        for sm in sems:
            if sm.n:
                e.wait_ge(sm.h, sm.n)


def build_fused_program(n_cores=8):
    b = B()
    io = {}
    for n, sh, dt in A_IO + B_IO + C_IO + D_IO:
        if n not in io:
            io[n] = b.dram(n, sh, dt, "ExternalInput")
    io["out"] = b.dram("out", [T, D], F32, "ExternalOutput")
    for n in ["x1", "x2", "x3"]:
        io[n] = b.dram(n, [T, D], F32, "Internal")
    io["qT_h"] = b.dram("qT_h", [D, T], BF16, "Internal")
    io["kT_h"] = b.dram("kT_h", [D, T], BF16, "Internal")
    io["v_h"] = b.dram("v_h", [T, D], BF16, "Internal")
    io["cc_in"] = b.dram("cc_in", [1, 1], F32, "Internal")
    io["cc_out"] = b.dram("cc_out", [1, 1], F32, "Internal")
    with b.ctx:
        blk = b.ctx.enter_context(b.nc.Block())

        @blk.gpsimd
        def _(g):
            for fn in [build_stage_a, build_stage_b,
                       lambda bb, ii: build_stage_c(bb, ii, n_cores=n_cores),
                       build_stage_d]:
                with ExitStack() as sc:
                    b.cur = sc
                    fn(b, io)
                b.cur = None
                _stage_barrier(b)
    return b.nc


def kernel_fused(**inputs):
    x = np.asarray(inputs["L0_attn_x"])
    ncores = 8
    hp = {}
    hp.update(hostprep_a(inputs))
    hp.update(hostprep_b(inputs))
    hp.update(hostprep_c(inputs))
    hp.update(hostprep_d(inputs))
    nc = _get_prog("fused", build_fused_program)
    in_maps = [dict(hp, x0=np.ascontiguousarray(x[c])) for c in range(ncores)]
    res = run_bass_kernel_spmd(nc, in_maps, core_ids=list(range(ncores)))
    return np.stack([np.asarray(res.results[c]["out"]) for c in range(ncores)])


# version 07
# speedup vs baseline: 1.0253x; 1.0095x over previous
import math
from contextlib import ExitStack

import numpy as np
import ml_dtypes

import concourse.bass as bass
import concourse.mybir as mybir
from concourse.bass_utils import run_bass_kernel_spmd

F32 = mybir.dt.float32
BF16 = mybir.dt.bfloat16
AF = mybir.ActivationFunctionType
ALU = mybir.AluOpType
AX = mybir.AxisListType
BF = ml_dtypes.bfloat16

A_MAXPH = 6
A_P5MODE = 4
A_NCH = None
A_SKIP_P2 = False
A_SKIP_P4 = False
A_SKIP_P1 = False
T = 4096
D = 1024
NH = 16
HD = 64
NE = 16
CH = 1024
NCH = T // CH
NT = CH // 128
EPS = 1e-5
BIG = 1.0e4


class Sem:
    def __init__(self, nc, name):
        self.h = nc.alloc_semaphore(name)
        self.n = 0

    def inc(self, inst, k=1):
        inst.then_inc(self.h, k)
        self.n += k
        return self.n


_WRAP = {
    "activation", "sign", "square", "sqrt", "mul", "copy", "add", "memzero",
    "memset", "tensor_copy", "tensor_tensor", "tensor_add", "tensor_sub",
    "tensor_mul", "tensor_max", "tensor_scalar", "tensor_single_scalar",
    "tensor_scalar_add", "tensor_scalar_sub", "tensor_scalar_mul",
    "tensor_scalar_max", "tensor_scalar_min", "tensor_relu", "tensor_reduce",
    "reduce_max", "reduce_sum", "scalar_tensor_tensor", "tensor_tensor_scan",
    "select", "copy_predicated", "transpose", "reciprocal", "iota",
    "affine_select",
}


class SEng:

    def __init__(self, eng, sem):
        self._e = eng
        self._s = sem

    def __getattr__(self, name):
        attr = getattr(self._e, name)
        if name in _WRAP:
            def f(*a, **k):
                self._e.wait_ge(self._s.h, self._s.n)
                inst = attr(*a, **k)
                self._s.inc(inst)
                return inst
            return f
        return attr


class B:

    def __init__(self):
        nc = bass.Bass("TRN2", target_bir_lowering=False)
        self.nc = nc
        self.ctx = ExitStack()
        self.cur = None
        self.pe = nc.tensor
        self.sp = nc.sync
        self.dma = Sem(nc, "dma")
        self.mm = Sem(nc, "mm")
        self.ea = Sem(nc, "ea")
        self.ev = Sem(nc, "ev")
        self.ep = Sem(nc, "ep")
        self.act = SEng(nc.scalar, self.ea)
        self.dve = SEng(nc.vector, self.ev)
        self.pool = SEng(nc.gpsimd, self.ep)
        self.out = Sem(nc, "outs")
        self.pdma = Sem(nc, "pdma")
        self.cdma = Sem(nc, "cdma")
        self.xdma = Sem(nc, "xdma")
        self.xdma2 = Sem(nc, "xdma2")
        self.wd = [Sem(nc, "wd0"), Sem(nc, "wd1")]
        self.hd = [Sem(nc, "hd0"), Sem(nc, "hd1")]
        self.out2 = Sem(nc, "outs2")

    def dram(self, name, shape, dtype, kind):
        return self.nc.dram_tensor(name, list(shape), dtype, kind=kind).ap()

    def sb(self, name, shape, dtype):
        return (self.cur or self.ctx).enter_context(
            self.nc.sbuf_tensor(name, list(shape), dtype))

    def ps(self, name, shape, dtype=F32):
        return (self.cur or self.ctx).enter_context(
            self.nc.psum_tensor(name, list(shape), dtype))


def _ln_stats(b, src, col):
    b.dve.tensor_reduce(b.a_s1[:, col:col + 1], src, axis=AX.X, op=ALU.add)
    b.act.activation(b.a_sqscr[:], src, AF.Square, accum_out=b.a_ssq[:, col:col + 1])


def _ln_mr(b, cols):
    dve, act, ea, ev = b.dve, b.act, b.ea, b.ev
    s = cols
    dve.tensor_scalar_mul(b.a_mcol[:, s], b.a_s1[:, s], 1.0 / D)
    dve.tensor_scalar_mul(b.a_vcol[:, s], b.a_ssq[:, s], 1.0 / D)
    dve.tensor_tensor(b.a_rcol[:, s], b.a_mcol[:, s], b.a_mcol[:, s], ALU.mult)
    dve.tensor_tensor(b.a_vcol[:, s], b.a_vcol[:, s], b.a_rcol[:, s], ALU.subtract)
    act.wait_ge(ev.h, ev.n)
    act.activation(b.a_rcol[:, s], b.a_vcol[:, s], AF.Sqrt, bias=b.a_ecol[:])
    dve.wait_ge(ea.h, ea.n)
    dve.reciprocal(b.a_rcol[:, s], b.a_rcol[:, s])


def build_stage_a(b, io):
    CH = 512
    NCH = A_NCH if A_NCH else T // CH
    NT = CH // 128
    nc = b.nc
    pe, act, dve, pool, sp = b.pe, b.act, b.dve, b.pool, b.sp
    dma, mm, ea, ev, ep = b.dma, b.mm, b.ea, b.ev, b.ep

    xa = b.sb("a_x", [128, NT, D], F32)
    xn = b.sb("a_xn", [128, NT, D], BF16)
    xnT = b.sb("a_xnT", [128, 8, CH], BF16)
    qf = b.sb("a_qf", [128, 8, CH], BF16)
    kf = b.sb("a_kf", [128, 8, 128 + CH], BF16)
    vt = b.sb("a_vt", [128, 1 + NT, D], BF16)
    msgT = b.sb("a_msgT", [128, 8, CH], BF16)
    tau = b.sb("a_tau", [128, 2048], BF16)
    wq_sb = b.sb("a_wq", [128, 8, D], BF16)
    wk_sb = b.sb("a_wk", [128, 8, D], BF16)
    wv_sb = b.sb("a_wv", [128, 8, D], BF16)
    wo_sb = b.sb("a_wo", [128, 8, D], BF16)
    mask_sb = b.sb("a_mask", [128, 2048], BF16)
    id_sb = b.sb("a_id", [128, 128], BF16)
    grow = b.sb("a_grow", [1, D], F32)
    brow = b.sb("a_brow", [1, D], F32)
    ones1 = b.sb("a_ones1", [1, 128], F32)
    gbc = b.sb("a_gbc", [128, D], F32)
    bbc = b.sb("a_bbc", [128, D], F32)
    b.a_s1 = b.sb("a_s1", [128, 16], F32)
    b.a_ssq = b.sb("a_ssq", [128, 16], F32)
    b.a_mcol = b.sb("a_mcol", [128, 16], F32)
    b.a_vcol = b.sb("a_vcol", [128, 16], F32)
    b.a_rcol = b.sb("a_rcol", [128, 16], F32)
    b.a_sqscr = b.sb("a_sqscr", [128, D], BF16)
    b.a_ecol = b.sb("a_ecol", [128, 1], F32)
    ya = b.sb("a_ya", [128, D], F32)
    yb = b.sb("a_yb", [128, D], F32)

    pp = b.ps("a_pp", [128, D])
    pp2 = b.ps("a_pp2", [128, D])
    psc = b.ps("a_psc", [128, 2048])

    pool.memset(ones1[:], 1.0)
    pool.memset(b.a_ecol[:], EPS)
    pool.memset(kf[:], 0.0)
    pool.memset(vt[:], 0.0)
    pn = ep.n
    for dst, src in [(id_sb, io["ident"]), (mask_sb, io["maskAB"])]:
        sp.dma_start(out=dst[:], in_=src).then_inc(b.cdma.h, 16)
        b.cdma.n += 16
    sp.dma_start(out=grow[:], in_=io["gamma"].unsqueeze(0)).then_inc(b.cdma.h, 16)
    sp.dma_start(out=brow[:], in_=io["beta"].unsqueeze(0)).then_inc(b.cdma.h, 16)
    b.cdma.n += 32
    for wsb, wdr in [(wq_sb, io["wqT"]), (wk_sb, io["wkT"]), (wv_sb, io["wvT"]),
                     (wo_sb, io["woT"])]:
        pool.dma_start(out=wsb[:], in_=wdr.rearrange("(kc p) n -> p kc n", p=128)
                       ).then_inc(b.pdma.h, 16)
        b.pdma.n += 16
    w_loaded = b.pdma.n

    pe.wait_ge(b.cdma.h, b.cdma.n)
    pe.wait_ge(b.pdma.h, w_loaded)
    pe.wait_ge(ep.h, pn)
    for half in range(2):
        sl = slice(half * 512, (half + 1) * 512)
        pe.matmul(pp[:, sl], ones1[:], grow[:, sl], start=True, stop=True)
        last = pe.matmul(pp2[:, sl], ones1[:], brow[:, sl], start=True, stop=True)
    gb_mm = mm.inc(last)
    dve.wait_ge(mm.h, gb_mm)
    dve.wait_ge(ep.h, pn)
    dve.tensor_copy(gbc[:], pp[:])
    dve.tensor_copy(bbc[:], pp2[:])
    gb_ev = ev.n

    dve.wait_ge(b.cdma.h, b.cdma.n)
    act.wait_ge(b.cdma.h, b.cdma.n)
    free_pp = gb_ev
    free_pp2 = gb_ev
    free_psc_ea = 0
    free_tau_mm = 0

    for c in range(NCH):
        sp.wait_ge(b.out.h, b.out.n)
        sp.dma_start(out=xa[:], in_=io["x0"][c * CH:(c + 1) * CH, :]
                     .rearrange("(i p) d -> p i d", p=128)).then_inc(b.xdma.h, 16)
        b.xdma.n += 16
        x_loaded = b.xdma.n

        if A_MAXPH < 1:
            sp.wait_ge(b.xdma.h, b.xdma.n)
            sp.dma_start(out=io["x1"][c * CH:(c + 1) * CH, :]
                         .rearrange("(i p) d -> p i d", p=128), in_=xa[:]).then_inc(b.out.h, 16)
            b.out.n += 16
            continue
        dve.wait_ge(b.xdma.h, x_loaded)
        act.wait_ge(b.xdma.h, x_loaded)
        if A_SKIP_P1:
            pool.memset(xn[:], 0.01)
            dve.wait_ge(ep.h, ep.n)
        el_dummy = 0
        if not A_SKIP_P1:
            for i in range(NT):
                _ln_stats(b, xa[:, i, :], i)
            dve.wait_ge(ea.h, ea.n)
            _ln_mr(b, slice(0, NT))
            for i in range(NT):
                dve.tensor_scalar(xn[:, i, :], xa[:, i, :], b.a_mcol[:, i:i + 1],
                                  b.a_rcol[:, i:i + 1], ALU.subtract, ALU.mult)
        ev_xn = ev.n

        if A_MAXPH < 2:
            sp.wait_ge(ev.h, ev.n)
            sp.dma_start(out=io["x1"][c * CH:(c + 1) * CH, :]
                         .rearrange("(i p) d -> p i d", p=128), in_=xa[:]).then_inc(b.out.h, 16)
            b.out.n += 16
            continue
        if A_SKIP_P2:
            pool.memset(xnT[:], 0.01)
            dve.wait_ge(ep.h, ep.n)
            dve.tensor_copy(b.a_s1[:, 15:16], b.a_s1[:, 15:16])
        else:
            ptv = [pp[:, 0:64].bitcast(BF16), pp2[:, 0:64].bitcast(BF16)]
            free_pt = [free_pp, free_pp2]
            pe.wait_ge(ev.h, ev_xn)
            for u in range(NT * 8):
                i, fc = divmod(u, 8)
                k = u % 2
                pe.wait_ge(ev.h, free_pt[k])
                t_mm = mm.inc(pe.transpose(ptv[k], xn[:, i, fc * 128:(fc + 1) * 128], id_sb[:]))
                dve.wait_ge(mm.h, t_mm)
                dve.tensor_copy(xnT[:, fc, i * 128:(i + 1) * 128], ptv[k])
                free_pt[k] = ev.n
            free_pp, free_pp2 = free_pt

        if c > 0:
            dve.tensor_copy(kf[:, :, 0:128], kf[:, :, CH:CH + 128])
            dve.tensor_copy(vt[:, 0, :], vt[:, NT, :])
        else:
            dve.wait_ge(ep.h, pn)
        ev_carry = ev.n
        pe.wait_ge(ev.h, ev_carry)

        if A_MAXPH < 4:
            sp.wait_ge(ev.h, ev.n)
            sp.dma_start(out=io["x1"][c * CH:(c + 1) * CH, :]
                         .rearrange("(i p) d -> p i d", p=128), in_=xa[:]).then_inc(b.out.h, 16)
            b.out.n += 16
            continue
        if A_SKIP_P4:
            pool.memset(qf[:], 0.01)
            pool.memset(kf[:], 0.01)
            pool.memset(vt[:], 0.01)
            pe.wait_ge(ep.h, ep.n)
        pbuf = [pp, pp2]
        pfree = [free_pp, free_pp2]
        u = 0
        for wsb, kind in ([] if A_SKIP_P4 else [(wq_sb, "q"), (wk_sb, "k"), (wv_sb, "v")]):
            nm = NT if kind == "v" else 8
            for m in range(nm):
                k = u % 2
                u += 1
                ppu = pbuf[k]
                pe.wait_ge(ev.h, pfree[k])
                if kind == "v":
                    for kc in range(8):
                        for half in range(2):
                            sl = slice(half * 512, (half + 1) * 512)
                            last = pe.matmul(ppu[:, sl], xnT[:, kc, m * 128:(m + 1) * 128],
                                             wv_sb[:, kc, sl], start=(kc == 0), stop=(kc == 7))
                else:
                    for kc in range(8):
                        last = pe.matmul(ppu[:, 0:CH], wsb[:, kc, m * 128:(m + 1) * 128],
                                         xnT[:, kc, :], start=(kc == 0), stop=(kc == 7))
                m_mm = mm.inc(last)
                dve.wait_ge(mm.h, m_mm)
                if kind == "q":
                    dve.tensor_copy(qf[:, m, :], ppu[:, 0:CH])
                elif kind == "k":
                    dve.tensor_copy(kf[:, m, 128:128 + CH], ppu[:, 0:CH])
                else:
                    dve.tensor_copy(vt[:, 1 + m, :], ppu[:])
                pfree[k] = ev.n
        free_pp, free_pp2 = pfree
        pe.wait_ge(ev.h, ev.n)

        if A_MAXPH < 5:
            sp.wait_ge(ev.h, ev.n)
            sp.dma_start(out=io["x1"][c * CH:(c + 1) * CH, :]
                         .rearrange("(i p) d -> p i d", p=128), in_=xa[:]).then_inc(b.out.h, 16)
            b.out.n += 16
            continue
        for qt in range(NT):
            for half in range(2):
                if A_P5MODE == 0:
                    pe.wait_ge(ev.h, ev.n)
                else:
                    pe.wait_ge(ea.h, free_psc_ea)
                for hh in [0, 2, 4, 6, 1, 3, 5, 7]:
                    h = half * 8 + hh
                    fc, r0 = h // 2, 64 * (h % 2)
                    for piece in range(2):
                        kb = qt + piece
                        c0 = hh * 256 + piece * 128
                        last = pe.matmul(psc[:, c0:c0 + 128],
                                         kf[r0:r0 + 64, fc, kb * 128:(kb + 1) * 128],
                                         qf[r0:r0 + 64, fc, qt * 128:(qt + 1) * 128],
                                         start=True, stop=True)
                sc_mm = mm.inc(last)
                act.wait_ge(mm.h, sc_mm)
                act.wait_ge(mm.h, free_tau_mm)
                if A_P5MODE == 0:
                    dve.tensor_copy(tau[:], psc[:])
                    free_psc_ea = 0
                    continue
                act.activation(tau[:], psc[:], AF.Sigmoid, scale=0.125)
                free_psc_ea = ea.n
                if A_P5MODE < 2:
                    continue
                dve.wait_ge(ea.h, free_psc_ea)
                dve.tensor_tensor(tau[:], tau[:], mask_sb[:], ALU.mult)
                ev_tau = ev.n
                if A_P5MODE < 3:
                    free_tau_mm = mm.n
                    continue
                pe.wait_ge(ev.h, ev_tau)
                pe.wait_ge(ev.h, free_pp)
                for hh in range(8):
                    h = half * 8 + hh
                    for piece in range(2):
                        kb = qt + piece
                        last = pe.matmul(pp[0:64, hh * 128:(hh + 1) * 128],
                                         vt[:, kb, h * 64:(h + 1) * 64],
                                         tau[:, hh * 256 + piece * 128:hh * 256 + piece * 128 + 128],
                                         start=(piece == 0), stop=(piece == 1))
                free_tau_mm = mm.inc(last)
                if A_P5MODE < 4:
                    continue
                dve.wait_ge(mm.h, free_tau_mm)
                for hh in range(8):
                    h = half * 8 + hh
                    fc, r0 = h // 2, 64 * (h % 2)
                    dve.tensor_copy(msgT[r0:r0 + 64, fc, qt * 128:(qt + 1) * 128],
                                    pp[0:64, hh * 128:(hh + 1) * 128])
                free_pp = ev.n

        if A_MAXPH < 6:
            sp.wait_ge(ev.h, ev.n)
            sp.dma_start(out=io["x1"][c * CH:(c + 1) * CH, :]
                         .rearrange("(i p) d -> p i d", p=128), in_=xa[:]).then_inc(b.out.h, 16)
            b.out.n += 16
            continue
        ev_msg = ev.n
        pe.wait_ge(ev.h, ev_msg)
        for i in range(NT):
            pe.wait_ge(ev.h, free_pp2)
            for kc in range(8):
                for half in range(2):
                    sl = slice(half * 512, (half + 1) * 512)
                    last = pe.matmul(pp2[:, sl], msgT[:, kc, i * 128:(i + 1) * 128],
                                     wo_sb[:, kc, sl], start=(kc == 0), stop=(kc == 7))
            o_mm = mm.inc(last)
            dve.wait_ge(mm.h, o_mm)
            dve.scalar_tensor_tensor(ya[:], pp2[:], 1.0, xn[:, i, :], ALU.mult, ALU.add)
            free_pp2 = ev.n
            act.wait_ge(ev.h, free_pp2)
            _ln_stats(b, ya[:], 8)
            dve.wait_ge(ea.h, ea.n)
            _ln_mr(b, slice(8, 9))
            dve.tensor_scalar(yb[:], ya[:], b.a_mcol[:, 8:9], b.a_rcol[:, 8:9],
                              ALU.subtract, ALU.mult)
            dve.scalar_tensor_tensor(yb[:], yb[:], 1.0, gbc[:], ALU.mult, ALU.mult)
            dve.tensor_tensor(ya[:], yb[:], bbc[:], ALU.add)
            ev_ao = ev.n
            act.wait_ge(ev.h, ev_ao)
            _ln_stats(b, ya[:], 9)
            dve.wait_ge(ea.h, ea.n)
            _ln_mr(b, slice(9, 10))
            dve.tensor_scalar(yb[:], ya[:], b.a_mcol[:, 9:10], b.a_rcol[:, 9:10],
                              ALU.subtract, ALU.mult)
            dve.tensor_tensor(xa[:, i, :], xa[:, i, :], yb[:], ALU.add)

        sp.wait_ge(ev.h, ev.n)
        sp.dma_start(out=io["x1"][c * CH:(c + 1) * CH, :]
                     .rearrange("(i p) d -> p i d", p=128), in_=xa[:]).then_inc(b.out.h, 16)
        b.out.n += 16

    sp.wait_ge(b.out.h, b.out.n)


def hostprep_a(inputs):
    wq = np.ascontiguousarray(np.asarray(inputs["L0_attn_wq"]).T)
    wkv = np.asarray(inputs["L0_attn_wkv"]).reshape(NH, 2 * HD, D)
    wkT = np.ascontiguousarray(wkv[:, :HD].reshape(D, D).T)
    wvT = np.ascontiguousarray(wkv[:, HD:].reshape(D, D).T)
    woT = np.ascontiguousarray(np.asarray(inputs["L0_attn_wo"]).T)
    ki = np.arange(128)[:, None]
    qi = np.arange(128)[None, :]
    mA = ((ki - qi) >= 112).astype(BF)
    mB = (((ki - qi) >= -16) & ((ki - qi) <= -1)).astype(BF)
    maskAB = np.ascontiguousarray(np.concatenate([np.concatenate([mA, mB], 1)] * 8, 1))
    return dict(wqT=wq, wkT=wkT, wvT=wvT, woT=woT,
                gamma=np.asarray(inputs["L0_attn_gamma"]),
                beta=np.asarray(inputs["L0_attn_beta"]),
                maskAB=maskAB, ident=np.eye(128, dtype=BF))


A_IO = [("x0", [T, D], F32), ("wqT", [D, D], F32), ("wkT", [D, D], F32),
        ("wvT", [D, D], F32), ("woT", [D, D], F32), ("gamma", [D], F32),
        ("beta", [D], F32), ("maskAB", [128, 2048], BF16), ("ident", [128, 128], BF16)]


def build_a_program():
    b = B()
    io = {n: b.dram(n, s, d, "ExternalInput") for n, s, d in A_IO}
    io["x1"] = b.dram("x1", [T, D], F32, "ExternalOutput")
    with b.ctx:
        blk = b.ctx.enter_context(b.nc.Block())

        @blk.gpsimd
        def _(g):
            build_stage_a(b, io)
    return b.nc




def _free_as_ev(b, fr):
    sm, cnt = fr
    if sm is b.ev:
        return cnt
    b.dve.wait_ge(sm.h, cnt)
    return b.ev.n


def build_stage_b(b, io):
    CH = 1024
    NCH = T // CH
    NT = CH // 128
    nc = b.nc
    pe, act, dve, pool, sp = b.pe, b.act, b.dve, b.pool, b.sp
    dma, mm, ea, ev, ep = b.dma, b.mm, b.ea, b.ev, b.ep

    xt = b.sb("b_xt", [128, 2, D], F32)
    xn = b.sb("b_xn", [128, NT, D], BF16)
    xnT = b.sb("b_xnT", [128, 8, CH], BF16)
    acc = b.sb("b_acc", [128, NT, D], F32)
    h_sb = b.sb("b_h", [128, 8, CH], BF16)
    h1s = b.sb("b_h1s", [128, CH], BF16)
    wsl = [[b.sb(f"b_w{m}_{p}", [128, 8, D], BF16) for m in range(3)] for p in range(2)]
    gw_sb = b.sb("b_gw", [128, 8, 16], BF16)
    logits = b.sb("b_logits", [128, NT, 16], F32)
    g_all = b.sb("b_g", [128, NT, 16], F32)
    bwork = b.sb("b_bwork", [128, 16], F32)
    selm = b.sb("b_selm", [128, 16], F32)
    t1 = b.sb("b_t1", [128, 16], F32)
    t2 = b.sb("b_t2", [128, 16], F32)
    lmrow = b.sb("b_lm", [128, 16], F32)
    den = b.sb("b_den", [128, 1], F32)
    ebias_bc = b.sb("b_ebias", [128, 16], F32)
    iota_bc = b.sb("b_iota", [128, 16], F32)
    iotamB_bc = b.sb("b_iotamB", [128, 16], F32)
    gones = b.sb("b_gones", [128, 1], F32)
    ebrow = b.sb("b_ebrow", [1, 16], F32)
    ones1 = b.sb("b_ones1", [1, 128], F32)
    id_sb = b.sb("b_id", [128, 128], BF16)
    b.a_s1 = b.sb("b_s1", [128, 16], F32)
    b.a_ssq = b.sb("b_ssq", [128, 16], F32)
    b.a_mcol = b.sb("b_mcol", [128, 16], F32)
    b.a_vcol = b.sb("b_vcol", [128, 16], F32)
    b.a_rcol = b.sb("b_rcol", [128, 16], F32)
    b.a_sqscr = b.sb("b_sqscr", [128, D], BF16)
    b.a_ecol = b.sb("b_ecol", [128, 1], F32)
    ya = b.sb("b_ya", [128, 2, D], F32)

    pp = b.ps("b_pp", [128, D])
    pp2 = b.ps("b_pp2", [128, D])
    pp3 = b.ps("b_pp3", [128, D])
    pr = b.ps("b_pr", [128, 16])
    pr2 = b.ps("b_pr2", [128, 16])

    pool.memset(ones1[:], 1.0)
    pool.memset(b.a_ecol[:], EPS)
    pool.memset(gones[:], 1.0)
    pn = ep.n
    for dst, src in [(id_sb[:], io["ident"]), (iota_bc[:], io["iota16"]),
                     (iotamB_bc[:], io["iotamB16"]),
                     (ebrow[:], io["expert_bias"].unsqueeze(0))]:
        sp.dma_start(out=dst, in_=src).then_inc(b.cdma.h, 16)
        b.cdma.n += 16
    pool.dma_start(out=gw_sb[:], in_=io["gate_wT"].rearrange("(kc p) n -> p kc n", p=128)
                   ).then_inc(b.pdma.h, 16)
    b.pdma.n += 16
    gw_loaded = b.pdma.n
    pe.wait_ge(b.cdma.h, b.cdma.n)
    pe.wait_ge(ep.h, pn)
    last = pe.matmul(pr[:], ones1[:], ebrow[:], start=True, stop=True)
    eb_mm = mm.inc(last)
    dve.wait_ge(mm.h, eb_mm)
    dve.tensor_copy(ebias_bc[:], pr[:])
    dve.wait_ge(b.cdma.h, b.cdma.n)
    ev_const = ev.n

    worder = [("s", None)] + [("e", e) for e in range(NE)]
    loaded = {}
    compute_done = {}

    def issue_weight_load(idx):
        kind, e = worder[idx]
        par = idx % 2
        ws = b.wd[par]
        if idx >= 2:
            pool.wait_ge(mm.h, compute_done[idx - 2])
        for j, nm in enumerate(["w1", "w3", "w2"]):
            src = io["s" + nm + "T"] if kind == "s" else io[nm + "T"][e]
            pool.dma_start(out=wsl[par][j][:],
                           in_=src.rearrange("(kc p) n -> p kc n", p=128)
                           ).then_inc(ws.h, 16)
            ws.n += 16
        loaded[idx] = ws.n

    free_pp = ev_const
    free_pp2 = ev_const
    free_pr_ev = [ev.n, ev.n]

    for c in range(NCH):
        pool.wait_ge(ev.h, ev.n)
        pool.wait_ge(ea.h, ea.n)
        pool.memset(acc[:], 0.0)
        acc_zero_ep = ep.n

        xt_free = [b.ev.n, b.ev.n]
        ptv = [pp[:, 0:64].bitcast(BF16), pp2[:, 0:64].bitcast(BF16)]
        free_pt = [free_pp, free_pp2]
        for i in range(NT):
            k = i % 2
            xs = b.xdma if k == 0 else b.xdma2
            sp.wait_ge(ev.h, xt_free[k])
            sp.dma_start(out=xt[:, k, :], in_=io["x1"][c * CH + i * 128:c * CH + (i + 1) * 128, :]
                         ).then_inc(xs.h, 16)
            xs.n += 16
            dve.wait_ge(xs.h, xs.n)
            act.wait_ge(xs.h, xs.n)
            _ln_stats(b, xt[:, k, :], i % 8)
            dve.wait_ge(ea.h, ea.n)
            _ln_mr(b, slice(i % 8, i % 8 + 1))
            dve.tensor_scalar(xn[:, i, :], xt[:, k, :], b.a_mcol[:, i % 8:i % 8 + 1],
                              b.a_rcol[:, i % 8:i % 8 + 1], ALU.subtract, ALU.mult)
            xt_free[k] = ev.n
            ev_xn_i = ev.n
            for fc in range(8):
                kk = (i * 8 + fc) % 2
                pe.wait_ge(ev.h, max(free_pt[kk], ev_xn_i))
                t_mm = mm.inc(pe.transpose(ptv[kk], xn[:, i, fc * 128:(fc + 1) * 128],
                                           id_sb[:]))
                dve.wait_ge(mm.h, t_mm)
                dve.tensor_copy(xnT[:, fc, i * 128:(i + 1) * 128], ptv[kk])
                free_pt[kk] = ev.n
        free_pp, free_pp2 = free_pt

        issue_weight_load(0)
        issue_weight_load(1)

        pe.wait_ge(ev.h, ev.n)
        pe.wait_ge(b.pdma.h, gw_loaded)
        for i in range(NT):
            pru = pr if i % 2 == 0 else pr2
            pe.wait_ge(ev.h, free_pr_ev[i % 2])
            for kc in range(8):
                last = pe.matmul(pru[:], xnT[:, kc, i * 128:(i + 1) * 128],
                                 gw_sb[:, kc, :], start=(kc == 0), stop=(kc == 7))
            r_mm = mm.inc(last)
            dve.wait_ge(mm.h, r_mm)
            dve.tensor_copy(logits[:, i, :], pru[:])
            free_pr_ev[i % 2] = ev.n
            dve.tensor_tensor(bwork[:], logits[:, i, :], ebias_bc[:], ALU.add)
            dve.memset(selm[:], 0.0)
            for kk in range(4):
                dve.tensor_reduce(den[:], bwork[:], axis=AX.X, op=ALU.max)
                dve.tensor_scalar(t1[:], bwork[:], den[:], None, ALU.is_equal)
                dve.tensor_tensor(t2[:], t1[:], iotamB_bc[:], ALU.mult)
                dve.tensor_scalar_add(t2[:], t2[:], BIG)
                dve.tensor_reduce(den[:], t2[:], axis=AX.X, op=ALU.min)
                dve.tensor_scalar(t1[:], iota_bc[:], den[:], None, ALU.is_equal)
                dve.tensor_tensor(selm[:], selm[:], t1[:], ALU.add)
                dve.scalar_tensor_tensor(bwork[:], t1[:], -BIG, bwork[:], ALU.mult, ALU.add)
            dve.tensor_scalar_add(t1[:], logits[:, i, :], -BIG)
            dve.scalar_tensor_tensor(lmrow[:], selm[:], BIG, t1[:], ALU.mult, ALU.add)
            dve.tensor_reduce(den[:], lmrow[:], axis=AX.X, op=ALU.max)
            dve.tensor_scalar_mul(den[:], den[:], -1.0)
            act.wait_ge(ev.h, ev.n)
            act.activation(g_all[:, i, :], lmrow[:], AF.Exp, bias=den[:],
                           accum_out=b.a_vcol[:, 15:16])
            dve.wait_ge(ea.h, ea.n)
            dve.reciprocal(den[:], b.a_vcol[:, 15:16])
            dve.tensor_scalar(g_all[:, i, :], g_all[:, i, :], den[:], None, ALU.mult)

        dve.wait_ge(ep.h, acc_zero_ep)
        R = [pp, pp2, pp3]
        buf_free = [(ev, free_pp), (ev, free_pp2), (ev, ev.n)]
        for idx in range(len(worder)):
            kind, e = worder[idx]
            par = idx % 2
            pe.wait_ge(b.wd[par].h, loaded[idx])
            w1s, w3s, w2s = wsl[par]
            for m in range(8):
                ja = (2 * m) % 3
                jb = (2 * m + 1) % 3
                sm, cnt = buf_free[ja]
                pe.wait_ge(sm.h, cnt)
                for kc in range(8):
                    for half in range(2):
                        sl = slice(half * 512, (half + 1) * 512)
                        last = pe.matmul(R[ja][:, sl], w1s[:, kc, m * 128:(m + 1) * 128],
                                         xnT[:, kc, sl], start=(kc == 0), stop=(kc == 7))
                h1_mm = mm.inc(last)
                sm, cnt = buf_free[jb]
                pe.wait_ge(sm.h, cnt)
                for kc in range(8):
                    for half in range(2):
                        sl = slice(half * 512, (half + 1) * 512)
                        last = pe.matmul(R[jb][:, sl], w3s[:, kc, m * 128:(m + 1) * 128],
                                         xnT[:, kc, sl], start=(kc == 0), stop=(kc == 7))
                h3_mm = mm.inc(last)
                act.wait_ge(mm.h, h1_mm)
                act.wait_ge(ev.h, ev.n)
                act.activation(h1s[:], R[ja][:], AF.Silu)
                buf_free[ja] = (ea, ea.n)
                dve.wait_ge(mm.h, h3_mm)
                dve.wait_ge(ea.h, ea.n)
                dve.scalar_tensor_tensor(h_sb[:, m, :], R[jb][:], 1.0, h1s[:],
                                         ALU.mult, ALU.mult)
                buf_free[jb] = (ev, ev.n)
            ev_h = ev.n
            pe.wait_ge(ev.h, ev_h)
            for i in range(NT):
                k = i % 2
                sm, cnt = buf_free[k]
                pe.wait_ge(sm.h, cnt)
                for ic in range(8):
                    for half in range(2):
                        sl = slice(half * 512, (half + 1) * 512)
                        last = pe.matmul(R[k][:, sl], h_sb[:, ic, i * 128:(i + 1) * 128],
                                         w2s[:, ic, sl], start=(ic == 0), stop=(ic == 7))
                d_mm = mm.inc(last)
                dve.wait_ge(mm.h, d_mm)
                gcol = gones[:, 0:1] if kind == "s" else g_all[:, i, e:e + 1]
                dve.scalar_tensor_tensor(acc[:, i, :], R[k][:], gcol, acc[:, i, :],
                                         ALU.mult, ALU.add)
                buf_free[k] = (ev, ev.n)
            compute_done[idx] = mm.n
            if idx + 2 < len(worder):
                issue_weight_load(idx + 2)
        free_pp = _free_as_ev(b, buf_free[0])
        free_pp2 = _free_as_ev(b, buf_free[1])

        act.wait_ge(ev.h, ev.n)
        xt_free = [b.ev.n, b.ev.n]
        ya_free = [b.out.n, b.out2.n]
        for i in range(NT):
            k = i % 2
            xs = b.xdma if k == 0 else b.xdma2
            sp.wait_ge(ev.h, xt_free[k])
            sp.dma_start(out=xt[:, k, :], in_=io["x1"][c * CH + i * 128:c * CH + (i + 1) * 128, :]
                         ).then_inc(xs.h, 16)
            xs.n += 16
            _ln_stats(b, acc[:, i, :], 8 + (i % 2))
            dve.wait_ge(ea.h, ea.n)
            _ln_mr(b, slice(8 + (i % 2), 9 + (i % 2)))
            dve.wait_ge((b.out if k == 0 else b.out2).h, ya_free[k])
            dve.wait_ge(xs.h, xs.n)
            dve.tensor_scalar(ya[:, k, :], acc[:, i, :], b.a_mcol[:, 8 + i % 2:9 + i % 2],
                              b.a_rcol[:, 8 + i % 2:9 + i % 2], ALU.subtract, ALU.mult)
            dve.tensor_tensor(ya[:, k, :], ya[:, k, :], xt[:, k, :], ALU.add)
            xt_free[k] = ev.n
            os_ = b.out if k == 0 else b.out2
            sp.wait_ge(ev.h, ev.n)
            sp.dma_start(out=io["x2"][c * CH + i * 128:c * CH + (i + 1) * 128, :],
                         in_=ya[:, k, :]).then_inc(os_.h, 16)
            os_.n += 16
            ya_free[k] = os_.n

    sp.wait_ge(b.out.h, b.out.n)
    sp.wait_ge(b.out2.h, b.out2.n)


def hostprep_b(inputs):
    iota = np.tile(np.arange(16, dtype=np.float32), (128, 1))
    return dict(
        gate_wT=np.ascontiguousarray(np.asarray(inputs["L0_ffn_gate_w"]).T),
        expert_bias=np.asarray(inputs["L0_ffn_expert_bias"]),
        w1T=np.ascontiguousarray(np.asarray(inputs["L0_ffn_w1"]).transpose(0, 2, 1)),
        w2T=np.ascontiguousarray(np.asarray(inputs["L0_ffn_w2"]).transpose(0, 2, 1)),
        w3T=np.ascontiguousarray(np.asarray(inputs["L0_ffn_w3"]).transpose(0, 2, 1)),
        sw1T=np.ascontiguousarray(np.asarray(inputs["L0_ffn_sw1"]).T),
        sw2T=np.ascontiguousarray(np.asarray(inputs["L0_ffn_sw2"]).T),
        sw3T=np.ascontiguousarray(np.asarray(inputs["L0_ffn_sw3"]).T),
        iota16=iota, iotamB16=np.ascontiguousarray(iota - BIG),
        ident=np.eye(128, dtype=BF))


B_IO = [("gate_wT", [D, 16], F32), ("expert_bias", [16], F32),
        ("w1T", [NE, D, D], F32), ("w2T", [NE, D, D], F32), ("w3T", [NE, D, D], F32),
        ("sw1T", [D, D], F32), ("sw2T", [D, D], F32), ("sw3T", [D, D], F32),
        ("iota16", [128, 16], F32), ("iotamB16", [128, 16], F32),
        ("ident", [128, 128], BF16)]


def build_b_program():
    b = B()
    io = {n: b.dram(n, s, d, "ExternalInput") for n, s, d in B_IO}
    io["x1"] = b.dram("x1", [T, D], F32, "ExternalInput")
    io["x2"] = b.dram("x2", [T, D], F32, "ExternalOutput")
    with b.ctx:
        blk = b.ctx.enter_context(b.nc.Block())

        @blk.gpsimd
        def _(g):
            build_stage_b(b, io)
    return b.nc




def build_stage_c(b, io, n_cores=8):
    CH = 1024
    NCH = T // CH
    NT = CH // 128
    nc = b.nc
    pe, act, dve, pool, sp = b.pe, b.act, b.dve, b.pool, b.sp
    dma, mm, ea, ev, ep = b.dma, b.mm, b.ea, b.ev, b.ep

    qT_h = io["qT_h"]
    kT_h = io["kT_h"]
    v_h = io["v_h"]
    cc_in = io["cc_in"]
    cc_out = io["cc_out"]

    energy = b.sb("c_energy", [16, T], F32)
    mask_all = b.sb("c_mask", [16, T], BF16)
    ones1c = b.sb("c_ones1c", [1, 128], F32)
    mk16 = b.sb("c_mk16", [16, 1], F32)
    qstab = b.sb("c_qstab", [16, T], BF16)
    thr = b.sb("c_thr", [1, 1], F32)
    thr16 = b.sb("c_thr16", [16, 1], F32)
    srow = b.sb("c_srow", [16, 1], F32)
    s11 = b.sb("c_s11", [1, 1], F32)
    ones2 = b.sb("c_ones2", [128, 16], F32)
    selmat = b.sb("c_selmat", [128, 8, 16], BF16)
    eh64 = b.sb("c_eh64", [16, NH * 64], BF16)
    ctx_fm = b.sb("c_ctx", [128, 8, T], BF16)
    id_sb = b.sb("c_id", [128, 128], BF16)
    b.a_s1 = b.sb("c_s1", [128, 16], F32)
    b.a_ssq = b.sb("c_ssq", [128, 16], F32)
    b.a_mcol = b.sb("c_mcol", [128, 16], F32)
    b.a_vcol = b.sb("c_vcol", [128, 16], F32)
    b.a_rcol = b.sb("c_rcol", [128, 16], F32)
    b.a_sqscr = b.sb("c_sqscr", [128, D], BF16)
    b.a_ecol = b.sb("c_ecol", [128, 1], F32)

    pool.memset(ones2[:], 1.0)
    pool.memset(ones1c[:], 1.0)
    pool.memset(b.a_ecol[:], EPS)
    pool.memset(mk16[:], 0.0)
    pn0 = ep.n
    for dst, src in [(id_sb[:], io["ident"]), (selmat[:], io["selmat"]),
                     (eh64[:], io["eh64"])]:
        sp.dma_start(out=dst, in_=src).then_inc(b.cdma.h, 16)
        b.cdma.n += 16
    c_const = b.cdma.n

    with ExitStack() as c1ctx:
        sb1 = lambda n, sh, dt: c1ctx.enter_context(nc.sbuf_tensor(n, sh, dt))
        xt = sb1("c_xt", [128, 2, D], F32)
        xn = sb1("c_xn", [128, NT, D], BF16)
        xnT = sb1("c_xnT", [128, 8, CH], BF16)
        wq_sb = sb1("c_wq", [128, 8, D], BF16)
        wk_sb = sb1("c_wk", [128, 8, D], BF16)
        wv_sb = sb1("c_wv", [128, 8, D], BF16)
        bq_sb = sb1("c_bq", [128, 8], F32)
        bk_sb = sb1("c_bk", [128, 8], F32)
        bvrow = sb1("c_bvrow", [1, D], F32)
        bv_bc = sb1("c_bvbc", [128, D], F32)
        stage = sb1("c_stage", [128, 2, D], BF16)
        sqs = sb1("c_sqs", [128, CH], BF16)
        ek_scr = sb1("c_ekscr", [16, CH], BF16)
        ekcol = sb1("c_ekcol", [16, 1], F32)

        ones1 = sb1("c_ones1", [1, 128], F32)
        with ExitStack() as p1ctx:
            pp = p1ctx.enter_context(nc.psum_tensor("c_pp", [128, D], F32))
            pp2 = p1ctx.enter_context(nc.psum_tensor("c_pp2", [128, D], F32))
            pes = p1ctx.enter_context(nc.psum_tensor("c_pes", [16, CH], F32))

            pool.memset(ones1[:], 1.0)
            pn = ep.n
            for dst, src in [(wq_sb[:], io["wqT1"]), (wk_sb[:], io["wkT1"]),
                             (wv_sb[:], io["wvT1"])]:
                pool.dma_start(out=dst, in_=src.rearrange("(kc p) n -> p kc n", p=128)
                               ).then_inc(b.pdma.h, 16)
                b.pdma.n += 16
            for dst, src in [(bq_sb[:], io["bq1"]), (bk_sb[:], io["bk1"]),
                             (bvrow[:], io["bv1"].unsqueeze(0))]:
                sp.dma_start(out=dst, in_=src).then_inc(b.cdma.h, 16)
                b.cdma.n += 16
            w1_loaded = b.pdma.n
            pe.wait_ge(b.cdma.h, b.cdma.n)
            pe.wait_ge(ep.h, pn)
            for half in range(2):
                sl = slice(half * 512, (half + 1) * 512)
                last = pe.matmul(pp[:, sl], ones1[:], bvrow[:, sl], start=True, stop=True)
            bv_mm = mm.inc(last)
            dve.wait_ge(mm.h, bv_mm)
            dve.tensor_copy(bv_bc[:], pp[:])
            free_pp = ev.n
            free_pp2 = ev.n
            pe.wait_ge(b.pdma.h, w1_loaded)

            for c in range(NCH):
                xt_free = [ev.n, ev.n]
                ptv = [pp[:, 0:64].bitcast(BF16), pp2[:, 0:64].bitcast(BF16)]
                free_pt = [free_pp, free_pp2]
                for i in range(NT):
                    k = i % 2
                    xs = b.xdma if k == 0 else b.xdma2
                    sp.wait_ge(ev.h, xt_free[k])
                    sp.dma_start(out=xt[:, k, :],
                                 in_=io["x2"][c * CH + i * 128:c * CH + (i + 1) * 128, :]
                                 ).then_inc(xs.h, 16)
                    xs.n += 16
                    dve.wait_ge(xs.h, xs.n)
                    act.wait_ge(xs.h, xs.n)
                    _ln_stats(b, xt[:, k, :], i)
                    dve.wait_ge(ea.h, ea.n)
                    _ln_mr(b, slice(i, i + 1))
                    dve.tensor_scalar(xn[:, i, :], xt[:, k, :], b.a_mcol[:, i:i + 1],
                                      b.a_rcol[:, i:i + 1], ALU.subtract, ALU.mult)
                    xt_free[k] = ev.n
                    ev_xn_i = ev.n
                    for fc in range(8):
                        kk = (i * 8 + fc) % 2
                        pe.wait_ge(ev.h, max(free_pt[kk], ev_xn_i))
                        t_mm = mm.inc(pe.transpose(ptv[kk], xn[:, i, fc * 128:(fc + 1) * 128],
                                                   id_sb[:]))
                        dve.wait_ge(mm.h, t_mm)
                        dve.tensor_copy(xnT[:, fc, i * 128:(i + 1) * 128], ptv[kk])
                        free_pt[kk] = ev.n
                free_pp, free_pp2 = free_pt
                pe.wait_ge(ev.h, ev.n)

                for wsb, bcol, isq, hdr in [(wq_sb, bq_sb, True, qT_h),
                                            (wk_sb, bk_sb, False, kT_h)]:
                    st_free = [b.out.n, b.out2.n]
                    for m in range(8):
                        k = m % 2
                        ppu = pp if k == 0 else pp2
                        pe.wait_ge(ev.h, free_pp if k == 0 else free_pp2)
                        for kc in range(8):
                            for half in range(2):
                                sl = slice(half * 512, (half + 1) * 512)
                                last = pe.matmul(ppu[:, sl],
                                                 wsb[:, kc, m * 128:(m + 1) * 128],
                                                 xnT[:, kc, sl],
                                                 start=(kc == 0), stop=(kc == 7))
                        q_mm = mm.inc(last)
                        dve.wait_ge(mm.h, q_mm)
                        os_ = b.out if k == 0 else b.out2
                        dve.wait_ge(os_.h, st_free[k])
                        dve.tensor_scalar(stage[:, k, :], ppu[:], bcol[:, m:m + 1], None,
                                          ALU.add)
                        if k == 0:
                            free_pp = ev.n
                        else:
                            free_pp2 = ev.n
                        act.wait_ge(ev.h, ev.n)
                        act.activation(sqs[:], stage[:, k, :], AF.Square)
                        pe.wait_ge(ea.h, ea.n)
                        for half in range(2):
                            sl = slice(half * 512, (half + 1) * 512)
                            last = pe.matmul(pes[:, sl], selmat[:, m, :], sqs[:, sl],
                                             start=(m == 0), stop=(m == 7))
                        es_mm = mm.inc(last)
                        sp.wait_ge(ev.h, ev.n)
                        sp.dma_start(out=hdr[m * 128:(m + 1) * 128, c * CH:(c + 1) * CH],
                                     in_=stage[:, k, :]).then_inc(os_.h, 16)
                        os_.n += 16
                        st_free[k] = os_.n
                    act.wait_ge(mm.h, mm.n)
                    act.wait_ge(ep.h, pn0)
                    if isq:
                        act.activation(energy[:, c * CH:(c + 1) * CH], pes[:], AF.Sqrt)
                    else:
                        act.activation(ek_scr[:], pes[:], AF.Sqrt)
                        dve.wait_ge(ea.h, ea.n)
                        dve.tensor_reduce(ekcol[:], ek_scr[:], axis=AX.X, op=ALU.max)
                        dve.tensor_tensor(mk16[:], mk16[:], ekcol[:], ALU.max)
                    pe.wait_ge(ea.h, ea.n)
                st_free = [b.out.n, b.out2.n]
                for i in range(NT):
                    k = i % 2
                    ppu = pp if k == 0 else pp2
                    pe.wait_ge(ev.h, free_pp if k == 0 else free_pp2)
                    for kc in range(8):
                        for half in range(2):
                            sl = slice(half * 512, (half + 1) * 512)
                            last = pe.matmul(ppu[:, sl], xnT[:, kc, i * 128:(i + 1) * 128],
                                             wv_sb[:, kc, sl], start=(kc == 0), stop=(kc == 7))
                    v_mm = mm.inc(last)
                    dve.wait_ge(mm.h, v_mm)
                    os_ = b.out if k == 0 else b.out2
                    dve.wait_ge(os_.h, st_free[k])
                    dve.scalar_tensor_tensor(stage[:, k, :], ppu[:], 1.0, bv_bc[:],
                                             ALU.mult, ALU.add)
                    if k == 0:
                        free_pp = ev.n
                    else:
                        free_pp2 = ev.n
                    sp.wait_ge(ev.h, ev.n)
                    sp.dma_start(out=v_h[c * CH + i * 128:c * CH + (i + 1) * 128, :],
                                 in_=stage[:, k, :]).then_inc(os_.h, 16)
                    os_.n += 16
                    st_free[k] = os_.n

    dve.wait_ge(ea.h, ea.n)
    dve.wait_ge(b.cdma.h, b.cdma.n)
    dve.tensor_reduce(srow[:], energy[:], axis=AX.X, op=ALU.add)
    with ExitStack() as ptx:
        p11 = ptx.enter_context(nc.psum_tensor("c_p11", [128, 16], F32))
        pe.wait_ge(ev.h, ev.n)
        pe.wait_ge(ep.h, pn0)
        l_mm = mm.inc(pe.matmul(p11[0:1, 0:1], srow[:], ones2[0:16, 0:1],
                                start=True, stop=True))
        dve.wait_ge(mm.h, l_mm)
        dve.tensor_copy(s11[:], p11[0:1, 0:1])
        pool.wait_ge(ev.h, ev.n)
        pool.dma_start(out=cc_in, in_=s11[:]).then_inc(b.pdma.h, 16)
        b.pdma.n += 16
        pool.wait_ge(b.pdma.h, b.pdma.n)
        pool.collective_compute("AllReduce", ALU.add,
                                replica_groups=[list(range(n_cores))],
                                ins=[cc_in.opt()], outs=[cc_out.opt()]).then_inc(b.ep.h, 1)
        b.ep.n += 1
        pool.wait_ge(b.ep.h, b.ep.n)
        pool.dma_start(out=thr[:], in_=cc_out).then_inc(b.pdma.h, 16)
        b.pdma.n += 16
        dve.wait_ge(b.pdma.h, b.pdma.n)
        dve.tensor_scalar_mul(thr[:], thr[:], 0.1 / (n_cores * T * NH))
        ev_thr = ev.n
        pe.wait_ge(ev.h, ev_thr)
        t_mm = mm.inc(pe.matmul(p11[0:16, 0:1], ones2[0:1, :], thr[:], start=True, stop=True))
        dve.wait_ge(mm.h, t_mm)
        dve.tensor_copy(thr16[:], p11[0:16, 0:1])
        dve.tensor_scalar(mask_all[:], energy[:], thr16[:], None, ALU.is_gt)
        dve.tensor_scalar_mul(ekcol2 := b.a_vcol[0:16, 14:15], mk16[:], -1.0)
        dve.tensor_tensor(energy[:], energy[:], mask_all[:], ALU.mult)
        dve.tensor_scalar(qstab[:], energy[:], b.a_vcol[0:16, 14:15], None, ALU.mult)
        ev_mask = ev.n

    KTG = 2
    with ExitStack() as c2ctx:
        sb2 = lambda n, sh, dt: c2ctx.enter_context(nc.sbuf_tensor(n, sh, dt))
        qh = [sb2(f"c_qh{p}", [64, T], BF16) for p in range(2)]
        kh = [sb2(f"c_kh{p}", [65, T], BF16) for p in range(2)]
        vh = [sb2(f"c_vh{p}", [128, 32, 65], BF16) for p in range(2)]
        qm = [sb2(f"c_qm{p}", [65, 512], BF16) for p in range(2)]
        phat = [sb2(f"c_ph{p}", [128, KTG * 512], BF16) for p in range(2)]
        rrow = sb2("c_rrow", [128, 512], F32)
        recipb = sb2("c_recipb", [64, 512], BF16)
        onesc = sb2("c_onesc", [128, 128], F32)
        with ExitStack() as p2ctx:
            psc2 = [p2ctx.enter_context(nc.psum_tensor(f"c_psc{p}", [128, KTG * 512], F32))
                    for p in range(2)]
            pctx = [p2ctx.enter_context(nc.psum_tensor(f"c_pctx{p}", [65, 512], F32))
                    for p in range(2)]
            pmb = [p2ctx.enter_context(nc.psum_tensor(f"c_pmb{p}", [128, 512], F32))
                   for p in range(2)]

            pool.wait_ge(ev.h, ev.n)
            pool.wait_ge(ea.h, ea.n)
            pool.wait_ge(mm.h, mm.n)
            pool.memset(onesc[:], 1.0)
            pool.memset(vh[0][:], 1.0)
            pool.memset(vh[1][:], 1.0)
            pool.memset(kh[0][:], 1.0)
            pool.memset(kh[1][:], 1.0)
            pn2 = ep.n
            hd_sem = [b.hd[0], b.hd[1]]
            h_loaded = {}
            h_done = {}

            def load_head(h):
                par = h % 2
                ws = hd_sem[par]
                if h >= 2:
                    sp.wait_ge(mm.h, h_done[h - 2])
                sp.dma_start(out=qh[par][:], in_=qT_h[h * 64:(h + 1) * 64, :]
                             ).then_inc(ws.h, 16)
                sp.dma_start(out=kh[par][0:64, :], in_=kT_h[h * 64:(h + 1) * 64, :]
                             ).then_inc(ws.h, 16)
                sp.dma_start(out=vh[par][:, :, 0:64],
                             in_=v_h[:, h * 64:(h + 1) * 64]
                             .rearrange("(kt p) d -> p kt d", p=128)).then_inc(ws.h, 16)
                ws.n += 48
                h_loaded[h] = ws.n

            sp.wait_ge(b.out.h, b.out.n)
            sp.wait_ge(b.out2.h, b.out2.n)
            sp.wait_ge(ep.h, pn2)
            load_head(0)
            load_head(1)
            free_phat = [mm.n, mm.n]
            free_psc_ea = [ea.n, ea.n]
            free_pctx = [ev.n, ev.n]
            qmf = [mm.n, mm.n]
            free_pmb = [ev.n, ev.n]
            ev_qm_s = [ev.n, ev.n]
            ngrp = 32 // KTG
            pending = None
            act.wait_ge(ev.h, ev_mask)
            pairs = [(h, qt) for h in range(NH) for qt in range(8)]

            def emit_prologue(idx):
                ph, pqt = pairs[idx]
                ppar = ph % 2
                qs = pqt % 2
                if pqt == 0:
                    pe.wait_ge(b.hd[ppar].h, h_loaded[ph])
                    pe.wait_ge(ev.h, ev_mask)
                    pe.wait_ge(ep.h, pn2)
                pe.wait_ge(ev.h, free_pmb[qs])
                pe.matmul(pmb[qs][0:64, :], eh64[:, ph * 64:(ph + 1) * 64],
                          mask_all[:, pqt * 512:(pqt + 1) * 512],
                          start=True, stop=True)
                mb_mm = mm.inc(pe.matmul(pmb[qs][64:65, :], id_sb[0:16, ph:ph + 1],
                                         qstab[:, pqt * 512:(pqt + 1) * 512],
                                         start=True, stop=True))
                dve.wait_ge(mm.h, mb_mm)
                dve.wait_ge(mm.h, qmf[qs])
                dve.tensor_tensor(qm[qs][0:64, :],
                                  qh[ppar][:, pqt * 512:(pqt + 1) * 512],
                                  pmb[qs][0:64, :], ALU.mult)
                dve.tensor_copy(qm[qs][64:65, :], pmb[qs][64:65, :])
                free_pmb[qs] = ev.n
                ev_qm_s[qs] = ev.n

            def emit_recip(ph, pqt, pctx_mm):
                pqs = pqt % 2
                dve.wait_ge(mm.h, pctx_mm)
                dve.reciprocal(rrow[64:65, :], pctx[pqs][64:65, :])
                return ev.n

            def emit_normalize(ph, pqt, pctx_mm, ev_rec):
                pqs = pqt % 2
                pe.wait_ge(ev.h, ev_rec)
                pe.wait_ge(ev.h, free_pmb[pqs])
                r_mm = mm.inc(pe.matmul(pmb[pqs][0:64, :], onesc[64:65, 0:64],
                                        rrow[64:65, :], start=True, stop=True))
                dve.wait_ge(mm.h, r_mm)
                dve.tensor_copy(recipb[:], pmb[pqs][0:64, :])
                free_pmb[pqs] = ev.n
                pfc, pr0 = ph // 2, 64 * (ph % 2)
                dve.scalar_tensor_tensor(
                    ctx_fm[pr0:pr0 + 64, pfc, pqt * 512:(pqt + 1) * 512],
                    pctx[pqs][0:64, :], 1.0, recipb[:], ALU.mult, ALU.mult)
                free_pctx[pqs] = ev.n

            emit_prologue(0)
            for idx, (h, qt) in enumerate(pairs):
                par = h % 2
                qs = qt % 2

                def emit_sc(g):
                    sl_par = g % 2
                    for j, kt in enumerate(range(g * KTG, (g + 1) * KTG)):
                        last = pe.matmul(psc2[sl_par][:, j * 512:(j + 1) * 512],
                                         kh[par][:, kt * 128:(kt + 1) * 128],
                                         qm[qs][:], start=True, stop=True)
                    return mm.inc(last)

                sc_mm = [0] * ngrp
                if pending is not None:
                    pend_rec = emit_recip(*pending)
                pe.wait_ge(ev.h, ev_qm_s[qs])
                pe.wait_ge(ea.h, free_psc_ea[0])
                sc_mm[0] = emit_sc(0)
                for g in range(ngrp):
                    sl_par = g % 2
                    if g + 1 < ngrp:
                        pe.wait_ge(ea.h, free_psc_ea[(g + 1) % 2])
                        sc_mm[g + 1] = emit_sc(g + 1)
                    if g == 1 and pending is not None:
                        emit_normalize(*pending, pend_rec)
                        pending = None
                    if g == 3 and idx + 1 < len(pairs):
                        emit_prologue(idx + 1)
                    act.wait_ge(mm.h, sc_mm[g])
                    act.wait_ge(mm.h, free_phat[sl_par])
                    act.activation(phat[sl_par][:], psc2[sl_par][:],
                                   AF.Exp, scale=0.125)
                    free_psc_ea[sl_par] = ea.n
                    pe.wait_ge(ea.h, free_psc_ea[sl_par])
                    if g == 0:
                        pe.wait_ge(ev.h, free_pctx[qs])
                    for j, kt in enumerate(range(g * KTG, (g + 1) * KTG)):
                        last = pe.matmul(pctx[qs][:], vh[par][:, kt, :],
                                         phat[sl_par][:, j * 512:(j + 1) * 512],
                                         start=(kt == 0), stop=(kt == 31))
                    free_phat[sl_par] = mm.inc(last)
                qmf[qs] = sc_mm[ngrp - 1]
                pending = (h, qt, free_phat[(ngrp - 1) % 2])
                if qt == 7:
                    h_done[h] = mm.n
                    if h + 2 < NH:
                        load_head(h + 2)
            emit_normalize(*pending, emit_recip(*pending))
            pending = None

    with ExitStack() as c3ctx:
        sb3 = lambda n, sh, dt: c3ctx.enter_context(nc.sbuf_tensor(n, sh, dt))
        wo_sb = sb3("c_wo", [128, 8, D], BF16)
        borow = sb3("c_borow", [1, D], F32)
        bo_bc = sb3("c_bobc", [128, D], F32)
        xt = sb3("c_xt3", [128, 2, D], F32)
        ya = sb3("c_ya3", [128, 2, D], F32)
        yb = sb3("c_yb3", [128, D], F32)
        ones1b = sb3("c_ones1b", [1, 128], F32)
        with ExitStack() as p3ctx:
            pp = p3ctx.enter_context(nc.psum_tensor("c3_pp", [128, D], F32))
            pp2 = p3ctx.enter_context(nc.psum_tensor("c3_pp2", [128, D], F32))
            pool.wait_ge(ev.h, ev.n)
            pool.wait_ge(ea.h, ea.n)
            pool.wait_ge(mm.h, mm.n)
            pool.memset(ones1b[:], 1.0)
            pn3 = ep.n
            pool.dma_start(out=wo_sb[:], in_=io["woT1"].rearrange("(kc p) n -> p kc n", p=128)
                           ).then_inc(b.pdma.h, 16)
            b.pdma.n += 16
            sp.wait_ge(mm.h, mm.n)
            sp.wait_ge(ev.h, ev.n)
            sp.wait_ge(ea.h, ea.n)
            sp.dma_start(out=borow[:], in_=io["bo1"].unsqueeze(0)).then_inc(b.cdma.h, 16)
            b.cdma.n += 16
            pe.wait_ge(b.cdma.h, b.cdma.n)
            pe.wait_ge(ep.h, pn3)
            for half in range(2):
                sl = slice(half * 512, (half + 1) * 512)
                last = pe.matmul(pp[:, sl], ones1b[:], borow[:, sl], start=True, stop=True)
            bo_mm = mm.inc(last)
            dve.wait_ge(mm.h, bo_mm)
            dve.tensor_copy(bo_bc[:], pp[:])
            free_pp = ev.n
            free_pp2 = ev.n
            pe.wait_ge(b.pdma.h, b.pdma.n)
            pe.wait_ge(ev.h, ev.n)
            xt_free = [ev.n, ev.n]
            ya_free = [b.out.n, b.out2.n]
            for ti in range(T // 128):
                k = ti % 2
                xs = b.xdma if k == 0 else b.xdma2
                sp.wait_ge(ev.h, xt_free[k])
                sp.dma_start(out=xt[:, k, :], in_=io["x2"][ti * 128:(ti + 1) * 128, :]
                             ).then_inc(xs.h, 16)
                xs.n += 16
                ppu = pp if k == 0 else pp2
                pe.wait_ge(ev.h, free_pp if k == 0 else free_pp2)
                for kc in range(8):
                    for half in range(2):
                        sl = slice(half * 512, (half + 1) * 512)
                        last = pe.matmul(ppu[:, sl], ctx_fm[:, kc, ti * 128:(ti + 1) * 128],
                                         wo_sb[:, kc, sl], start=(kc == 0), stop=(kc == 7))
                o_mm = mm.inc(last)
                dve.wait_ge(mm.h, o_mm)
                dve.wait_ge((b.out if k == 0 else b.out2).h, ya_free[k])
                dve.scalar_tensor_tensor(ya[:, k, :], ppu[:], 1.0, bo_bc[:],
                                         ALU.mult, ALU.add)
                if k == 0:
                    free_pp = ev.n
                else:
                    free_pp2 = ev.n
                act.wait_ge(ev.h, ev.n)
                _ln_stats(b, ya[:, k, :], 8 + k)
                dve.wait_ge(ea.h, ea.n)
                _ln_mr(b, slice(8 + k, 9 + k))
                os_ = b.out if k == 0 else b.out2
                dve.wait_ge(xs.h, xs.n)
                dve.tensor_scalar(yb[:], ya[:, k, :], b.a_mcol[:, 8 + k:9 + k],
                                  b.a_rcol[:, 8 + k:9 + k], ALU.subtract, ALU.mult)
                dve.tensor_tensor(ya[:, k, :], yb[:], xt[:, k, :], ALU.add)
                xt_free[k] = ev.n
                sp.wait_ge(ev.h, ev.n)
                sp.dma_start(out=io["x3"][ti * 128:(ti + 1) * 128, :],
                             in_=ya[:, k, :]).then_inc(os_.h, 16)
                os_.n += 16
                ya_free[k] = os_.n

    sp.wait_ge(b.out.h, b.out.n)
    sp.wait_ge(b.out2.h, b.out2.n)


def hostprep_c(inputs):
    wqkv = np.asarray(inputs["L1_attn_w_qkv"])
    bqkv = np.asarray(inputs["L1_attn_b_qkv"])
    selmat = np.zeros((128, 8, 16), np.float32)
    for m in range(8):
        selmat[0:64, m, 2 * m] = 1.0
        selmat[64:128, m, 2 * m + 1] = 1.0
    eh64 = np.zeros((16, NH * 64), np.float32)
    for h in range(NH):
        eh64[h, h * 64:(h + 1) * 64] = 1.0
    return dict(
        wqT1=np.ascontiguousarray(wqkv[0:D].T),
        wkT1=np.ascontiguousarray(wqkv[D:2 * D].T),
        wvT1=np.ascontiguousarray(wqkv[2 * D:].T),
        bq1=np.ascontiguousarray(bqkv[0:D].reshape(8, 128).T),
        bk1=np.ascontiguousarray(bqkv[D:2 * D].reshape(8, 128).T),
        bv1=np.ascontiguousarray(bqkv[2 * D:]),
        woT1=np.ascontiguousarray(np.asarray(inputs["L1_attn_w_out"]).T),
        bo1=np.asarray(inputs["L1_attn_b_out"]),
        selmat=selmat.astype(BF), eh64=eh64.astype(BF),
        ident=np.eye(128, dtype=BF))


C_IO = [("wqT1", [D, D], F32), ("wkT1", [D, D], F32), ("wvT1", [D, D], F32),
        ("bq1", [128, 8], F32), ("bk1", [128, 8], F32), ("bv1", [D], F32),
        ("woT1", [D, D], F32), ("bo1", [D], F32),
        ("selmat", [128, 8, 16], BF16),
        ("eh64", [16, NH * 64], BF16), ("ident", [128, 128], BF16)]


def build_c_program(n_cores=8):
    b = B()
    io = {n: b.dram(n, s, d, "ExternalInput") for n, s, d in C_IO}
    io["x2"] = b.dram("x2", [T, D], F32, "ExternalInput")
    io["x3"] = b.dram("x3", [T, D], F32, "ExternalOutput")
    io["qT_h"] = b.dram("qT_h", [D, T], BF16, "Internal")
    io["kT_h"] = b.dram("kT_h", [D, T], BF16, "Internal")
    io["v_h"] = b.dram("v_h", [T, D], BF16, "Internal")
    io["cc_in"] = b.dram("cc_in", [1, 1], F32, "Internal")
    io["cc_out"] = b.dram("cc_out", [1, 1], F32, "Internal")
    with b.ctx:
        blk = b.ctx.enter_context(b.nc.Block())

        @blk.gpsimd
        def _(g):
            build_stage_c(b, io, n_cores=n_cores)
    return b.nc




def build_stage_d(b, io):
    nc = b.nc
    pe, act, dve, pool, sp = b.pe, b.act, b.dve, b.pool, b.sp
    dma, mm, ea, ev, ep = b.dma, b.mm, b.ea, b.ev, b.ep

    swT = b.sb("d_swT", [128, 8, D], BF16)
    id_sb = b.sb("d_id", [128, 128], BF16)
    xt = b.sb("d_xt", [128, 2, D], F32)
    xn = b.sb("d_xn", [128, D], BF16)
    xnT = b.sb("d_xnT", [128, 8, 128], BF16)
    ya = b.sb("d_ya", [128, 2, D], F32)
    yb = b.sb("d_yb", [128, D], F32)
    b.a_s1 = b.sb("d_s1", [128, 16], F32)
    b.a_ssq = b.sb("d_ssq", [128, 16], F32)
    b.a_mcol = b.sb("d_mcol", [128, 16], F32)
    b.a_vcol = b.sb("d_vcol", [128, 16], F32)
    b.a_rcol = b.sb("d_rcol", [128, 16], F32)
    b.a_sqscr = b.sb("d_sqscr", [128, D], BF16)
    b.a_ecol = b.sb("d_ecol", [128, 1], F32)

    with ExitStack() as pctx:
        pp = pctx.enter_context(nc.psum_tensor("d_pp", [128, D], F32))
        pp2 = pctx.enter_context(nc.psum_tensor("d_pp2", [128, D], F32))

        pool.wait_ge(ev.h, ev.n)
        pool.wait_ge(ea.h, ea.n)
        pool.wait_ge(mm.h, mm.n)
        pool.memset(b.a_ecol[:], EPS)
        pool.dma_start(out=swT[:], in_=io["wT_d"].rearrange("(kc p) n -> p kc n", p=128)
                       ).then_inc(b.pdma.h, 16)
        b.pdma.n += 16
        sp.wait_ge(mm.h, mm.n)
        sp.wait_ge(ev.h, ev.n)
        sp.wait_ge(ea.h, ea.n)
        sp.dma_start(out=id_sb[:], in_=io["ident"]).then_inc(b.cdma.h, 16)
        b.cdma.n += 16
        act.wait_ge(b.pdma.h, b.pdma.n)
        act.activation(swT[:], swT[:], AF.Sign)
        sw_ea = ea.n
        pe.wait_ge(ea.h, sw_ea)
        pe.wait_ge(b.cdma.h, b.cdma.n)

        free_pp = ev.n
        free_pp2 = ev.n
        xt_free = [ev.n, ev.n]
        ya_free = [b.out.n, b.out2.n]
        for ti in range(T // 128):
            k = ti % 2
            xs = b.xdma if k == 0 else b.xdma2
            os_ = b.out if k == 0 else b.out2
            sp.wait_ge(ev.h, xt_free[k])
            sp.dma_start(out=xt[:, k, :], in_=io["x3"][ti * 128:(ti + 1) * 128, :]
                         ).then_inc(xs.h, 16)
            xs.n += 16
            dve.wait_ge(xs.h, xs.n)
            act.wait_ge(xs.h, xs.n)
            _ln_stats(b, xt[:, k, :], k)
            dve.wait_ge(ea.h, ea.n)
            _ln_mr(b, slice(k, k + 1))
            dve.wait_ge(mm.h, mm.n)
            dve.tensor_scalar(xn[:], xt[:, k, :], b.a_mcol[:, k:k + 1],
                              b.a_rcol[:, k:k + 1], ALU.subtract, ALU.mult)
            ev_xn = ev.n
            ptv = [pp[:, 0:64].bitcast(BF16), pp2[:, 0:64].bitcast(BF16)]
            free_pt = [free_pp, free_pp2]
            pe.wait_ge(ev.h, ev_xn)
            for fc in range(8):
                kk = fc % 2
                pe.wait_ge(ev.h, free_pt[kk])
                t_mm = mm.inc(pe.transpose(ptv[kk], xn[:, fc * 128:(fc + 1) * 128], id_sb[:]))
                dve.wait_ge(mm.h, t_mm)
                dve.tensor_copy(xnT[:, fc, :], ptv[kk])
                free_pt[kk] = ev.n
            free_pp, free_pp2 = free_pt
            ppu = pp if k == 0 else pp2
            pe.wait_ge(ev.h, ev.n)
            for kc in range(8):
                for half in range(2):
                    sl = slice(half * 512, (half + 1) * 512)
                    last = pe.matmul(ppu[:, sl], xnT[:, kc, :], swT[:, kc, sl],
                                     start=(kc == 0), stop=(kc == 7))
            y_mm = mm.inc(last)
            dve.wait_ge(mm.h, y_mm)
            dve.wait_ge(os_.h, ya_free[k])
            dve.tensor_copy(ya[:, k, :], ppu[:])
            if k == 0:
                free_pp = ev.n
            else:
                free_pp2 = ev.n
            act.wait_ge(ev.h, ev.n)
            _ln_stats(b, ya[:, k, :], 8 + k)
            dve.wait_ge(ea.h, ea.n)
            _ln_mr(b, slice(8 + k, 9 + k))
            dve.tensor_scalar(yb[:], ya[:, k, :], b.a_mcol[:, 8 + k:9 + k],
                              b.a_rcol[:, 8 + k:9 + k], ALU.subtract, ALU.mult)
            dve.tensor_tensor(ya[:, k, :], yb[:], xt[:, k, :], ALU.add)
            xt_free[k] = ev.n
            sp.wait_ge(ev.h, ev.n)
            sp.dma_start(out=io["out"][ti * 128:(ti + 1) * 128, :],
                         in_=ya[:, k, :]).then_inc(os_.h, 16)
            os_.n += 16
            ya_free[k] = os_.n

    sp.wait_ge(b.out.h, b.out.n)
    sp.wait_ge(b.out2.h, b.out2.n)


def hostprep_d(inputs):
    return dict(wT_d=np.ascontiguousarray(np.asarray(inputs["L1_ffn_weight"]).T),
                ident=np.eye(128, dtype=BF))


D_IO = [("wT_d", [D, D], F32), ("ident", [128, 128], BF16)]


def build_d_program():
    b = B()
    io = {n: b.dram(n, s, d, "ExternalInput") for n, s, d in D_IO}
    io["x3"] = b.dram("x3", [T, D], F32, "ExternalInput")
    io["out"] = b.dram("out", [T, D], F32, "ExternalOutput")
    with b.ctx:
        blk = b.ctx.enter_context(b.nc.Block())

        @blk.gpsimd
        def _(g):
            build_stage_d(b, io)
    return b.nc


_PROG_CACHE = {}


def _get_prog(name, builder):
    if name not in _PROG_CACHE:
        _PROG_CACHE[name] = builder()
    return _PROG_CACHE[name]


def kernel(**inputs):
    return kernel_fused(**inputs)


def kernel_unfused(**inputs):
    x = np.asarray(inputs["L0_attn_x"])
    ncores = 8
    hpa = hostprep_a(inputs)
    hpb = hostprep_b(inputs)
    hpc = hostprep_c(inputs)
    hpd = hostprep_d(inputs)

    nc_a = _get_prog("a", build_a_program)
    in_maps = [dict(hpa, x0=np.ascontiguousarray(x[c])) for c in range(ncores)]
    res = run_bass_kernel_spmd(nc_a, in_maps, core_ids=list(range(ncores)))
    x1 = [np.asarray(res.results[c]["x1"]) for c in range(ncores)]

    nc_b = _get_prog("b", build_b_program)
    in_maps = [dict(hpb, x1=x1[c]) for c in range(ncores)]
    res = run_bass_kernel_spmd(nc_b, in_maps, core_ids=list(range(ncores)))
    x2 = [np.asarray(res.results[c]["x2"]) for c in range(ncores)]

    nc_c = _get_prog("c", build_c_program)
    in_maps = [dict(hpc, x2=x2[c]) for c in range(ncores)]
    res = run_bass_kernel_spmd(nc_c, in_maps, core_ids=list(range(ncores)))
    x3 = [np.asarray(res.results[c]["x3"]) for c in range(ncores)]

    nc_d = _get_prog("d", build_d_program)
    in_maps = [dict(hpd, x3=x3[c]) for c in range(ncores)]
    res = run_bass_kernel_spmd(nc_d, in_maps, core_ids=list(range(ncores)))
    out = np.stack([np.asarray(res.results[c]["out"]) for c in range(ncores)])
    return out



def _stage_barrier(b):
    sems = [b.mm, b.ea, b.ev, b.ep, b.out, b.out2, b.cdma, b.pdma,
            b.xdma, b.xdma2, b.wd[0], b.wd[1], b.hd[0], b.hd[1]]
    engs = [b.pe, b.nc.scalar, b.nc.vector, b.nc.gpsimd, b.sp]
    for e in engs:
        for sm in sems:
            if sm.n:
                e.wait_ge(sm.h, sm.n)


def build_fused_program(n_cores=8):
    b = B()
    io = {}
    for n, sh, dt in A_IO + B_IO + C_IO + D_IO:
        if n not in io:
            io[n] = b.dram(n, sh, dt, "ExternalInput")
    io["out"] = b.dram("out", [T, D], F32, "ExternalOutput")
    for n in ["x1", "x2", "x3"]:
        io[n] = b.dram(n, [T, D], F32, "Internal")
    io["qT_h"] = b.dram("qT_h", [D, T], BF16, "Internal")
    io["kT_h"] = b.dram("kT_h", [D, T], BF16, "Internal")
    io["v_h"] = b.dram("v_h", [T, D], BF16, "Internal")
    io["cc_in"] = b.dram("cc_in", [1, 1], F32, "Internal")
    io["cc_out"] = b.dram("cc_out", [1, 1], F32, "Internal")
    with b.ctx:
        blk = b.ctx.enter_context(b.nc.Block())

        @blk.gpsimd
        def _(g):
            for fn in [build_stage_a, build_stage_b,
                       lambda bb, ii: build_stage_c(bb, ii, n_cores=n_cores),
                       build_stage_d]:
                with ExitStack() as sc:
                    b.cur = sc
                    fn(b, io)
                b.cur = None
                _stage_barrier(b)
    return b.nc


def kernel_fused(**inputs):
    x = np.asarray(inputs["L0_attn_x"])
    ncores = 8
    hp = {}
    hp.update(hostprep_a(inputs))
    hp.update(hostprep_b(inputs))
    hp.update(hostprep_c(inputs))
    hp.update(hostprep_d(inputs))
    nc = _get_prog("fused", build_fused_program)
    in_maps = [dict(hp, x0=np.ascontiguousarray(x[c])) for c in range(ncores)]
    res = run_bass_kernel_spmd(nc, in_maps, core_ids=list(range(ncores)))
    return np.stack([np.asarray(res.results[c]["out"]) for c in range(ncores)])


# version 09
# speedup vs baseline: 1.0454x; 1.0196x over previous
import math
from contextlib import ExitStack

import numpy as np
import ml_dtypes

import concourse.bass as bass
import concourse.mybir as mybir
from concourse.bass_utils import run_bass_kernel_spmd

F32 = mybir.dt.float32
BF16 = mybir.dt.bfloat16
AF = mybir.ActivationFunctionType
ALU = mybir.AluOpType
AX = mybir.AxisListType
BF = ml_dtypes.bfloat16

A_MAXPH = 6
A_P5MODE = 4
A_NCH = None
A_SKIP_P2 = False
A_SKIP_P4 = False
A_SKIP_P1 = False
T = 4096
D = 1024
NH = 16
HD = 64
NE = 16
CH = 1024
NCH = T // CH
NT = CH // 128
EPS = 1e-5
BIG = 1.0e4


class Sem:
    def __init__(self, nc, name):
        self.h = nc.alloc_semaphore(name)
        self.n = 0

    def inc(self, inst, k=1):
        inst.then_inc(self.h, k)
        self.n += k
        return self.n


_WRAP = {
    "activation", "sign", "square", "sqrt", "mul", "copy", "add", "memzero",
    "memset", "tensor_copy", "tensor_tensor", "tensor_add", "tensor_sub",
    "tensor_mul", "tensor_max", "tensor_scalar", "tensor_single_scalar",
    "tensor_scalar_add", "tensor_scalar_sub", "tensor_scalar_mul",
    "tensor_scalar_max", "tensor_scalar_min", "tensor_relu", "tensor_reduce",
    "reduce_max", "reduce_sum", "scalar_tensor_tensor", "tensor_tensor_scan",
    "select", "copy_predicated", "transpose", "reciprocal", "iota",
    "affine_select",
}


class SEng:

    def __init__(self, eng, sem):
        self._e = eng
        self._s = sem

    def __getattr__(self, name):
        attr = getattr(self._e, name)
        if name in _WRAP:
            def f(*a, **k):
                self._e.wait_ge(self._s.h, self._s.n)
                inst = attr(*a, **k)
                self._s.inc(inst)
                return inst
            return f
        return attr


class B:

    def __init__(self):
        nc = bass.Bass("TRN2", target_bir_lowering=False)
        self.nc = nc
        self.ctx = ExitStack()
        self.cur = None
        self.pe = nc.tensor
        self.sp = nc.sync
        self.dma = Sem(nc, "dma")
        self.mm = Sem(nc, "mm")
        self.ea = Sem(nc, "ea")
        self.ev = Sem(nc, "ev")
        self.ep = Sem(nc, "ep")
        self.act = SEng(nc.scalar, self.ea)
        self.dve = SEng(nc.vector, self.ev)
        self.pool = SEng(nc.gpsimd, self.ep)
        self.out = Sem(nc, "outs")
        self.pdma = Sem(nc, "pdma")
        self.cdma = Sem(nc, "cdma")
        self.xdma = Sem(nc, "xdma")
        self.xdma2 = Sem(nc, "xdma2")
        self.wd = [Sem(nc, "wd0"), Sem(nc, "wd1")]
        self.hd = [Sem(nc, "hd0"), Sem(nc, "hd1")]
        self.out2 = Sem(nc, "outs2")

    def dram(self, name, shape, dtype, kind):
        return self.nc.dram_tensor(name, list(shape), dtype, kind=kind).ap()

    def sb(self, name, shape, dtype):
        return (self.cur or self.ctx).enter_context(
            self.nc.sbuf_tensor(name, list(shape), dtype))

    def ps(self, name, shape, dtype=F32):
        return (self.cur or self.ctx).enter_context(
            self.nc.psum_tensor(name, list(shape), dtype))


def _ln_stats(b, src, col):
    b.dve.tensor_reduce(b.a_s1[:, col:col + 1], src, axis=AX.X, op=ALU.add)
    b.act.activation(b.a_sqscr[:], src, AF.Square, accum_out=b.a_ssq[:, col:col + 1])


def _ln_mr(b, cols):
    dve, act, ea, ev = b.dve, b.act, b.ea, b.ev
    s = cols
    dve.tensor_scalar_mul(b.a_mcol[:, s], b.a_s1[:, s], 1.0 / D)
    dve.tensor_scalar_mul(b.a_vcol[:, s], b.a_ssq[:, s], 1.0 / D)
    dve.tensor_tensor(b.a_rcol[:, s], b.a_mcol[:, s], b.a_mcol[:, s], ALU.mult)
    dve.tensor_tensor(b.a_vcol[:, s], b.a_vcol[:, s], b.a_rcol[:, s], ALU.subtract)
    act.wait_ge(ev.h, ev.n)
    act.activation(b.a_rcol[:, s], b.a_vcol[:, s], AF.Sqrt, bias=b.a_ecol[:])
    dve.wait_ge(ea.h, ea.n)
    dve.reciprocal(b.a_rcol[:, s], b.a_rcol[:, s])


def build_stage_a(b, io):
    CH = 512
    NCH = A_NCH if A_NCH else T // CH
    NT = CH // 128
    nc = b.nc
    pe, act, dve, pool, sp = b.pe, b.act, b.dve, b.pool, b.sp
    dma, mm, ea, ev, ep = b.dma, b.mm, b.ea, b.ev, b.ep

    xa = b.sb("a_x", [128, NT, D], F32)
    xn = b.sb("a_xn", [128, NT, D], BF16)
    xnT = b.sb("a_xnT", [128, 8, CH], BF16)
    qf = b.sb("a_qf", [128, 8, CH], BF16)
    kf = b.sb("a_kf", [128, 8, 128 + CH], BF16)
    vt = b.sb("a_vt", [128, 1 + NT, D], BF16)
    msgT = b.sb("a_msgT", [128, 8, CH], BF16)
    tau = b.sb("a_tau", [128, 2048], BF16)
    wq_sb = b.sb("a_wq", [128, 8, D], BF16)
    wk_sb = b.sb("a_wk", [128, 8, D], BF16)
    wv_sb = b.sb("a_wv", [128, 8, D], BF16)
    wo_sb = b.sb("a_wo", [128, 8, D], BF16)
    mask_sb = b.sb("a_mask", [128, 2048], BF16)
    id_sb = b.sb("a_id", [128, 128], BF16)
    grow = b.sb("a_grow", [1, D], F32)
    brow = b.sb("a_brow", [1, D], F32)
    ones1 = b.sb("a_ones1", [1, 128], F32)
    gbc = b.sb("a_gbc", [128, D], F32)
    bbc = b.sb("a_bbc", [128, D], F32)
    b.a_s1 = b.sb("a_s1", [128, 16], F32)
    b.a_ssq = b.sb("a_ssq", [128, 16], F32)
    b.a_mcol = b.sb("a_mcol", [128, 16], F32)
    b.a_vcol = b.sb("a_vcol", [128, 16], F32)
    b.a_rcol = b.sb("a_rcol", [128, 16], F32)
    b.a_sqscr = b.sb("a_sqscr", [128, D], BF16)
    b.a_ecol = b.sb("a_ecol", [128, 1], F32)
    ya = b.sb("a_ya", [128, D], F32)
    yb = b.sb("a_yb", [128, D], F32)

    pp = b.ps("a_pp", [128, D])
    pp2 = b.ps("a_pp2", [128, D])
    psc = b.ps("a_psc", [128, 2048])

    pool.memset(ones1[:], 1.0)
    pool.memset(b.a_ecol[:], EPS)
    pool.memset(kf[:], 0.0)
    pool.memset(vt[:], 0.0)
    pn = ep.n
    for dst, src in [(id_sb, io["ident"]), (mask_sb, io["maskAB"])]:
        sp.dma_start(out=dst[:], in_=src).then_inc(b.cdma.h, 16)
        b.cdma.n += 16
    sp.dma_start(out=grow[:], in_=io["gamma"].unsqueeze(0)).then_inc(b.cdma.h, 16)
    sp.dma_start(out=brow[:], in_=io["beta"].unsqueeze(0)).then_inc(b.cdma.h, 16)
    b.cdma.n += 32
    for wsb, wdr in [(wq_sb, io["wqT"]), (wk_sb, io["wkT"]), (wv_sb, io["wvT"]),
                     (wo_sb, io["woT"])]:
        pool.dma_start(out=wsb[:], in_=wdr.rearrange("(kc p) n -> p kc n", p=128)
                       ).then_inc(b.pdma.h, 16)
        b.pdma.n += 16
    w_loaded = b.pdma.n

    pe.wait_ge(b.cdma.h, b.cdma.n)
    pe.wait_ge(b.pdma.h, w_loaded)
    pe.wait_ge(ep.h, pn)
    for half in range(2):
        sl = slice(half * 512, (half + 1) * 512)
        pe.matmul(pp[:, sl], ones1[:], grow[:, sl], start=True, stop=True)
        last = pe.matmul(pp2[:, sl], ones1[:], brow[:, sl], start=True, stop=True)
    gb_mm = mm.inc(last)
    dve.wait_ge(mm.h, gb_mm)
    dve.wait_ge(ep.h, pn)
    dve.tensor_copy(gbc[:], pp[:])
    dve.tensor_copy(bbc[:], pp2[:])
    gb_ev = ev.n

    dve.wait_ge(b.cdma.h, b.cdma.n)
    act.wait_ge(b.cdma.h, b.cdma.n)
    free_pp = gb_ev
    free_pp2 = gb_ev
    free_psc_ea = 0
    free_tau_mm = 0

    for c in range(NCH):
        sp.wait_ge(b.out.h, b.out.n)
        sp.dma_start(out=xa[:], in_=io["x0"][c * CH:(c + 1) * CH, :]
                     .rearrange("(i p) d -> p i d", p=128)).then_inc(b.xdma.h, 16)
        b.xdma.n += 16
        x_loaded = b.xdma.n

        if A_MAXPH < 1:
            sp.wait_ge(b.xdma.h, b.xdma.n)
            sp.dma_start(out=io["x1"][c * CH:(c + 1) * CH, :]
                         .rearrange("(i p) d -> p i d", p=128), in_=xa[:]).then_inc(b.out.h, 16)
            b.out.n += 16
            continue
        dve.wait_ge(b.xdma.h, x_loaded)
        act.wait_ge(b.xdma.h, x_loaded)
        if A_SKIP_P1:
            pool.memset(xn[:], 0.01)
            dve.wait_ge(ep.h, ep.n)
        el_dummy = 0
        if not A_SKIP_P1:
            for i in range(NT):
                _ln_stats(b, xa[:, i, :], i)
            dve.wait_ge(ea.h, ea.n)
            _ln_mr(b, slice(0, NT))
            for i in range(NT):
                dve.tensor_scalar(xn[:, i, :], xa[:, i, :], b.a_mcol[:, i:i + 1],
                                  b.a_rcol[:, i:i + 1], ALU.subtract, ALU.mult)
        ev_xn = ev.n

        if A_MAXPH < 2:
            sp.wait_ge(ev.h, ev.n)
            sp.dma_start(out=io["x1"][c * CH:(c + 1) * CH, :]
                         .rearrange("(i p) d -> p i d", p=128), in_=xa[:]).then_inc(b.out.h, 16)
            b.out.n += 16
            continue
        if A_SKIP_P2:
            pool.memset(xnT[:], 0.01)
            dve.wait_ge(ep.h, ep.n)
            dve.tensor_copy(b.a_s1[:, 15:16], b.a_s1[:, 15:16])
        else:
            ptv = [pp[:, 0:64].bitcast(BF16), pp2[:, 0:64].bitcast(BF16)]
            free_pt = [free_pp, free_pp2]
            pe.wait_ge(ev.h, ev_xn)
            for u in range(NT * 8):
                i, fc = divmod(u, 8)
                k = u % 2
                pe.wait_ge(ev.h, free_pt[k])
                t_mm = mm.inc(pe.transpose(ptv[k], xn[:, i, fc * 128:(fc + 1) * 128], id_sb[:]))
                dve.wait_ge(mm.h, t_mm)
                dve.tensor_copy(xnT[:, fc, i * 128:(i + 1) * 128], ptv[k])
                free_pt[k] = ev.n
            free_pp, free_pp2 = free_pt

        if c > 0:
            dve.tensor_copy(kf[:, :, 0:128], kf[:, :, CH:CH + 128])
            dve.tensor_copy(vt[:, 0, :], vt[:, NT, :])
        else:
            dve.wait_ge(ep.h, pn)
        ev_carry = ev.n
        pe.wait_ge(ev.h, ev_carry)

        if A_MAXPH < 4:
            sp.wait_ge(ev.h, ev.n)
            sp.dma_start(out=io["x1"][c * CH:(c + 1) * CH, :]
                         .rearrange("(i p) d -> p i d", p=128), in_=xa[:]).then_inc(b.out.h, 16)
            b.out.n += 16
            continue
        if A_SKIP_P4:
            pool.memset(qf[:], 0.01)
            pool.memset(kf[:], 0.01)
            pool.memset(vt[:], 0.01)
            pe.wait_ge(ep.h, ep.n)
        pbuf = [pp, pp2]
        pfree = [free_pp, free_pp2]
        u = 0
        for wsb, kind in ([] if A_SKIP_P4 else [(wq_sb, "q"), (wk_sb, "k"), (wv_sb, "v")]):
            nm = NT if kind == "v" else 8
            for m in range(nm):
                k = u % 2
                u += 1
                ppu = pbuf[k]
                pe.wait_ge(ev.h, pfree[k])
                if kind == "v":
                    for kc in range(8):
                        for half in range(2):
                            sl = slice(half * 512, (half + 1) * 512)
                            last = pe.matmul(ppu[:, sl], xnT[:, kc, m * 128:(m + 1) * 128],
                                             wv_sb[:, kc, sl], start=(kc == 0), stop=(kc == 7))
                else:
                    for kc in range(8):
                        last = pe.matmul(ppu[:, 0:CH], wsb[:, kc, m * 128:(m + 1) * 128],
                                         xnT[:, kc, :], start=(kc == 0), stop=(kc == 7))
                m_mm = mm.inc(last)
                dve.wait_ge(mm.h, m_mm)
                if kind == "q":
                    dve.tensor_copy(qf[:, m, :], ppu[:, 0:CH])
                elif kind == "k":
                    dve.tensor_copy(kf[:, m, 128:128 + CH], ppu[:, 0:CH])
                else:
                    dve.tensor_copy(vt[:, 1 + m, :], ppu[:])
                pfree[k] = ev.n
        free_pp, free_pp2 = pfree
        pe.wait_ge(ev.h, ev.n)

        if A_MAXPH < 5:
            sp.wait_ge(ev.h, ev.n)
            sp.dma_start(out=io["x1"][c * CH:(c + 1) * CH, :]
                         .rearrange("(i p) d -> p i d", p=128), in_=xa[:]).then_inc(b.out.h, 16)
            b.out.n += 16
            continue
        for qt in range(NT):
            for half in range(2):
                if A_P5MODE == 0:
                    pe.wait_ge(ev.h, ev.n)
                else:
                    pe.wait_ge(ea.h, free_psc_ea)
                for hh in [0, 2, 4, 6, 1, 3, 5, 7]:
                    h = half * 8 + hh
                    fc, r0 = h // 2, 64 * (h % 2)
                    for piece in range(2):
                        kb = qt + piece
                        c0 = hh * 256 + piece * 128
                        last = pe.matmul(psc[:, c0:c0 + 128],
                                         kf[r0:r0 + 64, fc, kb * 128:(kb + 1) * 128],
                                         qf[r0:r0 + 64, fc, qt * 128:(qt + 1) * 128],
                                         start=True, stop=True)
                sc_mm = mm.inc(last)
                act.wait_ge(mm.h, sc_mm)
                act.wait_ge(mm.h, free_tau_mm)
                if A_P5MODE == 0:
                    dve.tensor_copy(tau[:], psc[:])
                    free_psc_ea = 0
                    continue
                act.activation(tau[:], psc[:], AF.Sigmoid, scale=0.125)
                free_psc_ea = ea.n
                if A_P5MODE < 2:
                    continue
                dve.wait_ge(ea.h, free_psc_ea)
                dve.tensor_tensor(tau[:], tau[:], mask_sb[:], ALU.mult)
                ev_tau = ev.n
                if A_P5MODE < 3:
                    free_tau_mm = mm.n
                    continue
                pe.wait_ge(ev.h, ev_tau)
                pe.wait_ge(ev.h, free_pp)
                for hh in range(8):
                    h = half * 8 + hh
                    for piece in range(2):
                        kb = qt + piece
                        last = pe.matmul(pp[0:64, hh * 128:(hh + 1) * 128],
                                         vt[:, kb, h * 64:(h + 1) * 64],
                                         tau[:, hh * 256 + piece * 128:hh * 256 + piece * 128 + 128],
                                         start=(piece == 0), stop=(piece == 1))
                free_tau_mm = mm.inc(last)
                if A_P5MODE < 4:
                    continue
                dve.wait_ge(mm.h, free_tau_mm)
                for hh in range(8):
                    h = half * 8 + hh
                    fc, r0 = h // 2, 64 * (h % 2)
                    dve.tensor_copy(msgT[r0:r0 + 64, fc, qt * 128:(qt + 1) * 128],
                                    pp[0:64, hh * 128:(hh + 1) * 128])
                free_pp = ev.n

        if A_MAXPH < 6:
            sp.wait_ge(ev.h, ev.n)
            sp.dma_start(out=io["x1"][c * CH:(c + 1) * CH, :]
                         .rearrange("(i p) d -> p i d", p=128), in_=xa[:]).then_inc(b.out.h, 16)
            b.out.n += 16
            continue
        ev_msg = ev.n
        pe.wait_ge(ev.h, ev_msg)
        for i in range(NT):
            pe.wait_ge(ev.h, free_pp2)
            for kc in range(8):
                for half in range(2):
                    sl = slice(half * 512, (half + 1) * 512)
                    last = pe.matmul(pp2[:, sl], msgT[:, kc, i * 128:(i + 1) * 128],
                                     wo_sb[:, kc, sl], start=(kc == 0), stop=(kc == 7))
            o_mm = mm.inc(last)
            dve.wait_ge(mm.h, o_mm)
            dve.scalar_tensor_tensor(ya[:], pp2[:], 1.0, xn[:, i, :], ALU.mult, ALU.add)
            free_pp2 = ev.n
            act.wait_ge(ev.h, free_pp2)
            _ln_stats(b, ya[:], 8)
            dve.wait_ge(ea.h, ea.n)
            _ln_mr(b, slice(8, 9))
            dve.tensor_scalar(yb[:], ya[:], b.a_mcol[:, 8:9], b.a_rcol[:, 8:9],
                              ALU.subtract, ALU.mult)
            dve.scalar_tensor_tensor(yb[:], yb[:], 1.0, gbc[:], ALU.mult, ALU.mult)
            dve.tensor_tensor(ya[:], yb[:], bbc[:], ALU.add)
            ev_ao = ev.n
            act.wait_ge(ev.h, ev_ao)
            _ln_stats(b, ya[:], 9)
            dve.wait_ge(ea.h, ea.n)
            _ln_mr(b, slice(9, 10))
            dve.tensor_scalar(yb[:], ya[:], b.a_mcol[:, 9:10], b.a_rcol[:, 9:10],
                              ALU.subtract, ALU.mult)
            dve.tensor_tensor(xa[:, i, :], xa[:, i, :], yb[:], ALU.add)

        sp.wait_ge(ev.h, ev.n)
        sp.dma_start(out=io["x1"][c * CH:(c + 1) * CH, :]
                     .rearrange("(i p) d -> p i d", p=128), in_=xa[:]).then_inc(b.out.h, 16)
        b.out.n += 16

    sp.wait_ge(b.out.h, b.out.n)


def hostprep_a(inputs):
    wq = np.ascontiguousarray(np.asarray(inputs["L0_attn_wq"]).T)
    wkv = np.asarray(inputs["L0_attn_wkv"]).reshape(NH, 2 * HD, D)
    wkT = np.ascontiguousarray(wkv[:, :HD].reshape(D, D).T)
    wvT = np.ascontiguousarray(wkv[:, HD:].reshape(D, D).T)
    woT = np.ascontiguousarray(np.asarray(inputs["L0_attn_wo"]).T)
    ki = np.arange(128)[:, None]
    qi = np.arange(128)[None, :]
    mA = ((ki - qi) >= 112).astype(BF)
    mB = (((ki - qi) >= -16) & ((ki - qi) <= -1)).astype(BF)
    maskAB = np.ascontiguousarray(np.concatenate([np.concatenate([mA, mB], 1)] * 8, 1))
    return dict(wqT=wq, wkT=wkT, wvT=wvT, woT=woT,
                gamma=np.asarray(inputs["L0_attn_gamma"]),
                beta=np.asarray(inputs["L0_attn_beta"]),
                maskAB=maskAB, ident=np.eye(128, dtype=BF))


A_IO = [("x0", [T, D], F32), ("wqT", [D, D], F32), ("wkT", [D, D], F32),
        ("wvT", [D, D], F32), ("woT", [D, D], F32), ("gamma", [D], F32),
        ("beta", [D], F32), ("maskAB", [128, 2048], BF16), ("ident", [128, 128], BF16)]


def build_a_program():
    b = B()
    io = {n: b.dram(n, s, d, "ExternalInput") for n, s, d in A_IO}
    io["x1"] = b.dram("x1", [T, D], F32, "ExternalOutput")
    with b.ctx:
        blk = b.ctx.enter_context(b.nc.Block())

        @blk.gpsimd
        def _(g):
            build_stage_a(b, io)
    return b.nc




def _free_as_ev(b, fr):
    sm, cnt = fr
    if sm is b.ev:
        return cnt
    b.dve.wait_ge(sm.h, cnt)
    return b.ev.n


def build_stage_b(b, io):
    CH = 1024
    NCH = T // CH
    NT = CH // 128
    nc = b.nc
    pe, act, dve, pool, sp = b.pe, b.act, b.dve, b.pool, b.sp
    dma, mm, ea, ev, ep = b.dma, b.mm, b.ea, b.ev, b.ep

    xt = b.sb("b_xt", [128, 2, D], F32)
    xn = b.sb("b_xn", [128, NT, D], BF16)
    xnT = b.sb("b_xnT", [128, 8, CH], BF16)
    acc = b.sb("b_acc", [128, NT, D], F32)
    h_sb = b.sb("b_h", [128, 8, CH], BF16)
    h1s = b.sb("b_h1s", [128, CH], BF16)
    wsl = [[b.sb(f"b_w{m}_{p}", [128, 8, D], BF16) for m in range(3)] for p in range(2)]
    gw_sb = b.sb("b_gw", [128, 8, 16], BF16)
    logits = b.sb("b_logits", [128, NT, 16], F32)
    g_all = b.sb("b_g", [128, NT, 16], F32)
    bwork = b.sb("b_bwork", [128, 16], F32)
    selm = b.sb("b_selm", [128, 16], F32)
    t1 = b.sb("b_t1", [128, 16], F32)
    t2 = b.sb("b_t2", [128, 16], F32)
    lmrow = b.sb("b_lm", [128, 16], F32)
    den = b.sb("b_den", [128, 1], F32)
    ebias_bc = b.sb("b_ebias", [128, 16], F32)
    iota_bc = b.sb("b_iota", [128, 16], F32)
    iotamB_bc = b.sb("b_iotamB", [128, 16], F32)
    gones = b.sb("b_gones", [128, 1], F32)
    ebrow = b.sb("b_ebrow", [1, 16], F32)
    ones1 = b.sb("b_ones1", [1, 128], F32)
    id_sb = b.sb("b_id", [128, 128], BF16)
    b.a_s1 = b.sb("b_s1", [128, 16], F32)
    b.a_ssq = b.sb("b_ssq", [128, 16], F32)
    b.a_mcol = b.sb("b_mcol", [128, 16], F32)
    b.a_vcol = b.sb("b_vcol", [128, 16], F32)
    b.a_rcol = b.sb("b_rcol", [128, 16], F32)
    b.a_sqscr = b.sb("b_sqscr", [128, D], BF16)
    b.a_ecol = b.sb("b_ecol", [128, 1], F32)
    ya = b.sb("b_ya", [128, 2, D], F32)

    pp = b.ps("b_pp", [128, D])
    pp2 = b.ps("b_pp2", [128, D])
    pp3 = b.ps("b_pp3", [128, D])
    pr = b.ps("b_pr", [128, 16])
    pr2 = b.ps("b_pr2", [128, 16])

    pool.memset(ones1[:], 1.0)
    pool.memset(b.a_ecol[:], EPS)
    pool.memset(gones[:], 1.0)
    pn = ep.n
    for dst, src in [(id_sb[:], io["ident"]), (iota_bc[:], io["iota16"]),
                     (iotamB_bc[:], io["iotamB16"]),
                     (ebrow[:], io["expert_bias"].unsqueeze(0))]:
        sp.dma_start(out=dst, in_=src).then_inc(b.cdma.h, 16)
        b.cdma.n += 16
    pool.dma_start(out=gw_sb[:], in_=io["gate_wT"].rearrange("(kc p) n -> p kc n", p=128)
                   ).then_inc(b.pdma.h, 16)
    b.pdma.n += 16
    gw_loaded = b.pdma.n
    pe.wait_ge(b.cdma.h, b.cdma.n)
    pe.wait_ge(ep.h, pn)
    last = pe.matmul(pr[:], ones1[:], ebrow[:], start=True, stop=True)
    eb_mm = mm.inc(last)
    dve.wait_ge(mm.h, eb_mm)
    dve.tensor_copy(ebias_bc[:], pr[:])
    dve.wait_ge(b.cdma.h, b.cdma.n)
    ev_const = ev.n

    worder = [("s", None)] + [("e", e) for e in range(NE)]
    loaded = {}
    compute_done = {}

    def issue_weight_load(idx):
        kind, e = worder[idx]
        par = idx % 2
        ws = b.wd[par]
        if idx >= 2:
            pool.wait_ge(mm.h, compute_done[idx - 2])
        for j, nm in enumerate(["w1", "w3", "w2"]):
            src = io["s" + nm + "T"] if kind == "s" else io[nm + "T"][e]
            pool.dma_start(out=wsl[par][j][:],
                           in_=src.rearrange("(kc p) n -> p kc n", p=128)
                           ).then_inc(ws.h, 16)
            ws.n += 16
        loaded[idx] = ws.n

    free_pp = ev_const
    free_pp2 = ev_const
    free_pr_ev = [ev.n, ev.n]

    for c in range(NCH):
        pool.wait_ge(ev.h, ev.n)
        pool.wait_ge(ea.h, ea.n)
        pool.memset(acc[:], 0.0)
        acc_zero_ep = ep.n

        xt_free = [b.ev.n, b.ev.n]
        ptv = [pp[:, 0:64].bitcast(BF16), pp2[:, 0:64].bitcast(BF16)]
        free_pt = [free_pp, free_pp2]
        for i in range(NT):
            k = i % 2
            xs = b.xdma if k == 0 else b.xdma2
            sp.wait_ge(ev.h, xt_free[k])
            sp.dma_start(out=xt[:, k, :], in_=io["x1"][c * CH + i * 128:c * CH + (i + 1) * 128, :]
                         ).then_inc(xs.h, 16)
            xs.n += 16
            dve.wait_ge(xs.h, xs.n)
            act.wait_ge(xs.h, xs.n)
            _ln_stats(b, xt[:, k, :], i % 8)
            dve.wait_ge(ea.h, ea.n)
            _ln_mr(b, slice(i % 8, i % 8 + 1))
            dve.tensor_scalar(xn[:, i, :], xt[:, k, :], b.a_mcol[:, i % 8:i % 8 + 1],
                              b.a_rcol[:, i % 8:i % 8 + 1], ALU.subtract, ALU.mult)
            xt_free[k] = ev.n
            ev_xn_i = ev.n
            for fc in range(8):
                kk = (i * 8 + fc) % 2
                pe.wait_ge(ev.h, max(free_pt[kk], ev_xn_i))
                t_mm = mm.inc(pe.transpose(ptv[kk], xn[:, i, fc * 128:(fc + 1) * 128],
                                           id_sb[:]))
                dve.wait_ge(mm.h, t_mm)
                dve.tensor_copy(xnT[:, fc, i * 128:(i + 1) * 128], ptv[kk])
                free_pt[kk] = ev.n
        free_pp, free_pp2 = free_pt

        issue_weight_load(0)
        issue_weight_load(1)

        pe.wait_ge(ev.h, ev.n)
        pe.wait_ge(b.pdma.h, gw_loaded)
        for i in range(NT):
            pru = pr if i % 2 == 0 else pr2
            pe.wait_ge(ev.h, free_pr_ev[i % 2])
            for kc in range(8):
                last = pe.matmul(pru[:], xnT[:, kc, i * 128:(i + 1) * 128],
                                 gw_sb[:, kc, :], start=(kc == 0), stop=(kc == 7))
            r_mm = mm.inc(last)
            dve.wait_ge(mm.h, r_mm)
            dve.tensor_copy(logits[:, i, :], pru[:])
            free_pr_ev[i % 2] = ev.n
            dve.tensor_tensor(bwork[:], logits[:, i, :], ebias_bc[:], ALU.add)
            dve.memset(selm[:], 0.0)
            for kk in range(4):
                dve.tensor_reduce(den[:], bwork[:], axis=AX.X, op=ALU.max)
                dve.tensor_scalar(t1[:], bwork[:], den[:], None, ALU.is_equal)
                dve.tensor_tensor(t2[:], t1[:], iotamB_bc[:], ALU.mult)
                dve.tensor_scalar_add(t2[:], t2[:], BIG)
                dve.tensor_reduce(den[:], t2[:], axis=AX.X, op=ALU.min)
                dve.tensor_scalar(t1[:], iota_bc[:], den[:], None, ALU.is_equal)
                dve.tensor_tensor(selm[:], selm[:], t1[:], ALU.add)
                dve.scalar_tensor_tensor(bwork[:], t1[:], -BIG, bwork[:], ALU.mult, ALU.add)
            dve.tensor_scalar_add(t1[:], logits[:, i, :], -BIG)
            dve.scalar_tensor_tensor(lmrow[:], selm[:], BIG, t1[:], ALU.mult, ALU.add)
            dve.tensor_reduce(den[:], lmrow[:], axis=AX.X, op=ALU.max)
            dve.tensor_scalar_mul(den[:], den[:], -1.0)
            act.wait_ge(ev.h, ev.n)
            act.activation(g_all[:, i, :], lmrow[:], AF.Exp, bias=den[:],
                           accum_out=b.a_vcol[:, 15:16])
            dve.wait_ge(ea.h, ea.n)
            dve.reciprocal(den[:], b.a_vcol[:, 15:16])
            dve.tensor_scalar(g_all[:, i, :], g_all[:, i, :], den[:], None, ALU.mult)

        dve.wait_ge(ep.h, acc_zero_ep)
        R = [pp, pp2, pp3]
        buf_free = [(ev, free_pp), (ev, free_pp2), (ev, ev.n)]
        for idx in range(len(worder)):
            kind, e = worder[idx]
            par = idx % 2
            pe.wait_ge(b.wd[par].h, loaded[idx])
            w1s, w3s, w2s = wsl[par]
            for m in range(8):
                ja = (2 * m) % 3
                jb = (2 * m + 1) % 3
                sm, cnt = buf_free[ja]
                pe.wait_ge(sm.h, cnt)
                for kc in range(8):
                    for half in range(2):
                        sl = slice(half * 512, (half + 1) * 512)
                        last = pe.matmul(R[ja][:, sl], w1s[:, kc, m * 128:(m + 1) * 128],
                                         xnT[:, kc, sl], start=(kc == 0), stop=(kc == 7))
                h1_mm = mm.inc(last)
                sm, cnt = buf_free[jb]
                pe.wait_ge(sm.h, cnt)
                for kc in range(8):
                    for half in range(2):
                        sl = slice(half * 512, (half + 1) * 512)
                        last = pe.matmul(R[jb][:, sl], w3s[:, kc, m * 128:(m + 1) * 128],
                                         xnT[:, kc, sl], start=(kc == 0), stop=(kc == 7))
                h3_mm = mm.inc(last)
                act.wait_ge(mm.h, h1_mm)
                act.wait_ge(ev.h, ev.n)
                act.activation(h1s[:], R[ja][:], AF.Silu)
                buf_free[ja] = (ea, ea.n)
                dve.wait_ge(mm.h, h3_mm)
                dve.wait_ge(ea.h, ea.n)
                dve.scalar_tensor_tensor(h_sb[:, m, :], R[jb][:], 1.0, h1s[:],
                                         ALU.mult, ALU.mult)
                buf_free[jb] = (ev, ev.n)
            ev_h = ev.n
            pe.wait_ge(ev.h, ev_h)
            for i in range(NT):
                k = i % 2
                sm, cnt = buf_free[k]
                pe.wait_ge(sm.h, cnt)
                for ic in range(8):
                    for half in range(2):
                        sl = slice(half * 512, (half + 1) * 512)
                        last = pe.matmul(R[k][:, sl], h_sb[:, ic, i * 128:(i + 1) * 128],
                                         w2s[:, ic, sl], start=(ic == 0), stop=(ic == 7))
                d_mm = mm.inc(last)
                dve.wait_ge(mm.h, d_mm)
                gcol = gones[:, 0:1] if kind == "s" else g_all[:, i, e:e + 1]
                dve.scalar_tensor_tensor(acc[:, i, :], R[k][:], gcol, acc[:, i, :],
                                         ALU.mult, ALU.add)
                buf_free[k] = (ev, ev.n)
            compute_done[idx] = mm.n
            if idx + 2 < len(worder):
                issue_weight_load(idx + 2)
        free_pp = _free_as_ev(b, buf_free[0])
        free_pp2 = _free_as_ev(b, buf_free[1])

        act.wait_ge(ev.h, ev.n)
        xt_free = [b.ev.n, b.ev.n]
        ya_free = [b.out.n, b.out2.n]
        for i in range(NT):
            k = i % 2
            xs = b.xdma if k == 0 else b.xdma2
            sp.wait_ge(ev.h, xt_free[k])
            sp.dma_start(out=xt[:, k, :], in_=io["x1"][c * CH + i * 128:c * CH + (i + 1) * 128, :]
                         ).then_inc(xs.h, 16)
            xs.n += 16
            _ln_stats(b, acc[:, i, :], 8 + (i % 2))
            dve.wait_ge(ea.h, ea.n)
            _ln_mr(b, slice(8 + (i % 2), 9 + (i % 2)))
            dve.wait_ge((b.out if k == 0 else b.out2).h, ya_free[k])
            dve.wait_ge(xs.h, xs.n)
            dve.tensor_scalar(ya[:, k, :], acc[:, i, :], b.a_mcol[:, 8 + i % 2:9 + i % 2],
                              b.a_rcol[:, 8 + i % 2:9 + i % 2], ALU.subtract, ALU.mult)
            dve.tensor_tensor(ya[:, k, :], ya[:, k, :], xt[:, k, :], ALU.add)
            xt_free[k] = ev.n
            os_ = b.out if k == 0 else b.out2
            sp.wait_ge(ev.h, ev.n)
            sp.dma_start(out=io["x2"][c * CH + i * 128:c * CH + (i + 1) * 128, :],
                         in_=ya[:, k, :]).then_inc(os_.h, 16)
            os_.n += 16
            ya_free[k] = os_.n

    sp.wait_ge(b.out.h, b.out.n)
    sp.wait_ge(b.out2.h, b.out2.n)


def hostprep_b(inputs):
    iota = np.tile(np.arange(16, dtype=np.float32), (128, 1))
    return dict(
        gate_wT=np.ascontiguousarray(np.asarray(inputs["L0_ffn_gate_w"]).T),
        expert_bias=np.asarray(inputs["L0_ffn_expert_bias"]),
        w1T=np.ascontiguousarray(np.asarray(inputs["L0_ffn_w1"]).transpose(0, 2, 1)),
        w2T=np.ascontiguousarray(np.asarray(inputs["L0_ffn_w2"]).transpose(0, 2, 1)),
        w3T=np.ascontiguousarray(np.asarray(inputs["L0_ffn_w3"]).transpose(0, 2, 1)),
        sw1T=np.ascontiguousarray(np.asarray(inputs["L0_ffn_sw1"]).T),
        sw2T=np.ascontiguousarray(np.asarray(inputs["L0_ffn_sw2"]).T),
        sw3T=np.ascontiguousarray(np.asarray(inputs["L0_ffn_sw3"]).T),
        iota16=iota, iotamB16=np.ascontiguousarray(iota - BIG),
        ident=np.eye(128, dtype=BF))


B_IO = [("gate_wT", [D, 16], F32), ("expert_bias", [16], F32),
        ("w1T", [NE, D, D], F32), ("w2T", [NE, D, D], F32), ("w3T", [NE, D, D], F32),
        ("sw1T", [D, D], F32), ("sw2T", [D, D], F32), ("sw3T", [D, D], F32),
        ("iota16", [128, 16], F32), ("iotamB16", [128, 16], F32),
        ("ident", [128, 128], BF16)]


def build_b_program():
    b = B()
    io = {n: b.dram(n, s, d, "ExternalInput") for n, s, d in B_IO}
    io["x1"] = b.dram("x1", [T, D], F32, "ExternalInput")
    io["x2"] = b.dram("x2", [T, D], F32, "ExternalOutput")
    with b.ctx:
        blk = b.ctx.enter_context(b.nc.Block())

        @blk.gpsimd
        def _(g):
            build_stage_b(b, io)
    return b.nc




def build_stage_c(b, io, n_cores=8):
    CH = 1024
    NCH = T // CH
    NT = CH // 128
    nc = b.nc
    pe, act, dve, pool, sp = b.pe, b.act, b.dve, b.pool, b.sp
    dma, mm, ea, ev, ep = b.dma, b.mm, b.ea, b.ev, b.ep

    qT_h = io["qT_h"]
    kT_h = io["kT_h"]
    v_h = io["v_h"]
    cc_in = io["cc_in"]
    cc_out = io["cc_out"]

    energy = b.sb("c_energy", [16, T], F32)
    mask_all = b.sb("c_mask", [16, T], BF16)
    ones1c = b.sb("c_ones1c", [1, 128], F32)
    mk16 = b.sb("c_mk16", [16, 1], F32)
    qstab = b.sb("c_qstab", [16, T], BF16)
    thr = b.sb("c_thr", [1, 1], F32)
    thr16 = b.sb("c_thr16", [16, 1], F32)
    srow = b.sb("c_srow", [16, 1], F32)
    s11 = b.sb("c_s11", [1, 1], F32)
    ones2 = b.sb("c_ones2", [128, 16], F32)
    selmat = b.sb("c_selmat", [128, 8, 16], BF16)
    eh64 = b.sb("c_eh64", [16, NH * 64], BF16)
    ctx_fm = b.sb("c_ctx", [128, 8, T], BF16)
    id_sb = b.sb("c_id", [128, 128], BF16)
    b.a_s1 = b.sb("c_s1", [128, 16], F32)
    b.a_ssq = b.sb("c_ssq", [128, 16], F32)
    b.a_mcol = b.sb("c_mcol", [128, 16], F32)
    b.a_vcol = b.sb("c_vcol", [128, 16], F32)
    b.a_rcol = b.sb("c_rcol", [128, 16], F32)
    b.a_sqscr = b.sb("c_sqscr", [128, D], BF16)
    b.a_ecol = b.sb("c_ecol", [128, 1], F32)

    pool.memset(ones2[:], 1.0)
    pool.memset(ones1c[:], 1.0)
    pool.memset(b.a_ecol[:], EPS)
    pool.memset(mk16[:], 0.0)
    pn0 = ep.n
    for dst, src in [(id_sb[:], io["ident"]), (selmat[:], io["selmat"]),
                     (eh64[:], io["eh64"])]:
        sp.dma_start(out=dst, in_=src).then_inc(b.cdma.h, 16)
        b.cdma.n += 16
    c_const = b.cdma.n

    with ExitStack() as c1ctx:
        sb1 = lambda n, sh, dt: c1ctx.enter_context(nc.sbuf_tensor(n, sh, dt))
        xt = sb1("c_xt", [128, 2, D], F32)
        xn = sb1("c_xn", [128, NT, D], BF16)
        xnT = sb1("c_xnT", [128, 8, CH], BF16)
        wq_sb = sb1("c_wq", [128, 8, D], BF16)
        wk_sb = sb1("c_wk", [128, 8, D], BF16)
        wv_sb = sb1("c_wv", [128, 8, D], BF16)
        bq_sb = sb1("c_bq", [128, 8], F32)
        bk_sb = sb1("c_bk", [128, 8], F32)
        bvrow = sb1("c_bvrow", [1, D], F32)
        bv_bc = sb1("c_bvbc", [128, D], F32)
        stage = sb1("c_stage", [128, 2, D], BF16)
        sqs = sb1("c_sqs", [128, CH], BF16)
        ek_scr = sb1("c_ekscr", [16, CH], BF16)
        ekcol = sb1("c_ekcol", [16, 1], F32)

        ones1 = sb1("c_ones1", [1, 128], F32)
        with ExitStack() as p1ctx:
            pp = p1ctx.enter_context(nc.psum_tensor("c_pp", [128, D], F32))
            pp2 = p1ctx.enter_context(nc.psum_tensor("c_pp2", [128, D], F32))
            pes = p1ctx.enter_context(nc.psum_tensor("c_pes", [16, CH], F32))

            pool.memset(ones1[:], 1.0)
            pn = ep.n
            for dst, src in [(wq_sb[:], io["wqT1"]), (wk_sb[:], io["wkT1"]),
                             (wv_sb[:], io["wvT1"])]:
                pool.dma_start(out=dst, in_=src.rearrange("(kc p) n -> p kc n", p=128)
                               ).then_inc(b.pdma.h, 16)
                b.pdma.n += 16
            for dst, src in [(bq_sb[:], io["bq1"]), (bk_sb[:], io["bk1"]),
                             (bvrow[:], io["bv1"].unsqueeze(0))]:
                sp.dma_start(out=dst, in_=src).then_inc(b.cdma.h, 16)
                b.cdma.n += 16
            w1_loaded = b.pdma.n
            pe.wait_ge(b.cdma.h, b.cdma.n)
            pe.wait_ge(ep.h, pn)
            for half in range(2):
                sl = slice(half * 512, (half + 1) * 512)
                last = pe.matmul(pp[:, sl], ones1[:], bvrow[:, sl], start=True, stop=True)
            bv_mm = mm.inc(last)
            dve.wait_ge(mm.h, bv_mm)
            dve.tensor_copy(bv_bc[:], pp[:])
            free_pp = ev.n
            free_pp2 = ev.n
            pe.wait_ge(b.pdma.h, w1_loaded)

            for c in range(NCH):
                xt_free = [ev.n, ev.n]
                ptv = [pp[:, 0:64].bitcast(BF16), pp2[:, 0:64].bitcast(BF16)]
                free_pt = [free_pp, free_pp2]
                for i in range(NT):
                    k = i % 2
                    xs = b.xdma if k == 0 else b.xdma2
                    sp.wait_ge(ev.h, xt_free[k])
                    sp.dma_start(out=xt[:, k, :],
                                 in_=io["x2"][c * CH + i * 128:c * CH + (i + 1) * 128, :]
                                 ).then_inc(xs.h, 16)
                    xs.n += 16
                    dve.wait_ge(xs.h, xs.n)
                    act.wait_ge(xs.h, xs.n)
                    _ln_stats(b, xt[:, k, :], i)
                    dve.wait_ge(ea.h, ea.n)
                    _ln_mr(b, slice(i, i + 1))
                    dve.tensor_scalar(xn[:, i, :], xt[:, k, :], b.a_mcol[:, i:i + 1],
                                      b.a_rcol[:, i:i + 1], ALU.subtract, ALU.mult)
                    xt_free[k] = ev.n
                    ev_xn_i = ev.n
                    for fc in range(8):
                        kk = (i * 8 + fc) % 2
                        pe.wait_ge(ev.h, max(free_pt[kk], ev_xn_i))
                        t_mm = mm.inc(pe.transpose(ptv[kk], xn[:, i, fc * 128:(fc + 1) * 128],
                                                   id_sb[:]))
                        dve.wait_ge(mm.h, t_mm)
                        dve.tensor_copy(xnT[:, fc, i * 128:(i + 1) * 128], ptv[kk])
                        free_pt[kk] = ev.n
                free_pp, free_pp2 = free_pt
                pe.wait_ge(ev.h, ev.n)

                for wsb, bcol, isq, hdr in [(wq_sb, bq_sb, True, qT_h),
                                            (wk_sb, bk_sb, False, kT_h)]:
                    st_free = [b.out.n, b.out2.n]
                    for m in range(8):
                        k = m % 2
                        ppu = pp if k == 0 else pp2
                        pe.wait_ge(ev.h, free_pp if k == 0 else free_pp2)
                        for kc in range(8):
                            for half in range(2):
                                sl = slice(half * 512, (half + 1) * 512)
                                last = pe.matmul(ppu[:, sl],
                                                 wsb[:, kc, m * 128:(m + 1) * 128],
                                                 xnT[:, kc, sl],
                                                 start=(kc == 0), stop=(kc == 7))
                        q_mm = mm.inc(last)
                        dve.wait_ge(mm.h, q_mm)
                        os_ = b.out if k == 0 else b.out2
                        dve.wait_ge(os_.h, st_free[k])
                        dve.tensor_scalar(stage[:, k, :], ppu[:], bcol[:, m:m + 1], None,
                                          ALU.add)
                        if k == 0:
                            free_pp = ev.n
                        else:
                            free_pp2 = ev.n
                        act.wait_ge(ev.h, ev.n)
                        act.activation(sqs[:], stage[:, k, :], AF.Square)
                        pe.wait_ge(ea.h, ea.n)
                        for half in range(2):
                            sl = slice(half * 512, (half + 1) * 512)
                            last = pe.matmul(pes[:, sl], selmat[:, m, :], sqs[:, sl],
                                             start=(m == 0), stop=(m == 7))
                        es_mm = mm.inc(last)
                        sp.wait_ge(ev.h, ev.n)
                        sp.dma_start(out=hdr[m * 128:(m + 1) * 128, c * CH:(c + 1) * CH],
                                     in_=stage[:, k, :]).then_inc(os_.h, 16)
                        os_.n += 16
                        st_free[k] = os_.n
                    act.wait_ge(mm.h, mm.n)
                    act.wait_ge(ep.h, pn0)
                    if isq:
                        act.activation(energy[:, c * CH:(c + 1) * CH], pes[:], AF.Sqrt)
                    else:
                        act.activation(ek_scr[:], pes[:], AF.Sqrt)
                        dve.wait_ge(ea.h, ea.n)
                        dve.tensor_reduce(ekcol[:], ek_scr[:], axis=AX.X, op=ALU.max)
                        dve.tensor_tensor(mk16[:], mk16[:], ekcol[:], ALU.max)
                    pe.wait_ge(ea.h, ea.n)
                st_free = [b.out.n, b.out2.n]
                for i in range(NT):
                    k = i % 2
                    ppu = pp if k == 0 else pp2
                    pe.wait_ge(ev.h, free_pp if k == 0 else free_pp2)
                    for kc in range(8):
                        for half in range(2):
                            sl = slice(half * 512, (half + 1) * 512)
                            last = pe.matmul(ppu[:, sl], xnT[:, kc, i * 128:(i + 1) * 128],
                                             wv_sb[:, kc, sl], start=(kc == 0), stop=(kc == 7))
                    v_mm = mm.inc(last)
                    dve.wait_ge(mm.h, v_mm)
                    os_ = b.out if k == 0 else b.out2
                    dve.wait_ge(os_.h, st_free[k])
                    dve.scalar_tensor_tensor(stage[:, k, :], ppu[:], 1.0, bv_bc[:],
                                             ALU.mult, ALU.add)
                    if k == 0:
                        free_pp = ev.n
                    else:
                        free_pp2 = ev.n
                    sp.wait_ge(ev.h, ev.n)
                    sp.dma_start(out=v_h[c * CH + i * 128:c * CH + (i + 1) * 128, :],
                                 in_=stage[:, k, :]).then_inc(os_.h, 16)
                    os_.n += 16
                    st_free[k] = os_.n

    dve.wait_ge(ea.h, ea.n)
    dve.wait_ge(b.cdma.h, b.cdma.n)
    dve.tensor_reduce(srow[:], energy[:], axis=AX.X, op=ALU.add)
    with ExitStack() as ptx:
        p11 = ptx.enter_context(nc.psum_tensor("c_p11", [128, 16], F32))
        pe.wait_ge(ev.h, ev.n)
        pe.wait_ge(ep.h, pn0)
        l_mm = mm.inc(pe.matmul(p11[0:1, 0:1], srow[:], ones2[0:16, 0:1],
                                start=True, stop=True))
        dve.wait_ge(mm.h, l_mm)
        dve.tensor_copy(s11[:], p11[0:1, 0:1])
        pool.wait_ge(ev.h, ev.n)
        pool.dma_start(out=cc_in, in_=s11[:]).then_inc(b.pdma.h, 16)
        b.pdma.n += 16
        pool.wait_ge(b.pdma.h, b.pdma.n)
        pool.collective_compute("AllReduce", ALU.add,
                                replica_groups=[list(range(n_cores))],
                                ins=[cc_in.opt()], outs=[cc_out.opt()]).then_inc(b.ep.h, 1)
        b.ep.n += 1
        pool.wait_ge(b.ep.h, b.ep.n)
        pool.dma_start(out=thr[:], in_=cc_out).then_inc(b.pdma.h, 16)
        b.pdma.n += 16
        dve.wait_ge(b.pdma.h, b.pdma.n)
        dve.tensor_scalar_mul(thr[:], thr[:], 0.1 / (n_cores * T * NH))
        ev_thr = ev.n
        pe.wait_ge(ev.h, ev_thr)
        t_mm = mm.inc(pe.matmul(p11[0:16, 0:1], ones2[0:1, :], thr[:], start=True, stop=True))
        dve.wait_ge(mm.h, t_mm)
        dve.tensor_copy(thr16[:], p11[0:16, 0:1])
        dve.tensor_scalar(mask_all[:], energy[:], thr16[:], None, ALU.is_gt)
        dve.tensor_scalar_mul(ekcol2 := b.a_vcol[0:16, 14:15], mk16[:], -1.0)
        dve.tensor_tensor(energy[:], energy[:], mask_all[:], ALU.mult)
        dve.tensor_scalar(qstab[:], energy[:], b.a_vcol[0:16, 14:15], None, ALU.mult)
        ev_mask = ev.n

    KTG = 2
    with ExitStack() as c2ctx:
        sb2 = lambda n, sh, dt: c2ctx.enter_context(nc.sbuf_tensor(n, sh, dt))
        qh = [sb2(f"c_qh{p}", [64, T], BF16) for p in range(2)]
        kh = [sb2(f"c_kh{p}", [65, T], BF16) for p in range(2)]
        vh = [sb2(f"c_vh{p}", [128, 32, 65], BF16) for p in range(2)]
        qm = [sb2(f"c_qm{p}", [65, 512], BF16) for p in range(2)]
        phat = [sb2(f"c_ph{p}", [128, KTG * 512], BF16) for p in range(2)]
        rrow = sb2("c_rrow", [128, 512], F32)
        recipb = sb2("c_recipb", [64, 512], BF16)
        onesc = sb2("c_onesc", [128, 128], F32)
        with ExitStack() as p2ctx:
            psc2 = [p2ctx.enter_context(nc.psum_tensor(f"c_psc{p}", [128, KTG * 512], F32))
                    for p in range(2)]
            pctx = [p2ctx.enter_context(nc.psum_tensor(f"c_pctx{p}", [65, 512], F32))
                    for p in range(2)]
            pmb = [p2ctx.enter_context(nc.psum_tensor(f"c_pmb{p}", [128, 512], F32))
                   for p in range(2)]

            pool.wait_ge(ev.h, ev.n)
            pool.wait_ge(ea.h, ea.n)
            pool.wait_ge(mm.h, mm.n)
            pool.memset(onesc[:], 1.0)
            pool.memset(vh[0][:], 1.0)
            pool.memset(vh[1][:], 1.0)
            pool.memset(kh[0][:], 1.0)
            pool.memset(kh[1][:], 1.0)
            pn2 = ep.n
            hd_sem = [b.hd[0], b.hd[1]]
            h_loaded = {}
            h_done = {}

            def load_head(h):
                par = h % 2
                ws = hd_sem[par]
                if h >= 2:
                    sp.wait_ge(mm.h, h_done[h - 2])
                sp.dma_start(out=qh[par][:], in_=qT_h[h * 64:(h + 1) * 64, :]
                             ).then_inc(ws.h, 16)
                sp.dma_start(out=kh[par][0:64, :], in_=kT_h[h * 64:(h + 1) * 64, :]
                             ).then_inc(ws.h, 16)
                sp.dma_start(out=vh[par][:, :, 0:64],
                             in_=v_h[:, h * 64:(h + 1) * 64]
                             .rearrange("(kt p) d -> p kt d", p=128)).then_inc(ws.h, 16)
                ws.n += 48
                h_loaded[h] = ws.n

            sp.wait_ge(b.out.h, b.out.n)
            sp.wait_ge(b.out2.h, b.out2.n)
            sp.wait_ge(ep.h, pn2)
            load_head(0)
            load_head(1)
            free_phat = [mm.n, mm.n]
            free_psc_ea = [ea.n, ea.n]
            free_pctx = [ev.n, ev.n]
            qmf = [mm.n, mm.n]
            free_pmb = [ev.n, ev.n]
            ev_qm_s = [ev.n, ev.n]
            ngrp = 32 // KTG
            pending = None
            act.wait_ge(ev.h, ev_mask)
            pairs = [(h, qt) for h in range(NH) for qt in range(8)]

            def emit_prologue(idx):
                ph, pqt = pairs[idx]
                ppar = ph % 2
                qs = pqt % 2
                if pqt == 0:
                    pe.wait_ge(b.hd[ppar].h, h_loaded[ph])
                    pe.wait_ge(ev.h, ev_mask)
                    pe.wait_ge(ep.h, pn2)
                pe.wait_ge(ev.h, free_pmb[qs])
                pe.matmul(pmb[qs][0:64, :], eh64[:, ph * 64:(ph + 1) * 64],
                          mask_all[:, pqt * 512:(pqt + 1) * 512],
                          start=True, stop=True)
                mb_mm = mm.inc(pe.matmul(pmb[qs][64:65, :], id_sb[0:16, ph:ph + 1],
                                         qstab[:, pqt * 512:(pqt + 1) * 512],
                                         start=True, stop=True))
                dve.wait_ge(mm.h, mb_mm)
                dve.wait_ge(mm.h, qmf[qs])
                dve.tensor_tensor(qm[qs][0:64, :],
                                  qh[ppar][:, pqt * 512:(pqt + 1) * 512],
                                  pmb[qs][0:64, :], ALU.mult)
                dve.tensor_copy(qm[qs][64:65, :], pmb[qs][64:65, :])
                free_pmb[qs] = ev.n
                ev_qm_s[qs] = ev.n

            def emit_recip(ph, pqt, pctx_mm):
                pqs = pqt % 2
                dve.wait_ge(mm.h, pctx_mm)
                dve.reciprocal(rrow[64:65, :], pctx[pqs][64:65, :])
                return ev.n

            def emit_normalize(ph, pqt, pctx_mm, ev_rec):
                pqs = pqt % 2
                pe.wait_ge(ev.h, ev_rec)
                pe.wait_ge(ev.h, free_pmb[pqs])
                r_mm = mm.inc(pe.matmul(pmb[pqs][0:64, :], onesc[64:65, 0:64],
                                        rrow[64:65, :], start=True, stop=True))
                dve.wait_ge(mm.h, r_mm)
                dve.tensor_copy(recipb[:], pmb[pqs][0:64, :])
                free_pmb[pqs] = ev.n
                pfc, pr0 = ph // 2, 64 * (ph % 2)
                dve.scalar_tensor_tensor(
                    ctx_fm[pr0:pr0 + 64, pfc, pqt * 512:(pqt + 1) * 512],
                    pctx[pqs][0:64, :], 1.0, recipb[:], ALU.mult, ALU.mult)
                free_pctx[pqs] = ev.n

            emit_prologue(0)
            for idx, (h, qt) in enumerate(pairs):
                par = h % 2
                qs = qt % 2

                def emit_sc(g):
                    sl_par = g % 2
                    for j, kt in enumerate(range(g * KTG, (g + 1) * KTG)):
                        last = pe.matmul(psc2[sl_par][:, j * 512:(j + 1) * 512],
                                         kh[par][:, kt * 128:(kt + 1) * 128],
                                         qm[qs][:], start=True, stop=True)
                    return mm.inc(last)

                sc_mm = [0] * ngrp
                if pending is not None:
                    pend_rec = emit_recip(*pending)
                pe.wait_ge(ev.h, ev_qm_s[qs])
                pe.wait_ge(ea.h, free_psc_ea[0])
                sc_mm[0] = emit_sc(0)
                for g in range(ngrp):
                    sl_par = g % 2
                    if g + 1 < ngrp:
                        pe.wait_ge(ea.h, free_psc_ea[(g + 1) % 2])
                        sc_mm[g + 1] = emit_sc(g + 1)
                    if g == 1 and pending is not None:
                        emit_normalize(*pending, pend_rec)
                        pending = None
                    if g == 3 and idx + 1 < len(pairs):
                        emit_prologue(idx + 1)
                    act.wait_ge(mm.h, sc_mm[g])
                    act.wait_ge(mm.h, free_phat[sl_par])
                    act.activation(phat[sl_par][:], psc2[sl_par][:],
                                   AF.Exp, scale=0.125)
                    free_psc_ea[sl_par] = ea.n
                    pe.wait_ge(ea.h, free_psc_ea[sl_par])
                    if g == 0:
                        pe.wait_ge(ev.h, free_pctx[qs])
                    for j, kt in enumerate(range(g * KTG, (g + 1) * KTG)):
                        last = pe.matmul(pctx[qs][:], vh[par][:, kt, :],
                                         phat[sl_par][:, j * 512:(j + 1) * 512],
                                         start=(kt == 0), stop=(kt == 31))
                    free_phat[sl_par] = mm.inc(last)
                qmf[qs] = sc_mm[ngrp - 1]
                pending = (h, qt, free_phat[(ngrp - 1) % 2])
                if qt == 7:
                    h_done[h] = mm.n
                    if h + 2 < NH:
                        load_head(h + 2)
            emit_normalize(*pending, emit_recip(*pending))
            pending = None

    with ExitStack() as c3ctx:
        sb3 = lambda n, sh, dt: c3ctx.enter_context(nc.sbuf_tensor(n, sh, dt))
        wo_sb = sb3("c_wo", [128, 8, D], BF16)
        borow = sb3("c_borow", [1, D], F32)
        bo_bc = sb3("c_bobc", [128, D], F32)
        xt = sb3("c_xt3", [128, 2, D], F32)
        ya = sb3("c_ya3", [128, 2, D], F32)
        yb = sb3("c_yb3", [128, D], F32)
        ones1b = sb3("c_ones1b", [1, 128], F32)
        with ExitStack() as p3ctx:
            pp = p3ctx.enter_context(nc.psum_tensor("c3_pp", [128, D], F32))
            pp2 = p3ctx.enter_context(nc.psum_tensor("c3_pp2", [128, D], F32))
            pool.wait_ge(ev.h, ev.n)
            pool.wait_ge(ea.h, ea.n)
            pool.wait_ge(mm.h, mm.n)
            pool.memset(ones1b[:], 1.0)
            pn3 = ep.n
            pool.dma_start(out=wo_sb[:], in_=io["woT1"].rearrange("(kc p) n -> p kc n", p=128)
                           ).then_inc(b.pdma.h, 16)
            b.pdma.n += 16
            sp.wait_ge(mm.h, mm.n)
            sp.wait_ge(ev.h, ev.n)
            sp.wait_ge(ea.h, ea.n)
            sp.dma_start(out=borow[:], in_=io["bo1"].unsqueeze(0)).then_inc(b.cdma.h, 16)
            b.cdma.n += 16
            pe.wait_ge(b.cdma.h, b.cdma.n)
            pe.wait_ge(ep.h, pn3)
            for half in range(2):
                sl = slice(half * 512, (half + 1) * 512)
                last = pe.matmul(pp[:, sl], ones1b[:], borow[:, sl], start=True, stop=True)
            bo_mm = mm.inc(last)
            dve.wait_ge(mm.h, bo_mm)
            dve.tensor_copy(bo_bc[:], pp[:])
            free_pp = ev.n
            free_pp2 = ev.n
            pe.wait_ge(b.pdma.h, b.pdma.n)
            pe.wait_ge(ev.h, ev.n)
            xt_free = [ev.n, ev.n]
            ya_free = [b.out.n, b.out2.n]
            for ti in range(T // 128):
                k = ti % 2
                xs = b.xdma if k == 0 else b.xdma2
                sp.wait_ge(ev.h, xt_free[k])
                sp.dma_start(out=xt[:, k, :], in_=io["x2"][ti * 128:(ti + 1) * 128, :]
                             ).then_inc(xs.h, 16)
                xs.n += 16
                ppu = pp if k == 0 else pp2
                pe.wait_ge(ev.h, free_pp if k == 0 else free_pp2)
                for kc in range(8):
                    for half in range(2):
                        sl = slice(half * 512, (half + 1) * 512)
                        last = pe.matmul(ppu[:, sl], ctx_fm[:, kc, ti * 128:(ti + 1) * 128],
                                         wo_sb[:, kc, sl], start=(kc == 0), stop=(kc == 7))
                o_mm = mm.inc(last)
                dve.wait_ge(mm.h, o_mm)
                dve.wait_ge((b.out if k == 0 else b.out2).h, ya_free[k])
                dve.scalar_tensor_tensor(ya[:, k, :], ppu[:], 1.0, bo_bc[:],
                                         ALU.mult, ALU.add)
                if k == 0:
                    free_pp = ev.n
                else:
                    free_pp2 = ev.n
                act.wait_ge(ev.h, ev.n)
                _ln_stats(b, ya[:, k, :], 8 + k)
                dve.wait_ge(ea.h, ea.n)
                _ln_mr(b, slice(8 + k, 9 + k))
                os_ = b.out if k == 0 else b.out2
                dve.wait_ge(xs.h, xs.n)
                dve.tensor_scalar(yb[:], ya[:, k, :], b.a_mcol[:, 8 + k:9 + k],
                                  b.a_rcol[:, 8 + k:9 + k], ALU.subtract, ALU.mult)
                dve.tensor_tensor(ya[:, k, :], yb[:], xt[:, k, :], ALU.add)
                xt_free[k] = ev.n
                sp.wait_ge(ev.h, ev.n)
                sp.dma_start(out=io["x3"][ti * 128:(ti + 1) * 128, :],
                             in_=ya[:, k, :]).then_inc(os_.h, 16)
                os_.n += 16
                ya_free[k] = os_.n

    sp.wait_ge(b.out.h, b.out.n)
    sp.wait_ge(b.out2.h, b.out2.n)


def hostprep_c(inputs):
    wqkv = np.asarray(inputs["L1_attn_w_qkv"])
    bqkv = np.asarray(inputs["L1_attn_b_qkv"])
    selmat = np.zeros((128, 8, 16), np.float32)
    for m in range(8):
        selmat[0:64, m, 2 * m] = 1.0
        selmat[64:128, m, 2 * m + 1] = 1.0
    eh64 = np.zeros((16, NH * 64), np.float32)
    for h in range(NH):
        eh64[h, h * 64:(h + 1) * 64] = 1.0
    return dict(
        wqT1=np.ascontiguousarray(wqkv[0:D].T),
        wkT1=np.ascontiguousarray(wqkv[D:2 * D].T),
        wvT1=np.ascontiguousarray(wqkv[2 * D:].T),
        bq1=np.ascontiguousarray(bqkv[0:D].reshape(8, 128).T),
        bk1=np.ascontiguousarray(bqkv[D:2 * D].reshape(8, 128).T),
        bv1=np.ascontiguousarray(bqkv[2 * D:]),
        woT1=np.ascontiguousarray(np.asarray(inputs["L1_attn_w_out"]).T),
        bo1=np.asarray(inputs["L1_attn_b_out"]),
        selmat=selmat.astype(BF), eh64=eh64.astype(BF),
        ident=np.eye(128, dtype=BF))


C_IO = [("wqT1", [D, D], F32), ("wkT1", [D, D], F32), ("wvT1", [D, D], F32),
        ("bq1", [128, 8], F32), ("bk1", [128, 8], F32), ("bv1", [D], F32),
        ("woT1", [D, D], F32), ("bo1", [D], F32),
        ("selmat", [128, 8, 16], BF16),
        ("eh64", [16, NH * 64], BF16), ("ident", [128, 128], BF16)]


def build_c_program(n_cores=8):
    b = B()
    io = {n: b.dram(n, s, d, "ExternalInput") for n, s, d in C_IO}
    io["x2"] = b.dram("x2", [T, D], F32, "ExternalInput")
    io["x3"] = b.dram("x3", [T, D], F32, "ExternalOutput")
    io["qT_h"] = b.dram("qT_h", [D, T], BF16, "Internal")
    io["kT_h"] = b.dram("kT_h", [D, T], BF16, "Internal")
    io["v_h"] = b.dram("v_h", [T, D], BF16, "Internal")
    io["cc_in"] = b.dram("cc_in", [1, 1], F32, "Internal")
    io["cc_out"] = b.dram("cc_out", [1, 1], F32, "Internal")
    with b.ctx:
        blk = b.ctx.enter_context(b.nc.Block())

        @blk.gpsimd
        def _(g):
            build_stage_c(b, io, n_cores=n_cores)
    return b.nc




def build_stage_d(b, io):
    nc = b.nc
    pe, act, dve, pool, sp = b.pe, b.act, b.dve, b.pool, b.sp
    dma, mm, ea, ev, ep = b.dma, b.mm, b.ea, b.ev, b.ep
    NTT = T // 128

    swT = b.sb("d_swT", [128, 8, D], BF16)
    id_sb = b.sb("d_id", [128, 128], BF16)
    xt = b.sb("d_xt", [128, 2, D], F32)
    xn = b.sb("d_xn", [128, D], BF16)
    xnT_all = b.sb("d_xnT", [128, 8, T], BF16)
    ya = b.sb("d_ya", [128, 2, D], F32)
    yb = b.sb("d_yb", [128, D], F32)
    b.a_s1 = b.sb("d_s1", [128, 16], F32)
    b.a_ssq = b.sb("d_ssq", [128, 16], F32)
    b.a_mcol = b.sb("d_mcol", [128, 16], F32)
    b.a_vcol = b.sb("d_vcol", [128, 16], F32)
    b.a_rcol = b.sb("d_rcol", [128, 16], F32)
    b.a_sqscr = b.sb("d_sqscr", [128, D], BF16)
    b.a_ecol = b.sb("d_ecol", [128, 1], F32)

    with ExitStack() as pctx:
        pp = pctx.enter_context(nc.psum_tensor("d_pp", [128, D], F32))
        pp2 = pctx.enter_context(nc.psum_tensor("d_pp2", [128, D], F32))

        pool.wait_ge(ev.h, ev.n)
        pool.wait_ge(ea.h, ea.n)
        pool.wait_ge(mm.h, mm.n)
        pool.memset(b.a_ecol[:], EPS)
        pool.dma_start(out=swT[:], in_=io["wT_d"].rearrange("(kc p) n -> p kc n", p=128)
                       ).then_inc(b.pdma.h, 16)
        b.pdma.n += 16
        sp.wait_ge(mm.h, mm.n)
        sp.wait_ge(ev.h, ev.n)
        sp.wait_ge(ea.h, ea.n)
        sp.dma_start(out=id_sb[:], in_=io["ident"]).then_inc(b.cdma.h, 16)
        b.cdma.n += 16
        act.wait_ge(b.pdma.h, b.pdma.n)
        act.activation(swT[:], swT[:], AF.Sign)
        sw_ea = ea.n
        pe.wait_ge(ea.h, sw_ea)
        pe.wait_ge(b.cdma.h, b.cdma.n)

        free_pt = [ev.n, ev.n]
        xt_free = [ev.n, ev.n]
        ptv = [pp[:, 0:64].bitcast(BF16), pp2[:, 0:64].bitcast(BF16)]
        for ti in range(NTT):
            k = ti % 2
            xs = b.xdma if k == 0 else b.xdma2
            sp.wait_ge(ev.h, xt_free[k])
            sp.dma_start(out=xt[:, k, :], in_=io["x3"][ti * 128:(ti + 1) * 128, :]
                         ).then_inc(xs.h, 16)
            xs.n += 16
            dve.wait_ge(xs.h, xs.n)
            act.wait_ge(xs.h, xs.n)
            _ln_stats(b, xt[:, k, :], k)
            dve.wait_ge(ea.h, ea.n)
            _ln_mr(b, slice(k, k + 1))
            dve.wait_ge(mm.h, mm.n)
            dve.tensor_scalar(xn[:], xt[:, k, :], b.a_mcol[:, k:k + 1],
                              b.a_rcol[:, k:k + 1], ALU.subtract, ALU.mult)
            xt_free[k] = ev.n
            ev_xn = ev.n
            pe.wait_ge(ev.h, ev_xn)
            for fc in range(8):
                kk = fc % 2
                pe.wait_ge(ev.h, free_pt[kk])
                t_mm = mm.inc(pe.transpose(ptv[kk], xn[:, fc * 128:(fc + 1) * 128], id_sb[:]))
                dve.wait_ge(mm.h, t_mm)
                dve.tensor_copy(xnT_all[:, fc, ti * 128:(ti + 1) * 128], ptv[kk])
                free_pt[kk] = ev.n

        pe.wait_ge(ev.h, ev.n)
        free_pp = ev.n
        free_pp2 = ev.n
        xt_free = [ev.n, ev.n]
        ya_free = [b.out.n, b.out2.n]
        for ti in range(NTT):
            k = ti % 2
            xs = b.xdma if k == 0 else b.xdma2
            os_ = b.out if k == 0 else b.out2
            sp.wait_ge(ev.h, xt_free[k])
            sp.dma_start(out=xt[:, k, :], in_=io["x3"][ti * 128:(ti + 1) * 128, :]
                         ).then_inc(xs.h, 16)
            xs.n += 16
            ppu = pp if k == 0 else pp2
            pe.wait_ge(ev.h, free_pp if k == 0 else free_pp2)
            for kc in range(8):
                for half in range(2):
                    sl = slice(half * 512, (half + 1) * 512)
                    last = pe.matmul(ppu[:, sl], xnT_all[:, kc, ti * 128:(ti + 1) * 128],
                                     swT[:, kc, sl], start=(kc == 0), stop=(kc == 7))
            y_mm = mm.inc(last)
            dve.wait_ge(mm.h, y_mm)
            dve.wait_ge(os_.h, ya_free[k])
            dve.tensor_copy(ya[:, k, :], ppu[:])
            if k == 0:
                free_pp = ev.n
            else:
                free_pp2 = ev.n
            act.wait_ge(ev.h, ev.n)
            _ln_stats(b, ya[:, k, :], 8 + k)
            dve.wait_ge(ea.h, ea.n)
            _ln_mr(b, slice(8 + k, 9 + k))
            dve.tensor_scalar(yb[:], ya[:, k, :], b.a_mcol[:, 8 + k:9 + k],
                              b.a_rcol[:, 8 + k:9 + k], ALU.subtract, ALU.mult)
            dve.wait_ge(xs.h, xs.n)
            dve.tensor_tensor(ya[:, k, :], yb[:], xt[:, k, :], ALU.add)
            xt_free[k] = ev.n
            sp.wait_ge(ev.h, ev.n)
            sp.dma_start(out=io["out"][ti * 128:(ti + 1) * 128, :],
                         in_=ya[:, k, :]).then_inc(os_.h, 16)
            os_.n += 16
            ya_free[k] = os_.n

    sp.wait_ge(b.out.h, b.out.n)
    sp.wait_ge(b.out2.h, b.out2.n)


def hostprep_d(inputs):
    return dict(wT_d=np.ascontiguousarray(np.asarray(inputs["L1_ffn_weight"]).T),
                ident=np.eye(128, dtype=BF))


D_IO = [("wT_d", [D, D], F32), ("ident", [128, 128], BF16)]


def build_d_program():
    b = B()
    io = {n: b.dram(n, s, d, "ExternalInput") for n, s, d in D_IO}
    io["x3"] = b.dram("x3", [T, D], F32, "ExternalInput")
    io["out"] = b.dram("out", [T, D], F32, "ExternalOutput")
    with b.ctx:
        blk = b.ctx.enter_context(b.nc.Block())

        @blk.gpsimd
        def _(g):
            build_stage_d(b, io)
    return b.nc


_PROG_CACHE = {}


def _get_prog(name, builder):
    if name not in _PROG_CACHE:
        _PROG_CACHE[name] = builder()
    return _PROG_CACHE[name]


def kernel(**inputs):
    return kernel_fused(**inputs)


def kernel_unfused(**inputs):
    x = np.asarray(inputs["L0_attn_x"])
    ncores = 8
    hpa = hostprep_a(inputs)
    hpb = hostprep_b(inputs)
    hpc = hostprep_c(inputs)
    hpd = hostprep_d(inputs)

    nc_a = _get_prog("a", build_a_program)
    in_maps = [dict(hpa, x0=np.ascontiguousarray(x[c])) for c in range(ncores)]
    res = run_bass_kernel_spmd(nc_a, in_maps, core_ids=list(range(ncores)))
    x1 = [np.asarray(res.results[c]["x1"]) for c in range(ncores)]

    nc_b = _get_prog("b", build_b_program)
    in_maps = [dict(hpb, x1=x1[c]) for c in range(ncores)]
    res = run_bass_kernel_spmd(nc_b, in_maps, core_ids=list(range(ncores)))
    x2 = [np.asarray(res.results[c]["x2"]) for c in range(ncores)]

    nc_c = _get_prog("c", build_c_program)
    in_maps = [dict(hpc, x2=x2[c]) for c in range(ncores)]
    res = run_bass_kernel_spmd(nc_c, in_maps, core_ids=list(range(ncores)))
    x3 = [np.asarray(res.results[c]["x3"]) for c in range(ncores)]

    nc_d = _get_prog("d", build_d_program)
    in_maps = [dict(hpd, x3=x3[c]) for c in range(ncores)]
    res = run_bass_kernel_spmd(nc_d, in_maps, core_ids=list(range(ncores)))
    out = np.stack([np.asarray(res.results[c]["out"]) for c in range(ncores)])
    return out



def _stage_barrier(b):
    sems = [b.mm, b.ea, b.ev, b.ep, b.out, b.out2, b.cdma, b.pdma,
            b.xdma, b.xdma2, b.wd[0], b.wd[1], b.hd[0], b.hd[1]]
    engs = [b.pe, b.nc.scalar, b.nc.vector, b.nc.gpsimd, b.sp]
    for e in engs:
        for sm in sems:
            if sm.n:
                e.wait_ge(sm.h, sm.n)


def build_fused_program(n_cores=8):
    b = B()
    io = {}
    for n, sh, dt in A_IO + B_IO + C_IO + D_IO:
        if n not in io:
            io[n] = b.dram(n, sh, dt, "ExternalInput")
    io["out"] = b.dram("out", [T, D], F32, "ExternalOutput")
    for n in ["x1", "x2", "x3"]:
        io[n] = b.dram(n, [T, D], F32, "Internal")
    io["qT_h"] = b.dram("qT_h", [D, T], BF16, "Internal")
    io["kT_h"] = b.dram("kT_h", [D, T], BF16, "Internal")
    io["v_h"] = b.dram("v_h", [T, D], BF16, "Internal")
    io["cc_in"] = b.dram("cc_in", [1, 1], F32, "Internal")
    io["cc_out"] = b.dram("cc_out", [1, 1], F32, "Internal")
    with b.ctx:
        blk = b.ctx.enter_context(b.nc.Block())

        @blk.gpsimd
        def _(g):
            for fn in [build_stage_a, build_stage_b,
                       lambda bb, ii: build_stage_c(bb, ii, n_cores=n_cores),
                       build_stage_d]:
                with ExitStack() as sc:
                    b.cur = sc
                    fn(b, io)
                b.cur = None
                _stage_barrier(b)
    return b.nc


def kernel_fused(**inputs):
    x = np.asarray(inputs["L0_attn_x"])
    ncores = 8
    hp = {}
    hp.update(hostprep_a(inputs))
    hp.update(hostprep_b(inputs))
    hp.update(hostprep_c(inputs))
    hp.update(hostprep_d(inputs))
    nc = _get_prog("fused", build_fused_program)
    in_maps = [dict(hp, x0=np.ascontiguousarray(x[c])) for c in range(ncores)]
    res = run_bass_kernel_spmd(nc, in_maps, core_ids=list(range(ncores)))
    return np.stack([np.asarray(res.results[c]["out"]) for c in range(ncores)])


# version 10
# speedup vs baseline: 1.0543x; 1.0085x over previous
import math
from contextlib import ExitStack

import numpy as np
import ml_dtypes

import concourse.bass as bass
import concourse.mybir as mybir
from concourse.bass_utils import run_bass_kernel_spmd

F32 = mybir.dt.float32
BF16 = mybir.dt.bfloat16
AF = mybir.ActivationFunctionType
ALU = mybir.AluOpType
AX = mybir.AxisListType
BF = ml_dtypes.bfloat16

A_MAXPH = 6
A_P5MODE = 4
A_NCH = None
A_SKIP_P2 = False
A_SKIP_P4 = False
A_SKIP_P1 = False
T = 4096
D = 1024
NH = 16
HD = 64
NE = 16
CH = 1024
NCH = T // CH
NT = CH // 128
EPS = 1e-5
BIG = 1.0e4


class Sem:
    def __init__(self, nc, name):
        self.h = nc.alloc_semaphore(name)
        self.n = 0

    def inc(self, inst, k=1):
        inst.then_inc(self.h, k)
        self.n += k
        return self.n


_WRAP = {
    "activation", "sign", "square", "sqrt", "mul", "copy", "add", "memzero",
    "memset", "tensor_copy", "tensor_tensor", "tensor_add", "tensor_sub",
    "tensor_mul", "tensor_max", "tensor_scalar", "tensor_single_scalar",
    "tensor_scalar_add", "tensor_scalar_sub", "tensor_scalar_mul",
    "tensor_scalar_max", "tensor_scalar_min", "tensor_relu", "tensor_reduce",
    "reduce_max", "reduce_sum", "scalar_tensor_tensor", "tensor_tensor_scan",
    "select", "copy_predicated", "transpose", "reciprocal", "iota",
    "affine_select",
}


class SEng:

    def __init__(self, eng, sem):
        self._e = eng
        self._s = sem

    def __getattr__(self, name):
        attr = getattr(self._e, name)
        if name in _WRAP:
            def f(*a, **k):
                self._e.wait_ge(self._s.h, self._s.n)
                inst = attr(*a, **k)
                self._s.inc(inst)
                return inst
            return f
        return attr


class B:

    def __init__(self):
        nc = bass.Bass("TRN2", target_bir_lowering=False)
        self.nc = nc
        self.ctx = ExitStack()
        self.cur = None
        self.pe = nc.tensor
        self.sp = nc.sync
        self.dma = Sem(nc, "dma")
        self.mm = Sem(nc, "mm")
        self.ea = Sem(nc, "ea")
        self.ev = Sem(nc, "ev")
        self.ep = Sem(nc, "ep")
        self.act = SEng(nc.scalar, self.ea)
        self.dve = SEng(nc.vector, self.ev)
        self.pool = SEng(nc.gpsimd, self.ep)
        self.out = Sem(nc, "outs")
        self.pdma = Sem(nc, "pdma")
        self.cdma = Sem(nc, "cdma")
        self.xdma = Sem(nc, "xdma")
        self.xdma2 = Sem(nc, "xdma2")
        self.wd = [Sem(nc, "wd0"), Sem(nc, "wd1")]
        self.hd = [Sem(nc, "hd0"), Sem(nc, "hd1")]
        self.out2 = Sem(nc, "outs2")

    def dram(self, name, shape, dtype, kind):
        return self.nc.dram_tensor(name, list(shape), dtype, kind=kind).ap()

    def sb(self, name, shape, dtype):
        return (self.cur or self.ctx).enter_context(
            self.nc.sbuf_tensor(name, list(shape), dtype))

    def ps(self, name, shape, dtype=F32):
        return (self.cur or self.ctx).enter_context(
            self.nc.psum_tensor(name, list(shape), dtype))


def _ln_stats(b, src, col):
    b.dve.tensor_reduce(b.a_s1[:, col:col + 1], src, axis=AX.X, op=ALU.add)
    b.act.activation(b.a_sqscr[:], src, AF.Square, accum_out=b.a_ssq[:, col:col + 1])


def _ln_mr(b, cols):
    dve, act, ea, ev = b.dve, b.act, b.ea, b.ev
    s = cols
    dve.tensor_scalar_mul(b.a_mcol[:, s], b.a_s1[:, s], 1.0 / D)
    dve.tensor_scalar_mul(b.a_vcol[:, s], b.a_ssq[:, s], 1.0 / D)
    dve.tensor_tensor(b.a_rcol[:, s], b.a_mcol[:, s], b.a_mcol[:, s], ALU.mult)
    dve.tensor_tensor(b.a_vcol[:, s], b.a_vcol[:, s], b.a_rcol[:, s], ALU.subtract)
    act.wait_ge(ev.h, ev.n)
    act.activation(b.a_rcol[:, s], b.a_vcol[:, s], AF.Sqrt, bias=b.a_ecol[:])
    dve.wait_ge(ea.h, ea.n)
    dve.reciprocal(b.a_rcol[:, s], b.a_rcol[:, s])


def build_stage_a(b, io):
    CH = 512
    NCH = A_NCH if A_NCH else T // CH
    NT = CH // 128
    nc = b.nc
    pe, act, dve, pool, sp = b.pe, b.act, b.dve, b.pool, b.sp
    dma, mm, ea, ev, ep = b.dma, b.mm, b.ea, b.ev, b.ep

    xa = b.sb("a_x", [128, NT, D], F32)
    xn = b.sb("a_xn", [128, NT, D], BF16)
    xnT = b.sb("a_xnT", [128, 8, CH], BF16)
    qf = b.sb("a_qf", [128, 8, CH], BF16)
    kf = b.sb("a_kf", [128, 8, 128 + CH], BF16)
    vt = b.sb("a_vt", [128, 1 + NT, D], BF16)
    msgT = b.sb("a_msgT", [128, 8, CH], BF16)
    tau = b.sb("a_tau", [128, 2048], BF16)
    tau2 = b.sb("a_tau2", [128, 2048], BF16)
    wq_sb = b.sb("a_wq", [128, 8, D], BF16)
    wk_sb = b.sb("a_wk", [128, 8, D], BF16)
    wv_sb = b.sb("a_wv", [128, 8, D], BF16)
    wo_sb = b.sb("a_wo", [128, 8, D], BF16)
    mask_sb = b.sb("a_mask", [128, 2048], BF16)
    id_sb = b.sb("a_id", [128, 128], BF16)
    grow = b.sb("a_grow", [1, D], F32)
    brow = b.sb("a_brow", [1, D], F32)
    ones1 = b.sb("a_ones1", [1, 128], F32)
    gbc = b.sb("a_gbc", [128, D], F32)
    bbc = b.sb("a_bbc", [128, D], F32)
    b.a_s1 = b.sb("a_s1", [128, 16], F32)
    b.a_ssq = b.sb("a_ssq", [128, 16], F32)
    b.a_mcol = b.sb("a_mcol", [128, 16], F32)
    b.a_vcol = b.sb("a_vcol", [128, 16], F32)
    b.a_rcol = b.sb("a_rcol", [128, 16], F32)
    b.a_sqscr = b.sb("a_sqscr", [128, D], BF16)
    b.a_ecol = b.sb("a_ecol", [128, 1], F32)
    ya = b.sb("a_ya", [128, D], F32)
    yb = b.sb("a_yb", [128, D], F32)

    pp = b.ps("a_pp", [128, D])
    pp2 = b.ps("a_pp2", [128, D])
    psc = b.ps("a_psc", [128, 2048])

    pool.memset(ones1[:], 1.0)
    pool.memset(b.a_ecol[:], EPS)
    pool.memset(kf[:], 0.0)
    pool.memset(vt[:], 0.0)
    pn = ep.n
    for dst, src in [(id_sb, io["ident"]), (mask_sb, io["maskAB"])]:
        sp.dma_start(out=dst[:], in_=src).then_inc(b.cdma.h, 16)
        b.cdma.n += 16
    sp.dma_start(out=grow[:], in_=io["gamma"].unsqueeze(0)).then_inc(b.cdma.h, 16)
    sp.dma_start(out=brow[:], in_=io["beta"].unsqueeze(0)).then_inc(b.cdma.h, 16)
    b.cdma.n += 32
    for wsb, wdr in [(wq_sb, io["wqT"]), (wk_sb, io["wkT"]), (wv_sb, io["wvT"]),
                     (wo_sb, io["woT"])]:
        pool.dma_start(out=wsb[:], in_=wdr.rearrange("(kc p) n -> p kc n", p=128)
                       ).then_inc(b.pdma.h, 16)
        b.pdma.n += 16
    w_loaded = b.pdma.n

    pe.wait_ge(b.cdma.h, b.cdma.n)
    pe.wait_ge(b.pdma.h, w_loaded)
    pe.wait_ge(ep.h, pn)
    for half in range(2):
        sl = slice(half * 512, (half + 1) * 512)
        pe.matmul(pp[:, sl], ones1[:], grow[:, sl], start=True, stop=True)
        last = pe.matmul(pp2[:, sl], ones1[:], brow[:, sl], start=True, stop=True)
    gb_mm = mm.inc(last)
    dve.wait_ge(mm.h, gb_mm)
    dve.wait_ge(ep.h, pn)
    dve.tensor_copy(gbc[:], pp[:])
    dve.tensor_copy(bbc[:], pp2[:])
    gb_ev = ev.n

    dve.wait_ge(b.cdma.h, b.cdma.n)
    act.wait_ge(b.cdma.h, b.cdma.n)
    free_pp = gb_ev
    free_pp2 = gb_ev
    free_psc_ea = 0
    free_tau_mm = 0

    for c in range(NCH):
        sp.wait_ge(b.out.h, b.out.n)
        sp.dma_start(out=xa[:], in_=io["x0"][c * CH:(c + 1) * CH, :]
                     .rearrange("(i p) d -> p i d", p=128)).then_inc(b.xdma.h, 16)
        b.xdma.n += 16
        x_loaded = b.xdma.n

        if A_MAXPH < 1:
            sp.wait_ge(b.xdma.h, b.xdma.n)
            sp.dma_start(out=io["x1"][c * CH:(c + 1) * CH, :]
                         .rearrange("(i p) d -> p i d", p=128), in_=xa[:]).then_inc(b.out.h, 16)
            b.out.n += 16
            continue
        dve.wait_ge(b.xdma.h, x_loaded)
        act.wait_ge(b.xdma.h, x_loaded)
        if A_SKIP_P1:
            pool.memset(xn[:], 0.01)
            dve.wait_ge(ep.h, ep.n)
        el_dummy = 0
        if not A_SKIP_P1:
            for i in range(NT):
                _ln_stats(b, xa[:, i, :], i)
            dve.wait_ge(ea.h, ea.n)
            _ln_mr(b, slice(0, NT))
            for i in range(NT):
                dve.tensor_scalar(xn[:, i, :], xa[:, i, :], b.a_mcol[:, i:i + 1],
                                  b.a_rcol[:, i:i + 1], ALU.subtract, ALU.mult)
        ev_xn = ev.n

        if A_MAXPH < 2:
            sp.wait_ge(ev.h, ev.n)
            sp.dma_start(out=io["x1"][c * CH:(c + 1) * CH, :]
                         .rearrange("(i p) d -> p i d", p=128), in_=xa[:]).then_inc(b.out.h, 16)
            b.out.n += 16
            continue
        if A_SKIP_P2:
            pool.memset(xnT[:], 0.01)
            dve.wait_ge(ep.h, ep.n)
            dve.tensor_copy(b.a_s1[:, 15:16], b.a_s1[:, 15:16])
        else:
            ptv = [pp[:, 0:64].bitcast(BF16), pp2[:, 0:64].bitcast(BF16)]
            free_pt = [free_pp, free_pp2]
            pe.wait_ge(ev.h, ev_xn)
            for u in range(NT * 8):
                i, fc = divmod(u, 8)
                k = u % 2
                pe.wait_ge(ev.h, free_pt[k])
                t_mm = mm.inc(pe.transpose(ptv[k], xn[:, i, fc * 128:(fc + 1) * 128], id_sb[:]))
                dve.wait_ge(mm.h, t_mm)
                dve.tensor_copy(xnT[:, fc, i * 128:(i + 1) * 128], ptv[k])
                free_pt[k] = ev.n
            free_pp, free_pp2 = free_pt

        if c > 0:
            dve.tensor_copy(kf[:, :, 0:128], kf[:, :, CH:CH + 128])
            dve.tensor_copy(vt[:, 0, :], vt[:, NT, :])
        else:
            dve.wait_ge(ep.h, pn)
        ev_carry = ev.n
        pe.wait_ge(ev.h, ev_carry)

        if A_MAXPH < 4:
            sp.wait_ge(ev.h, ev.n)
            sp.dma_start(out=io["x1"][c * CH:(c + 1) * CH, :]
                         .rearrange("(i p) d -> p i d", p=128), in_=xa[:]).then_inc(b.out.h, 16)
            b.out.n += 16
            continue
        if A_SKIP_P4:
            pool.memset(qf[:], 0.01)
            pool.memset(kf[:], 0.01)
            pool.memset(vt[:], 0.01)
            pe.wait_ge(ep.h, ep.n)
        pbuf = [pp, pp2]
        pfree = [free_pp, free_pp2]
        u = 0
        for wsb, kind in ([] if A_SKIP_P4 else [(wq_sb, "q"), (wk_sb, "k"), (wv_sb, "v")]):
            nm = NT if kind == "v" else 8
            for m in range(nm):
                k = u % 2
                u += 1
                ppu = pbuf[k]
                pe.wait_ge(ev.h, pfree[k])
                if kind == "v":
                    for kc in range(8):
                        for half in range(2):
                            sl = slice(half * 512, (half + 1) * 512)
                            last = pe.matmul(ppu[:, sl], xnT[:, kc, m * 128:(m + 1) * 128],
                                             wv_sb[:, kc, sl], start=(kc == 0), stop=(kc == 7))
                else:
                    for kc in range(8):
                        last = pe.matmul(ppu[:, 0:CH], wsb[:, kc, m * 128:(m + 1) * 128],
                                         xnT[:, kc, :], start=(kc == 0), stop=(kc == 7))
                m_mm = mm.inc(last)
                dve.wait_ge(mm.h, m_mm)
                if kind == "q":
                    dve.tensor_copy(qf[:, m, :], ppu[:, 0:CH])
                elif kind == "k":
                    dve.tensor_copy(kf[:, m, 128:128 + CH], ppu[:, 0:CH])
                else:
                    dve.tensor_copy(vt[:, 1 + m, :], ppu[:])
                pfree[k] = ev.n
        free_pp, free_pp2 = pfree
        pe.wait_ge(ev.h, ev.n)

        if A_MAXPH < 5:
            sp.wait_ge(ev.h, ev.n)
            sp.dma_start(out=io["x1"][c * CH:(c + 1) * CH, :]
                         .rearrange("(i p) d -> p i d", p=128), in_=xa[:]).then_inc(b.out.h, 16)
            b.out.n += 16
            continue
        if not hasattr(b, "_a_p5"):
            b._a_p5 = dict(ftau=[mm.n, mm.n], fpsc=ea.n, fmsg=ev.n, pend=None)
        st5 = b._a_p5
        tauh = [tau, tau2]

        def emit_sc5(qt, half, sl):
            pe.wait_ge(ea.h, st5["fpsc"])
            for hh in [0, 2, 4, 6, 1, 3, 5, 7]:
                h = half * 8 + hh
                fc, r0 = h // 2, 64 * (h % 2)
                for piece in range(2):
                    kb = qt + piece
                    c0 = hh * 256 + piece * 128
                    last = pe.matmul(psc[:, c0:c0 + 128],
                                     kf[r0:r0 + 64, fc, kb * 128:(kb + 1) * 128],
                                     qf[r0:r0 + 64, fc, qt * 128:(qt + 1) * 128],
                                     start=True, stop=True)
            return mm.inc(last)

        def emit_msg5(qt, half, sl, ev_tau):
            pe.wait_ge(ev.h, ev_tau)
            pe.wait_ge(ev.h, st5["fmsg"])
            for hh in range(8):
                h = half * 8 + hh
                for piece in range(2):
                    kb = qt + piece
                    last = pe.matmul(pp[0:64, hh * 128:(hh + 1) * 128],
                                     vt[:, kb, h * 64:(h + 1) * 64],
                                     tauh[sl][:, hh * 256 + piece * 128:hh * 256 + piece * 128 + 128],
                                     start=(piece == 0), stop=(piece == 1))
            st5["ftau"][sl] = mm.inc(last)
            dve.wait_ge(mm.h, st5["ftau"][sl])
            for hh in range(8):
                h = half * 8 + hh
                fc, r0 = h // 2, 64 * (h % 2)
                dve.tensor_copy(msgT[r0:r0 + 64, fc, qt * 128:(qt + 1) * 128],
                                pp[0:64, hh * 128:(hh + 1) * 128])
            st5["fmsg"] = ev.n

        for u, (qt, half) in enumerate([(q, hf) for q in range(NT) for hf in range(2)]):
            sl = u % 2
            sc_mm = emit_sc5(qt, half, sl)
            act.wait_ge(mm.h, sc_mm)
            act.wait_ge(mm.h, st5["ftau"][sl])
            act.activation(tauh[sl][:], psc[:], AF.Sigmoid, scale=0.125)
            st5["fpsc"] = ea.n
            dve.wait_ge(ea.h, ea.n)
            dve.tensor_tensor(tauh[sl][:], tauh[sl][:], mask_sb[:], ALU.mult)
            ev_tau = ev.n
            if st5["pend"] is not None:
                emit_msg5(*st5["pend"])
            st5["pend"] = (qt, half, sl, ev_tau)
        emit_msg5(*st5["pend"])
        st5["pend"] = None
        free_pp = st5["fmsg"]

        if A_MAXPH < 6:
            sp.wait_ge(ev.h, ev.n)
            sp.dma_start(out=io["x1"][c * CH:(c + 1) * CH, :]
                         .rearrange("(i p) d -> p i d", p=128), in_=xa[:]).then_inc(b.out.h, 16)
            b.out.n += 16
            continue
        ev_msg = ev.n
        pe.wait_ge(ev.h, ev_msg)
        for i in range(NT):
            pe.wait_ge(ev.h, free_pp2)
            for kc in range(8):
                for half in range(2):
                    sl = slice(half * 512, (half + 1) * 512)
                    last = pe.matmul(pp2[:, sl], msgT[:, kc, i * 128:(i + 1) * 128],
                                     wo_sb[:, kc, sl], start=(kc == 0), stop=(kc == 7))
            o_mm = mm.inc(last)
            dve.wait_ge(mm.h, o_mm)
            dve.scalar_tensor_tensor(ya[:], pp2[:], 1.0, xn[:, i, :], ALU.mult, ALU.add)
            free_pp2 = ev.n
            act.wait_ge(ev.h, free_pp2)
            _ln_stats(b, ya[:], 8)
            dve.wait_ge(ea.h, ea.n)
            _ln_mr(b, slice(8, 9))
            dve.tensor_scalar(yb[:], ya[:], b.a_mcol[:, 8:9], b.a_rcol[:, 8:9],
                              ALU.subtract, ALU.mult)
            dve.scalar_tensor_tensor(yb[:], yb[:], 1.0, gbc[:], ALU.mult, ALU.mult)
            dve.tensor_tensor(ya[:], yb[:], bbc[:], ALU.add)
            ev_ao = ev.n
            act.wait_ge(ev.h, ev_ao)
            _ln_stats(b, ya[:], 9)
            dve.wait_ge(ea.h, ea.n)
            _ln_mr(b, slice(9, 10))
            dve.tensor_scalar(yb[:], ya[:], b.a_mcol[:, 9:10], b.a_rcol[:, 9:10],
                              ALU.subtract, ALU.mult)
            dve.tensor_tensor(xa[:, i, :], xa[:, i, :], yb[:], ALU.add)

        sp.wait_ge(ev.h, ev.n)
        sp.dma_start(out=io["x1"][c * CH:(c + 1) * CH, :]
                     .rearrange("(i p) d -> p i d", p=128), in_=xa[:]).then_inc(b.out.h, 16)
        b.out.n += 16

    sp.wait_ge(b.out.h, b.out.n)


def hostprep_a(inputs):
    wq = np.ascontiguousarray(np.asarray(inputs["L0_attn_wq"]).T)
    wkv = np.asarray(inputs["L0_attn_wkv"]).reshape(NH, 2 * HD, D)
    wkT = np.ascontiguousarray(wkv[:, :HD].reshape(D, D).T)
    wvT = np.ascontiguousarray(wkv[:, HD:].reshape(D, D).T)
    woT = np.ascontiguousarray(np.asarray(inputs["L0_attn_wo"]).T)
    ki = np.arange(128)[:, None]
    qi = np.arange(128)[None, :]
    mA = ((ki - qi) >= 112).astype(BF)
    mB = (((ki - qi) >= -16) & ((ki - qi) <= -1)).astype(BF)
    maskAB = np.ascontiguousarray(np.concatenate([np.concatenate([mA, mB], 1)] * 8, 1))
    return dict(wqT=wq, wkT=wkT, wvT=wvT, woT=woT,
                gamma=np.asarray(inputs["L0_attn_gamma"]),
                beta=np.asarray(inputs["L0_attn_beta"]),
                maskAB=maskAB, ident=np.eye(128, dtype=BF))


A_IO = [("x0", [T, D], F32), ("wqT", [D, D], F32), ("wkT", [D, D], F32),
        ("wvT", [D, D], F32), ("woT", [D, D], F32), ("gamma", [D], F32),
        ("beta", [D], F32), ("maskAB", [128, 2048], BF16), ("ident", [128, 128], BF16)]


def build_a_program():
    b = B()
    io = {n: b.dram(n, s, d, "ExternalInput") for n, s, d in A_IO}
    io["x1"] = b.dram("x1", [T, D], F32, "ExternalOutput")
    with b.ctx:
        blk = b.ctx.enter_context(b.nc.Block())

        @blk.gpsimd
        def _(g):
            build_stage_a(b, io)
    return b.nc




def _free_as_ev(b, fr):
    sm, cnt = fr
    if sm is b.ev:
        return cnt
    b.dve.wait_ge(sm.h, cnt)
    return b.ev.n


def build_stage_b(b, io):
    CH = 1024
    NCH = T // CH
    NT = CH // 128
    nc = b.nc
    pe, act, dve, pool, sp = b.pe, b.act, b.dve, b.pool, b.sp
    dma, mm, ea, ev, ep = b.dma, b.mm, b.ea, b.ev, b.ep

    xt = b.sb("b_xt", [128, 2, D], F32)
    xn = b.sb("b_xn", [128, NT, D], BF16)
    xnT = b.sb("b_xnT", [128, 8, CH], BF16)
    acc = b.sb("b_acc", [128, NT, D], F32)
    h_sb = b.sb("b_h", [128, 8, CH], BF16)
    h1s = b.sb("b_h1s", [128, CH], BF16)
    wsl = [[b.sb(f"b_w{m}_{p}", [128, 8, D], BF16) for m in range(3)] for p in range(2)]
    gw_sb = b.sb("b_gw", [128, 8, 16], BF16)
    logits = b.sb("b_logits", [128, NT, 16], F32)
    g_all = b.sb("b_g", [128, NT, 16], F32)
    bwork = b.sb("b_bwork", [128, 16], F32)
    selm = b.sb("b_selm", [128, 16], F32)
    t1 = b.sb("b_t1", [128, 16], F32)
    t2 = b.sb("b_t2", [128, 16], F32)
    lmrow = b.sb("b_lm", [128, 16], F32)
    den = b.sb("b_den", [128, 1], F32)
    ebias_bc = b.sb("b_ebias", [128, 16], F32)
    iota_bc = b.sb("b_iota", [128, 16], F32)
    iotamB_bc = b.sb("b_iotamB", [128, 16], F32)
    gones = b.sb("b_gones", [128, 1], F32)
    ebrow = b.sb("b_ebrow", [1, 16], F32)
    ones1 = b.sb("b_ones1", [1, 128], F32)
    id_sb = b.sb("b_id", [128, 128], BF16)
    b.a_s1 = b.sb("b_s1", [128, 16], F32)
    b.a_ssq = b.sb("b_ssq", [128, 16], F32)
    b.a_mcol = b.sb("b_mcol", [128, 16], F32)
    b.a_vcol = b.sb("b_vcol", [128, 16], F32)
    b.a_rcol = b.sb("b_rcol", [128, 16], F32)
    b.a_sqscr = b.sb("b_sqscr", [128, D], BF16)
    b.a_ecol = b.sb("b_ecol", [128, 1], F32)
    ya = b.sb("b_ya", [128, 2, D], F32)

    pp = b.ps("b_pp", [128, D])
    pp2 = b.ps("b_pp2", [128, D])
    pp3 = b.ps("b_pp3", [128, D])
    pr = b.ps("b_pr", [128, 16])
    pr2 = b.ps("b_pr2", [128, 16])

    pool.memset(ones1[:], 1.0)
    pool.memset(b.a_ecol[:], EPS)
    pool.memset(gones[:], 1.0)
    pn = ep.n
    for dst, src in [(id_sb[:], io["ident"]), (iota_bc[:], io["iota16"]),
                     (iotamB_bc[:], io["iotamB16"]),
                     (ebrow[:], io["expert_bias"].unsqueeze(0))]:
        sp.dma_start(out=dst, in_=src).then_inc(b.cdma.h, 16)
        b.cdma.n += 16
    pool.dma_start(out=gw_sb[:], in_=io["gate_wT"].rearrange("(kc p) n -> p kc n", p=128)
                   ).then_inc(b.pdma.h, 16)
    b.pdma.n += 16
    gw_loaded = b.pdma.n
    pe.wait_ge(b.cdma.h, b.cdma.n)
    pe.wait_ge(ep.h, pn)
    last = pe.matmul(pr[:], ones1[:], ebrow[:], start=True, stop=True)
    eb_mm = mm.inc(last)
    dve.wait_ge(mm.h, eb_mm)
    dve.tensor_copy(ebias_bc[:], pr[:])
    dve.wait_ge(b.cdma.h, b.cdma.n)
    ev_const = ev.n

    worder = [("s", None)] + [("e", e) for e in range(NE)]
    loaded = {}
    compute_done = {}

    def issue_weight_load(idx):
        kind, e = worder[idx]
        par = idx % 2
        ws = b.wd[par]
        if idx >= 2:
            pool.wait_ge(mm.h, compute_done[idx - 2])
        for j, nm in enumerate(["w1", "w3", "w2"]):
            src = io["s" + nm + "T"] if kind == "s" else io[nm + "T"][e]
            pool.dma_start(out=wsl[par][j][:],
                           in_=src.rearrange("(kc p) n -> p kc n", p=128)
                           ).then_inc(ws.h, 16)
            ws.n += 16
        loaded[idx] = ws.n

    free_pp = ev_const
    free_pp2 = ev_const
    free_pr_ev = [ev.n, ev.n]

    for c in range(NCH):
        pool.wait_ge(ev.h, ev.n)
        pool.wait_ge(ea.h, ea.n)
        pool.memset(acc[:], 0.0)
        acc_zero_ep = ep.n

        xt_free = [b.ev.n, b.ev.n]
        ptv = [pp[:, 0:64].bitcast(BF16), pp2[:, 0:64].bitcast(BF16)]
        free_pt = [free_pp, free_pp2]
        for i in range(NT):
            k = i % 2
            xs = b.xdma if k == 0 else b.xdma2
            sp.wait_ge(ev.h, xt_free[k])
            sp.dma_start(out=xt[:, k, :], in_=io["x1"][c * CH + i * 128:c * CH + (i + 1) * 128, :]
                         ).then_inc(xs.h, 16)
            xs.n += 16
            dve.wait_ge(xs.h, xs.n)
            act.wait_ge(xs.h, xs.n)
            _ln_stats(b, xt[:, k, :], i % 8)
            dve.wait_ge(ea.h, ea.n)
            _ln_mr(b, slice(i % 8, i % 8 + 1))
            dve.tensor_scalar(xn[:, i, :], xt[:, k, :], b.a_mcol[:, i % 8:i % 8 + 1],
                              b.a_rcol[:, i % 8:i % 8 + 1], ALU.subtract, ALU.mult)
            xt_free[k] = ev.n
            ev_xn_i = ev.n
            for fc in range(8):
                kk = (i * 8 + fc) % 2
                pe.wait_ge(ev.h, max(free_pt[kk], ev_xn_i))
                t_mm = mm.inc(pe.transpose(ptv[kk], xn[:, i, fc * 128:(fc + 1) * 128],
                                           id_sb[:]))
                dve.wait_ge(mm.h, t_mm)
                dve.tensor_copy(xnT[:, fc, i * 128:(i + 1) * 128], ptv[kk])
                free_pt[kk] = ev.n
        free_pp, free_pp2 = free_pt

        issue_weight_load(0)
        issue_weight_load(1)

        pe.wait_ge(ev.h, ev.n)
        pe.wait_ge(b.pdma.h, gw_loaded)
        for i in range(NT):
            pru = pr if i % 2 == 0 else pr2
            pe.wait_ge(ev.h, free_pr_ev[i % 2])
            for kc in range(8):
                last = pe.matmul(pru[:], xnT[:, kc, i * 128:(i + 1) * 128],
                                 gw_sb[:, kc, :], start=(kc == 0), stop=(kc == 7))
            r_mm = mm.inc(last)
            dve.wait_ge(mm.h, r_mm)
            dve.tensor_copy(logits[:, i, :], pru[:])
            free_pr_ev[i % 2] = ev.n
            dve.tensor_tensor(bwork[:], logits[:, i, :], ebias_bc[:], ALU.add)
            dve.memset(selm[:], 0.0)
            for kk in range(4):
                dve.tensor_reduce(den[:], bwork[:], axis=AX.X, op=ALU.max)
                dve.tensor_scalar(t1[:], bwork[:], den[:], None, ALU.is_equal)
                dve.tensor_tensor(t2[:], t1[:], iotamB_bc[:], ALU.mult)
                dve.tensor_scalar_add(t2[:], t2[:], BIG)
                dve.tensor_reduce(den[:], t2[:], axis=AX.X, op=ALU.min)
                dve.tensor_scalar(t1[:], iota_bc[:], den[:], None, ALU.is_equal)
                dve.tensor_tensor(selm[:], selm[:], t1[:], ALU.add)
                dve.scalar_tensor_tensor(bwork[:], t1[:], -BIG, bwork[:], ALU.mult, ALU.add)
            dve.tensor_scalar_add(t1[:], logits[:, i, :], -BIG)
            dve.scalar_tensor_tensor(lmrow[:], selm[:], BIG, t1[:], ALU.mult, ALU.add)
            dve.tensor_reduce(den[:], lmrow[:], axis=AX.X, op=ALU.max)
            dve.tensor_scalar_mul(den[:], den[:], -1.0)
            act.wait_ge(ev.h, ev.n)
            act.activation(g_all[:, i, :], lmrow[:], AF.Exp, bias=den[:],
                           accum_out=b.a_vcol[:, 15:16])
            dve.wait_ge(ea.h, ea.n)
            dve.reciprocal(den[:], b.a_vcol[:, 15:16])
            dve.tensor_scalar(g_all[:, i, :], g_all[:, i, :], den[:], None, ALU.mult)

        dve.wait_ge(ep.h, acc_zero_ep)
        R = [pp, pp2, pp3]
        buf_free = [(ev, free_pp), (ev, free_pp2), (ev, ev.n)]
        for idx in range(len(worder)):
            kind, e = worder[idx]
            par = idx % 2
            pe.wait_ge(b.wd[par].h, loaded[idx])
            w1s, w3s, w2s = wsl[par]
            for m in range(8):
                ja = (2 * m) % 3
                jb = (2 * m + 1) % 3
                sm, cnt = buf_free[ja]
                pe.wait_ge(sm.h, cnt)
                for kc in range(8):
                    for half in range(2):
                        sl = slice(half * 512, (half + 1) * 512)
                        last = pe.matmul(R[ja][:, sl], w1s[:, kc, m * 128:(m + 1) * 128],
                                         xnT[:, kc, sl], start=(kc == 0), stop=(kc == 7))
                h1_mm = mm.inc(last)
                sm, cnt = buf_free[jb]
                pe.wait_ge(sm.h, cnt)
                for kc in range(8):
                    for half in range(2):
                        sl = slice(half * 512, (half + 1) * 512)
                        last = pe.matmul(R[jb][:, sl], w3s[:, kc, m * 128:(m + 1) * 128],
                                         xnT[:, kc, sl], start=(kc == 0), stop=(kc == 7))
                h3_mm = mm.inc(last)
                act.wait_ge(mm.h, h1_mm)
                act.wait_ge(ev.h, ev.n)
                act.activation(h1s[:], R[ja][:], AF.Silu)
                buf_free[ja] = (ea, ea.n)
                dve.wait_ge(mm.h, h3_mm)
                dve.wait_ge(ea.h, ea.n)
                dve.scalar_tensor_tensor(h_sb[:, m, :], R[jb][:], 1.0, h1s[:],
                                         ALU.mult, ALU.mult)
                buf_free[jb] = (ev, ev.n)
            ev_h = ev.n
            pe.wait_ge(ev.h, ev_h)
            for i in range(NT):
                k = i % 2
                sm, cnt = buf_free[k]
                pe.wait_ge(sm.h, cnt)
                for ic in range(8):
                    for half in range(2):
                        sl = slice(half * 512, (half + 1) * 512)
                        last = pe.matmul(R[k][:, sl], h_sb[:, ic, i * 128:(i + 1) * 128],
                                         w2s[:, ic, sl], start=(ic == 0), stop=(ic == 7))
                d_mm = mm.inc(last)
                dve.wait_ge(mm.h, d_mm)
                gcol = gones[:, 0:1] if kind == "s" else g_all[:, i, e:e + 1]
                dve.scalar_tensor_tensor(acc[:, i, :], R[k][:], gcol, acc[:, i, :],
                                         ALU.mult, ALU.add)
                buf_free[k] = (ev, ev.n)
            compute_done[idx] = mm.n
            if idx + 2 < len(worder):
                issue_weight_load(idx + 2)
        free_pp = _free_as_ev(b, buf_free[0])
        free_pp2 = _free_as_ev(b, buf_free[1])

        act.wait_ge(ev.h, ev.n)
        xt_free = [b.ev.n, b.ev.n]
        ya_free = [b.out.n, b.out2.n]
        for i in range(NT):
            k = i % 2
            xs = b.xdma if k == 0 else b.xdma2
            sp.wait_ge(ev.h, xt_free[k])
            sp.dma_start(out=xt[:, k, :], in_=io["x1"][c * CH + i * 128:c * CH + (i + 1) * 128, :]
                         ).then_inc(xs.h, 16)
            xs.n += 16
            _ln_stats(b, acc[:, i, :], 8 + (i % 2))
            dve.wait_ge(ea.h, ea.n)
            _ln_mr(b, slice(8 + (i % 2), 9 + (i % 2)))
            dve.wait_ge((b.out if k == 0 else b.out2).h, ya_free[k])
            dve.wait_ge(xs.h, xs.n)
            dve.tensor_scalar(ya[:, k, :], acc[:, i, :], b.a_mcol[:, 8 + i % 2:9 + i % 2],
                              b.a_rcol[:, 8 + i % 2:9 + i % 2], ALU.subtract, ALU.mult)
            dve.tensor_tensor(ya[:, k, :], ya[:, k, :], xt[:, k, :], ALU.add)
            xt_free[k] = ev.n
            os_ = b.out if k == 0 else b.out2
            sp.wait_ge(ev.h, ev.n)
            sp.dma_start(out=io["x2"][c * CH + i * 128:c * CH + (i + 1) * 128, :],
                         in_=ya[:, k, :]).then_inc(os_.h, 16)
            os_.n += 16
            ya_free[k] = os_.n

    sp.wait_ge(b.out.h, b.out.n)
    sp.wait_ge(b.out2.h, b.out2.n)


def hostprep_b(inputs):
    iota = np.tile(np.arange(16, dtype=np.float32), (128, 1))
    return dict(
        gate_wT=np.ascontiguousarray(np.asarray(inputs["L0_ffn_gate_w"]).T),
        expert_bias=np.asarray(inputs["L0_ffn_expert_bias"]),
        w1T=np.ascontiguousarray(np.asarray(inputs["L0_ffn_w1"]).transpose(0, 2, 1)),
        w2T=np.ascontiguousarray(np.asarray(inputs["L0_ffn_w2"]).transpose(0, 2, 1)),
        w3T=np.ascontiguousarray(np.asarray(inputs["L0_ffn_w3"]).transpose(0, 2, 1)),
        sw1T=np.ascontiguousarray(np.asarray(inputs["L0_ffn_sw1"]).T),
        sw2T=np.ascontiguousarray(np.asarray(inputs["L0_ffn_sw2"]).T),
        sw3T=np.ascontiguousarray(np.asarray(inputs["L0_ffn_sw3"]).T),
        iota16=iota, iotamB16=np.ascontiguousarray(iota - BIG),
        ident=np.eye(128, dtype=BF))


B_IO = [("gate_wT", [D, 16], F32), ("expert_bias", [16], F32),
        ("w1T", [NE, D, D], F32), ("w2T", [NE, D, D], F32), ("w3T", [NE, D, D], F32),
        ("sw1T", [D, D], F32), ("sw2T", [D, D], F32), ("sw3T", [D, D], F32),
        ("iota16", [128, 16], F32), ("iotamB16", [128, 16], F32),
        ("ident", [128, 128], BF16)]


def build_b_program():
    b = B()
    io = {n: b.dram(n, s, d, "ExternalInput") for n, s, d in B_IO}
    io["x1"] = b.dram("x1", [T, D], F32, "ExternalInput")
    io["x2"] = b.dram("x2", [T, D], F32, "ExternalOutput")
    with b.ctx:
        blk = b.ctx.enter_context(b.nc.Block())

        @blk.gpsimd
        def _(g):
            build_stage_b(b, io)
    return b.nc




def build_stage_c(b, io, n_cores=8):
    CH = 1024
    NCH = T // CH
    NT = CH // 128
    nc = b.nc
    pe, act, dve, pool, sp = b.pe, b.act, b.dve, b.pool, b.sp
    dma, mm, ea, ev, ep = b.dma, b.mm, b.ea, b.ev, b.ep

    qT_h = io["qT_h"]
    kT_h = io["kT_h"]
    v_h = io["v_h"]
    cc_in = io["cc_in"]
    cc_out = io["cc_out"]

    energy = b.sb("c_energy", [16, T], F32)
    mask_all = b.sb("c_mask", [16, T], BF16)
    ones1c = b.sb("c_ones1c", [1, 128], F32)
    mk16 = b.sb("c_mk16", [16, 1], F32)
    qstab = b.sb("c_qstab", [16, T], BF16)
    thr = b.sb("c_thr", [1, 1], F32)
    thr16 = b.sb("c_thr16", [16, 1], F32)
    srow = b.sb("c_srow", [16, 1], F32)
    s11 = b.sb("c_s11", [1, 1], F32)
    ones2 = b.sb("c_ones2", [128, 16], F32)
    selmat = b.sb("c_selmat", [128, 8, 16], BF16)
    eh64 = b.sb("c_eh64", [16, NH * 64], BF16)
    ctx_fm = b.sb("c_ctx", [128, 8, T], BF16)
    id_sb = b.sb("c_id", [128, 128], BF16)
    b.a_s1 = b.sb("c_s1", [128, 16], F32)
    b.a_ssq = b.sb("c_ssq", [128, 16], F32)
    b.a_mcol = b.sb("c_mcol", [128, 16], F32)
    b.a_vcol = b.sb("c_vcol", [128, 16], F32)
    b.a_rcol = b.sb("c_rcol", [128, 16], F32)
    b.a_sqscr = b.sb("c_sqscr", [128, D], BF16)
    b.a_ecol = b.sb("c_ecol", [128, 1], F32)

    pool.memset(ones2[:], 1.0)
    pool.memset(ones1c[:], 1.0)
    pool.memset(b.a_ecol[:], EPS)
    pool.memset(mk16[:], 0.0)
    pn0 = ep.n
    for dst, src in [(id_sb[:], io["ident"]), (selmat[:], io["selmat"]),
                     (eh64[:], io["eh64"])]:
        sp.dma_start(out=dst, in_=src).then_inc(b.cdma.h, 16)
        b.cdma.n += 16
    c_const = b.cdma.n

    with ExitStack() as c1ctx:
        sb1 = lambda n, sh, dt: c1ctx.enter_context(nc.sbuf_tensor(n, sh, dt))
        xt = sb1("c_xt", [128, 2, D], F32)
        xn = sb1("c_xn", [128, NT, D], BF16)
        xnT = sb1("c_xnT", [128, 8, CH], BF16)
        wq_sb = sb1("c_wq", [128, 8, D], BF16)
        wk_sb = sb1("c_wk", [128, 8, D], BF16)
        wv_sb = sb1("c_wv", [128, 8, D], BF16)
        bq_sb = sb1("c_bq", [128, 8], F32)
        bk_sb = sb1("c_bk", [128, 8], F32)
        bvrow = sb1("c_bvrow", [1, D], F32)
        bv_bc = sb1("c_bvbc", [128, D], F32)
        stage = sb1("c_stage", [128, 2, D], BF16)
        sqs = sb1("c_sqs", [128, CH], BF16)
        ek_scr = sb1("c_ekscr", [16, CH], BF16)
        ekcol = sb1("c_ekcol", [16, 1], F32)

        ones1 = sb1("c_ones1", [1, 128], F32)
        with ExitStack() as p1ctx:
            pp = p1ctx.enter_context(nc.psum_tensor("c_pp", [128, D], F32))
            pp2 = p1ctx.enter_context(nc.psum_tensor("c_pp2", [128, D], F32))
            pes = p1ctx.enter_context(nc.psum_tensor("c_pes", [16, CH], F32))

            pool.memset(ones1[:], 1.0)
            pn = ep.n
            for dst, src in [(wq_sb[:], io["wqT1"]), (wk_sb[:], io["wkT1"]),
                             (wv_sb[:], io["wvT1"])]:
                pool.dma_start(out=dst, in_=src.rearrange("(kc p) n -> p kc n", p=128)
                               ).then_inc(b.pdma.h, 16)
                b.pdma.n += 16
            for dst, src in [(bq_sb[:], io["bq1"]), (bk_sb[:], io["bk1"]),
                             (bvrow[:], io["bv1"].unsqueeze(0))]:
                sp.dma_start(out=dst, in_=src).then_inc(b.cdma.h, 16)
                b.cdma.n += 16
            w1_loaded = b.pdma.n
            pe.wait_ge(b.cdma.h, b.cdma.n)
            pe.wait_ge(ep.h, pn)
            for half in range(2):
                sl = slice(half * 512, (half + 1) * 512)
                last = pe.matmul(pp[:, sl], ones1[:], bvrow[:, sl], start=True, stop=True)
            bv_mm = mm.inc(last)
            dve.wait_ge(mm.h, bv_mm)
            dve.tensor_copy(bv_bc[:], pp[:])
            free_pp = ev.n
            free_pp2 = ev.n
            pe.wait_ge(b.pdma.h, w1_loaded)

            for c in range(NCH):
                xt_free = [ev.n, ev.n]
                ptv = [pp[:, 0:64].bitcast(BF16), pp2[:, 0:64].bitcast(BF16)]
                free_pt = [free_pp, free_pp2]
                for i in range(NT):
                    k = i % 2
                    xs = b.xdma if k == 0 else b.xdma2
                    sp.wait_ge(ev.h, xt_free[k])
                    sp.dma_start(out=xt[:, k, :],
                                 in_=io["x2"][c * CH + i * 128:c * CH + (i + 1) * 128, :]
                                 ).then_inc(xs.h, 16)
                    xs.n += 16
                    dve.wait_ge(xs.h, xs.n)
                    act.wait_ge(xs.h, xs.n)
                    _ln_stats(b, xt[:, k, :], i)
                    dve.wait_ge(ea.h, ea.n)
                    _ln_mr(b, slice(i, i + 1))
                    dve.tensor_scalar(xn[:, i, :], xt[:, k, :], b.a_mcol[:, i:i + 1],
                                      b.a_rcol[:, i:i + 1], ALU.subtract, ALU.mult)
                    xt_free[k] = ev.n
                    ev_xn_i = ev.n
                    for fc in range(8):
                        kk = (i * 8 + fc) % 2
                        pe.wait_ge(ev.h, max(free_pt[kk], ev_xn_i))
                        t_mm = mm.inc(pe.transpose(ptv[kk], xn[:, i, fc * 128:(fc + 1) * 128],
                                                   id_sb[:]))
                        dve.wait_ge(mm.h, t_mm)
                        dve.tensor_copy(xnT[:, fc, i * 128:(i + 1) * 128], ptv[kk])
                        free_pt[kk] = ev.n
                free_pp, free_pp2 = free_pt
                pe.wait_ge(ev.h, ev.n)

                for wsb, bcol, isq, hdr in [(wq_sb, bq_sb, True, qT_h),
                                            (wk_sb, bk_sb, False, kT_h)]:
                    st_free = [b.out.n, b.out2.n]
                    for m in range(8):
                        k = m % 2
                        ppu = pp if k == 0 else pp2
                        pe.wait_ge(ev.h, free_pp if k == 0 else free_pp2)
                        for kc in range(8):
                            for half in range(2):
                                sl = slice(half * 512, (half + 1) * 512)
                                last = pe.matmul(ppu[:, sl],
                                                 wsb[:, kc, m * 128:(m + 1) * 128],
                                                 xnT[:, kc, sl],
                                                 start=(kc == 0), stop=(kc == 7))
                        q_mm = mm.inc(last)
                        dve.wait_ge(mm.h, q_mm)
                        os_ = b.out if k == 0 else b.out2
                        dve.wait_ge(os_.h, st_free[k])
                        dve.tensor_scalar(stage[:, k, :], ppu[:], bcol[:, m:m + 1], None,
                                          ALU.add)
                        if k == 0:
                            free_pp = ev.n
                        else:
                            free_pp2 = ev.n
                        act.wait_ge(ev.h, ev.n)
                        act.activation(sqs[:], stage[:, k, :], AF.Square)
                        pe.wait_ge(ea.h, ea.n)
                        for half in range(2):
                            sl = slice(half * 512, (half + 1) * 512)
                            last = pe.matmul(pes[:, sl], selmat[:, m, :], sqs[:, sl],
                                             start=(m == 0), stop=(m == 7))
                        es_mm = mm.inc(last)
                        sp.wait_ge(ev.h, ev.n)
                        sp.dma_start(out=hdr[m * 128:(m + 1) * 128, c * CH:(c + 1) * CH],
                                     in_=stage[:, k, :]).then_inc(os_.h, 16)
                        os_.n += 16
                        st_free[k] = os_.n
                    act.wait_ge(mm.h, mm.n)
                    act.wait_ge(ep.h, pn0)
                    if isq:
                        act.activation(energy[:, c * CH:(c + 1) * CH], pes[:], AF.Sqrt)
                    else:
                        act.activation(ek_scr[:], pes[:], AF.Sqrt)
                        dve.wait_ge(ea.h, ea.n)
                        dve.tensor_reduce(ekcol[:], ek_scr[:], axis=AX.X, op=ALU.max)
                        dve.tensor_tensor(mk16[:], mk16[:], ekcol[:], ALU.max)
                    pe.wait_ge(ea.h, ea.n)
                st_free = [b.out.n, b.out2.n]
                for i in range(NT):
                    k = i % 2
                    ppu = pp if k == 0 else pp2
                    pe.wait_ge(ev.h, free_pp if k == 0 else free_pp2)
                    for kc in range(8):
                        for half in range(2):
                            sl = slice(half * 512, (half + 1) * 512)
                            last = pe.matmul(ppu[:, sl], xnT[:, kc, i * 128:(i + 1) * 128],
                                             wv_sb[:, kc, sl], start=(kc == 0), stop=(kc == 7))
                    v_mm = mm.inc(last)
                    dve.wait_ge(mm.h, v_mm)
                    os_ = b.out if k == 0 else b.out2
                    dve.wait_ge(os_.h, st_free[k])
                    dve.scalar_tensor_tensor(stage[:, k, :], ppu[:], 1.0, bv_bc[:],
                                             ALU.mult, ALU.add)
                    if k == 0:
                        free_pp = ev.n
                    else:
                        free_pp2 = ev.n
                    sp.wait_ge(ev.h, ev.n)
                    sp.dma_start(out=v_h[c * CH + i * 128:c * CH + (i + 1) * 128, :],
                                 in_=stage[:, k, :]).then_inc(os_.h, 16)
                    os_.n += 16
                    st_free[k] = os_.n

    dve.wait_ge(ea.h, ea.n)
    dve.wait_ge(b.cdma.h, b.cdma.n)
    dve.tensor_reduce(srow[:], energy[:], axis=AX.X, op=ALU.add)
    with ExitStack() as ptx:
        p11 = ptx.enter_context(nc.psum_tensor("c_p11", [128, 16], F32))
        pe.wait_ge(ev.h, ev.n)
        pe.wait_ge(ep.h, pn0)
        l_mm = mm.inc(pe.matmul(p11[0:1, 0:1], srow[:], ones2[0:16, 0:1],
                                start=True, stop=True))
        dve.wait_ge(mm.h, l_mm)
        dve.tensor_copy(s11[:], p11[0:1, 0:1])
        pool.wait_ge(ev.h, ev.n)
        pool.dma_start(out=cc_in, in_=s11[:]).then_inc(b.pdma.h, 16)
        b.pdma.n += 16
        pool.wait_ge(b.pdma.h, b.pdma.n)
        pool.collective_compute("AllReduce", ALU.add,
                                replica_groups=[list(range(n_cores))],
                                ins=[cc_in.opt()], outs=[cc_out.opt()]).then_inc(b.ep.h, 1)
        b.ep.n += 1
        pool.wait_ge(b.ep.h, b.ep.n)
        pool.dma_start(out=thr[:], in_=cc_out).then_inc(b.pdma.h, 16)
        b.pdma.n += 16
        dve.wait_ge(b.pdma.h, b.pdma.n)
        dve.tensor_scalar_mul(thr[:], thr[:], 0.1 / (n_cores * T * NH))
        ev_thr = ev.n
        pe.wait_ge(ev.h, ev_thr)
        t_mm = mm.inc(pe.matmul(p11[0:16, 0:1], ones2[0:1, :], thr[:], start=True, stop=True))
        dve.wait_ge(mm.h, t_mm)
        dve.tensor_copy(thr16[:], p11[0:16, 0:1])
        dve.tensor_scalar(mask_all[:], energy[:], thr16[:], None, ALU.is_gt)
        dve.tensor_scalar_mul(ekcol2 := b.a_vcol[0:16, 14:15], mk16[:], -1.0)
        dve.tensor_tensor(energy[:], energy[:], mask_all[:], ALU.mult)
        dve.tensor_scalar(qstab[:], energy[:], b.a_vcol[0:16, 14:15], None, ALU.mult)
        ev_mask = ev.n

    KTG = 2
    with ExitStack() as c2ctx:
        sb2 = lambda n, sh, dt: c2ctx.enter_context(nc.sbuf_tensor(n, sh, dt))
        qh = [sb2(f"c_qh{p}", [64, T], BF16) for p in range(2)]
        kh = [sb2(f"c_kh{p}", [65, T], BF16) for p in range(2)]
        vh = [sb2(f"c_vh{p}", [128, 32, 65], BF16) for p in range(2)]
        qm = [sb2(f"c_qm{p}", [65, 512], BF16) for p in range(2)]
        phat = [sb2(f"c_ph{p}", [128, KTG * 512], BF16) for p in range(2)]
        rrow = sb2("c_rrow", [128, 512], F32)
        recipb = sb2("c_recipb", [64, 512], BF16)
        onesc = sb2("c_onesc", [128, 128], F32)
        with ExitStack() as p2ctx:
            psc2 = [p2ctx.enter_context(nc.psum_tensor(f"c_psc{p}", [128, KTG * 512], F32))
                    for p in range(2)]
            pctx = [p2ctx.enter_context(nc.psum_tensor(f"c_pctx{p}", [65, 512], F32))
                    for p in range(2)]
            pmb = [p2ctx.enter_context(nc.psum_tensor(f"c_pmb{p}", [128, 512], F32))
                   for p in range(2)]

            pool.wait_ge(ev.h, ev.n)
            pool.wait_ge(ea.h, ea.n)
            pool.wait_ge(mm.h, mm.n)
            pool.memset(onesc[:], 1.0)
            pool.memset(vh[0][:], 1.0)
            pool.memset(vh[1][:], 1.0)
            pool.memset(kh[0][:], 1.0)
            pool.memset(kh[1][:], 1.0)
            pn2 = ep.n
            hd_sem = [b.hd[0], b.hd[1]]
            h_loaded = {}
            h_done = {}

            def load_head(h):
                par = h % 2
                ws = hd_sem[par]
                if h >= 2:
                    sp.wait_ge(mm.h, h_done[h - 2])
                sp.dma_start(out=qh[par][:], in_=qT_h[h * 64:(h + 1) * 64, :]
                             ).then_inc(ws.h, 16)
                sp.dma_start(out=kh[par][0:64, :], in_=kT_h[h * 64:(h + 1) * 64, :]
                             ).then_inc(ws.h, 16)
                sp.dma_start(out=vh[par][:, :, 0:64],
                             in_=v_h[:, h * 64:(h + 1) * 64]
                             .rearrange("(kt p) d -> p kt d", p=128)).then_inc(ws.h, 16)
                ws.n += 48
                h_loaded[h] = ws.n

            sp.wait_ge(b.out.h, b.out.n)
            sp.wait_ge(b.out2.h, b.out2.n)
            sp.wait_ge(ep.h, pn2)
            load_head(0)
            load_head(1)
            free_phat = [mm.n, mm.n]
            free_psc_ea = [ea.n, ea.n]
            free_pctx = [ev.n, ev.n]
            qmf = [mm.n, mm.n]
            free_pmb = [ev.n, ev.n]
            ev_qm_s = [ev.n, ev.n]
            ngrp = 32 // KTG
            pending = None
            act.wait_ge(ev.h, ev_mask)
            pairs = [(h, qt) for h in range(NH) for qt in range(8)]

            def emit_prologue(idx):
                ph, pqt = pairs[idx]
                ppar = ph % 2
                qs = pqt % 2
                if pqt == 0:
                    pe.wait_ge(b.hd[ppar].h, h_loaded[ph])
                    pe.wait_ge(ev.h, ev_mask)
                    pe.wait_ge(ep.h, pn2)
                pe.wait_ge(ev.h, free_pmb[qs])
                pe.matmul(pmb[qs][0:64, :], eh64[:, ph * 64:(ph + 1) * 64],
                          mask_all[:, pqt * 512:(pqt + 1) * 512],
                          start=True, stop=True)
                mb_mm = mm.inc(pe.matmul(pmb[qs][64:65, :], id_sb[0:16, ph:ph + 1],
                                         qstab[:, pqt * 512:(pqt + 1) * 512],
                                         start=True, stop=True))
                dve.wait_ge(mm.h, mb_mm)
                dve.wait_ge(mm.h, qmf[qs])
                dve.tensor_tensor(qm[qs][0:64, :],
                                  qh[ppar][:, pqt * 512:(pqt + 1) * 512],
                                  pmb[qs][0:64, :], ALU.mult)
                dve.tensor_copy(qm[qs][64:65, :], pmb[qs][64:65, :])
                free_pmb[qs] = ev.n
                ev_qm_s[qs] = ev.n

            def emit_recip(ph, pqt, pctx_mm):
                pqs = pqt % 2
                dve.wait_ge(mm.h, pctx_mm)
                dve.reciprocal(rrow[64:65, :], pctx[pqs][64:65, :])
                return ev.n

            def emit_normalize(ph, pqt, pctx_mm, ev_rec):
                pqs = pqt % 2
                pe.wait_ge(ev.h, ev_rec)
                pe.wait_ge(ev.h, free_pmb[pqs])
                r_mm = mm.inc(pe.matmul(pmb[pqs][0:64, :], onesc[64:65, 0:64],
                                        rrow[64:65, :], start=True, stop=True))
                dve.wait_ge(mm.h, r_mm)
                dve.tensor_copy(recipb[:], pmb[pqs][0:64, :])
                free_pmb[pqs] = ev.n
                pfc, pr0 = ph // 2, 64 * (ph % 2)
                dve.scalar_tensor_tensor(
                    ctx_fm[pr0:pr0 + 64, pfc, pqt * 512:(pqt + 1) * 512],
                    pctx[pqs][0:64, :], 1.0, recipb[:], ALU.mult, ALU.mult)
                free_pctx[pqs] = ev.n

            emit_prologue(0)
            for idx, (h, qt) in enumerate(pairs):
                par = h % 2
                qs = qt % 2

                def emit_sc(g):
                    sl_par = g % 2
                    for j, kt in enumerate(range(g * KTG, (g + 1) * KTG)):
                        last = pe.matmul(psc2[sl_par][:, j * 512:(j + 1) * 512],
                                         kh[par][:, kt * 128:(kt + 1) * 128],
                                         qm[qs][:], start=True, stop=True)
                    return mm.inc(last)

                sc_mm = [0] * ngrp
                if pending is not None:
                    pend_rec = emit_recip(*pending)
                pe.wait_ge(ev.h, ev_qm_s[qs])
                pe.wait_ge(ea.h, free_psc_ea[0])
                sc_mm[0] = emit_sc(0)
                for g in range(ngrp):
                    sl_par = g % 2
                    if g + 1 < ngrp:
                        pe.wait_ge(ea.h, free_psc_ea[(g + 1) % 2])
                        sc_mm[g + 1] = emit_sc(g + 1)
                    if g == 1 and pending is not None:
                        emit_normalize(*pending, pend_rec)
                        pending = None
                    if g == 3 and idx + 1 < len(pairs):
                        emit_prologue(idx + 1)
                    act.wait_ge(mm.h, sc_mm[g])
                    act.wait_ge(mm.h, free_phat[sl_par])
                    act.activation(phat[sl_par][:], psc2[sl_par][:],
                                   AF.Exp, scale=0.125)
                    free_psc_ea[sl_par] = ea.n
                    pe.wait_ge(ea.h, free_psc_ea[sl_par])
                    if g == 0:
                        pe.wait_ge(ev.h, free_pctx[qs])
                    for j, kt in enumerate(range(g * KTG, (g + 1) * KTG)):
                        last = pe.matmul(pctx[qs][:], vh[par][:, kt, :],
                                         phat[sl_par][:, j * 512:(j + 1) * 512],
                                         start=(kt == 0), stop=(kt == 31))
                    free_phat[sl_par] = mm.inc(last)
                qmf[qs] = sc_mm[ngrp - 1]
                pending = (h, qt, free_phat[(ngrp - 1) % 2])
                if qt == 7:
                    h_done[h] = mm.n
                    if h + 2 < NH:
                        load_head(h + 2)
            emit_normalize(*pending, emit_recip(*pending))
            pending = None

    with ExitStack() as c3ctx:
        sb3 = lambda n, sh, dt: c3ctx.enter_context(nc.sbuf_tensor(n, sh, dt))
        wo_sb = sb3("c_wo", [128, 8, D], BF16)
        borow = sb3("c_borow", [1, D], F32)
        bo_bc = sb3("c_bobc", [128, D], F32)
        xt = sb3("c_xt3", [128, 2, D], F32)
        ya = sb3("c_ya3", [128, 2, D], F32)
        yb = sb3("c_yb3", [128, D], F32)
        ones1b = sb3("c_ones1b", [1, 128], F32)
        with ExitStack() as p3ctx:
            pp = p3ctx.enter_context(nc.psum_tensor("c3_pp", [128, D], F32))
            pp2 = p3ctx.enter_context(nc.psum_tensor("c3_pp2", [128, D], F32))
            pool.wait_ge(ev.h, ev.n)
            pool.wait_ge(ea.h, ea.n)
            pool.wait_ge(mm.h, mm.n)
            pool.memset(ones1b[:], 1.0)
            pn3 = ep.n
            pool.dma_start(out=wo_sb[:], in_=io["woT1"].rearrange("(kc p) n -> p kc n", p=128)
                           ).then_inc(b.pdma.h, 16)
            b.pdma.n += 16
            sp.wait_ge(mm.h, mm.n)
            sp.wait_ge(ev.h, ev.n)
            sp.wait_ge(ea.h, ea.n)
            sp.dma_start(out=borow[:], in_=io["bo1"].unsqueeze(0)).then_inc(b.cdma.h, 16)
            b.cdma.n += 16
            pe.wait_ge(b.cdma.h, b.cdma.n)
            pe.wait_ge(ep.h, pn3)
            for half in range(2):
                sl = slice(half * 512, (half + 1) * 512)
                last = pe.matmul(pp[:, sl], ones1b[:], borow[:, sl], start=True, stop=True)
            bo_mm = mm.inc(last)
            dve.wait_ge(mm.h, bo_mm)
            dve.tensor_copy(bo_bc[:], pp[:])
            free_pp = ev.n
            free_pp2 = ev.n
            pe.wait_ge(b.pdma.h, b.pdma.n)
            pe.wait_ge(ev.h, ev.n)
            xt_free = [ev.n, ev.n]
            ya_free = [b.out.n, b.out2.n]
            for ti in range(T // 128):
                k = ti % 2
                xs = b.xdma if k == 0 else b.xdma2
                sp.wait_ge(ev.h, xt_free[k])
                sp.dma_start(out=xt[:, k, :], in_=io["x2"][ti * 128:(ti + 1) * 128, :]
                             ).then_inc(xs.h, 16)
                xs.n += 16
                ppu = pp if k == 0 else pp2
                pe.wait_ge(ev.h, free_pp if k == 0 else free_pp2)
                for kc in range(8):
                    for half in range(2):
                        sl = slice(half * 512, (half + 1) * 512)
                        last = pe.matmul(ppu[:, sl], ctx_fm[:, kc, ti * 128:(ti + 1) * 128],
                                         wo_sb[:, kc, sl], start=(kc == 0), stop=(kc == 7))
                o_mm = mm.inc(last)
                dve.wait_ge(mm.h, o_mm)
                dve.wait_ge((b.out if k == 0 else b.out2).h, ya_free[k])
                dve.scalar_tensor_tensor(ya[:, k, :], ppu[:], 1.0, bo_bc[:],
                                         ALU.mult, ALU.add)
                if k == 0:
                    free_pp = ev.n
                else:
                    free_pp2 = ev.n
                act.wait_ge(ev.h, ev.n)
                _ln_stats(b, ya[:, k, :], 8 + k)
                dve.wait_ge(ea.h, ea.n)
                _ln_mr(b, slice(8 + k, 9 + k))
                os_ = b.out if k == 0 else b.out2
                dve.wait_ge(xs.h, xs.n)
                dve.tensor_scalar(yb[:], ya[:, k, :], b.a_mcol[:, 8 + k:9 + k],
                                  b.a_rcol[:, 8 + k:9 + k], ALU.subtract, ALU.mult)
                dve.tensor_tensor(ya[:, k, :], yb[:], xt[:, k, :], ALU.add)
                xt_free[k] = ev.n
                sp.wait_ge(ev.h, ev.n)
                sp.dma_start(out=io["x3"][ti * 128:(ti + 1) * 128, :],
                             in_=ya[:, k, :]).then_inc(os_.h, 16)
                os_.n += 16
                ya_free[k] = os_.n

    sp.wait_ge(b.out.h, b.out.n)
    sp.wait_ge(b.out2.h, b.out2.n)


def hostprep_c(inputs):
    wqkv = np.asarray(inputs["L1_attn_w_qkv"])
    bqkv = np.asarray(inputs["L1_attn_b_qkv"])
    selmat = np.zeros((128, 8, 16), np.float32)
    for m in range(8):
        selmat[0:64, m, 2 * m] = 1.0
        selmat[64:128, m, 2 * m + 1] = 1.0
    eh64 = np.zeros((16, NH * 64), np.float32)
    for h in range(NH):
        eh64[h, h * 64:(h + 1) * 64] = 1.0
    return dict(
        wqT1=np.ascontiguousarray(wqkv[0:D].T),
        wkT1=np.ascontiguousarray(wqkv[D:2 * D].T),
        wvT1=np.ascontiguousarray(wqkv[2 * D:].T),
        bq1=np.ascontiguousarray(bqkv[0:D].reshape(8, 128).T),
        bk1=np.ascontiguousarray(bqkv[D:2 * D].reshape(8, 128).T),
        bv1=np.ascontiguousarray(bqkv[2 * D:]),
        woT1=np.ascontiguousarray(np.asarray(inputs["L1_attn_w_out"]).T),
        bo1=np.asarray(inputs["L1_attn_b_out"]),
        selmat=selmat.astype(BF), eh64=eh64.astype(BF),
        ident=np.eye(128, dtype=BF))


C_IO = [("wqT1", [D, D], F32), ("wkT1", [D, D], F32), ("wvT1", [D, D], F32),
        ("bq1", [128, 8], F32), ("bk1", [128, 8], F32), ("bv1", [D], F32),
        ("woT1", [D, D], F32), ("bo1", [D], F32),
        ("selmat", [128, 8, 16], BF16),
        ("eh64", [16, NH * 64], BF16), ("ident", [128, 128], BF16)]


def build_c_program(n_cores=8):
    b = B()
    io = {n: b.dram(n, s, d, "ExternalInput") for n, s, d in C_IO}
    io["x2"] = b.dram("x2", [T, D], F32, "ExternalInput")
    io["x3"] = b.dram("x3", [T, D], F32, "ExternalOutput")
    io["qT_h"] = b.dram("qT_h", [D, T], BF16, "Internal")
    io["kT_h"] = b.dram("kT_h", [D, T], BF16, "Internal")
    io["v_h"] = b.dram("v_h", [T, D], BF16, "Internal")
    io["cc_in"] = b.dram("cc_in", [1, 1], F32, "Internal")
    io["cc_out"] = b.dram("cc_out", [1, 1], F32, "Internal")
    with b.ctx:
        blk = b.ctx.enter_context(b.nc.Block())

        @blk.gpsimd
        def _(g):
            build_stage_c(b, io, n_cores=n_cores)
    return b.nc




def build_stage_d(b, io):
    nc = b.nc
    pe, act, dve, pool, sp = b.pe, b.act, b.dve, b.pool, b.sp
    dma, mm, ea, ev, ep = b.dma, b.mm, b.ea, b.ev, b.ep
    NTT = T // 128

    swT = b.sb("d_swT", [128, 8, D], BF16)
    id_sb = b.sb("d_id", [128, 128], BF16)
    xt = b.sb("d_xt", [128, 2, D], F32)
    xn = b.sb("d_xn", [128, D], BF16)
    xnT_all = b.sb("d_xnT", [128, 8, T], BF16)
    ya = b.sb("d_ya", [128, 2, D], F32)
    yb = b.sb("d_yb", [128, D], F32)
    b.a_s1 = b.sb("d_s1", [128, 16], F32)
    b.a_ssq = b.sb("d_ssq", [128, 16], F32)
    b.a_mcol = b.sb("d_mcol", [128, 16], F32)
    b.a_vcol = b.sb("d_vcol", [128, 16], F32)
    b.a_rcol = b.sb("d_rcol", [128, 16], F32)
    b.a_sqscr = b.sb("d_sqscr", [128, D], BF16)
    b.a_ecol = b.sb("d_ecol", [128, 1], F32)

    with ExitStack() as pctx:
        pp = pctx.enter_context(nc.psum_tensor("d_pp", [128, D], F32))
        pp2 = pctx.enter_context(nc.psum_tensor("d_pp2", [128, D], F32))

        pool.wait_ge(ev.h, ev.n)
        pool.wait_ge(ea.h, ea.n)
        pool.wait_ge(mm.h, mm.n)
        pool.memset(b.a_ecol[:], EPS)
        pool.dma_start(out=swT[:], in_=io["wT_d"].rearrange("(kc p) n -> p kc n", p=128)
                       ).then_inc(b.pdma.h, 16)
        b.pdma.n += 16
        sp.wait_ge(mm.h, mm.n)
        sp.wait_ge(ev.h, ev.n)
        sp.wait_ge(ea.h, ea.n)
        sp.dma_start(out=id_sb[:], in_=io["ident"]).then_inc(b.cdma.h, 16)
        b.cdma.n += 16
        act.wait_ge(b.pdma.h, b.pdma.n)
        act.activation(swT[:], swT[:], AF.Sign)
        sw_ea = ea.n
        pe.wait_ge(ea.h, sw_ea)
        pe.wait_ge(b.cdma.h, b.cdma.n)

        free_pt = [ev.n, ev.n]
        xt_free = [ev.n, ev.n]
        ptv = [pp[:, 0:64].bitcast(BF16), pp2[:, 0:64].bitcast(BF16)]
        for ti in range(NTT):
            k = ti % 2
            xs = b.xdma if k == 0 else b.xdma2
            sp.wait_ge(ev.h, xt_free[k])
            sp.dma_start(out=xt[:, k, :], in_=io["x3"][ti * 128:(ti + 1) * 128, :]
                         ).then_inc(xs.h, 16)
            xs.n += 16
            dve.wait_ge(xs.h, xs.n)
            act.wait_ge(xs.h, xs.n)
            _ln_stats(b, xt[:, k, :], k)
            dve.wait_ge(ea.h, ea.n)
            _ln_mr(b, slice(k, k + 1))
            dve.wait_ge(mm.h, mm.n)
            dve.tensor_scalar(xn[:], xt[:, k, :], b.a_mcol[:, k:k + 1],
                              b.a_rcol[:, k:k + 1], ALU.subtract, ALU.mult)
            xt_free[k] = ev.n
            ev_xn = ev.n
            pe.wait_ge(ev.h, ev_xn)
            for fc in range(8):
                kk = fc % 2
                pe.wait_ge(ev.h, free_pt[kk])
                t_mm = mm.inc(pe.transpose(ptv[kk], xn[:, fc * 128:(fc + 1) * 128], id_sb[:]))
                dve.wait_ge(mm.h, t_mm)
                dve.tensor_copy(xnT_all[:, fc, ti * 128:(ti + 1) * 128], ptv[kk])
                free_pt[kk] = ev.n

        pe.wait_ge(ev.h, ev.n)
        free_pp = ev.n
        free_pp2 = ev.n
        xt_free = [ev.n, ev.n]
        ya_free = [b.out.n, b.out2.n]
        for ti in range(NTT):
            k = ti % 2
            xs = b.xdma if k == 0 else b.xdma2
            os_ = b.out if k == 0 else b.out2
            sp.wait_ge(ev.h, xt_free[k])
            sp.dma_start(out=xt[:, k, :], in_=io["x3"][ti * 128:(ti + 1) * 128, :]
                         ).then_inc(xs.h, 16)
            xs.n += 16
            ppu = pp if k == 0 else pp2
            pe.wait_ge(ev.h, free_pp if k == 0 else free_pp2)
            for kc in range(8):
                for half in range(2):
                    sl = slice(half * 512, (half + 1) * 512)
                    last = pe.matmul(ppu[:, sl], xnT_all[:, kc, ti * 128:(ti + 1) * 128],
                                     swT[:, kc, sl], start=(kc == 0), stop=(kc == 7))
            y_mm = mm.inc(last)
            dve.wait_ge(mm.h, y_mm)
            dve.wait_ge(os_.h, ya_free[k])
            dve.tensor_copy(ya[:, k, :], ppu[:])
            if k == 0:
                free_pp = ev.n
            else:
                free_pp2 = ev.n
            act.wait_ge(ev.h, ev.n)
            _ln_stats(b, ya[:, k, :], 8 + k)
            dve.wait_ge(ea.h, ea.n)
            _ln_mr(b, slice(8 + k, 9 + k))
            dve.tensor_scalar(yb[:], ya[:, k, :], b.a_mcol[:, 8 + k:9 + k],
                              b.a_rcol[:, 8 + k:9 + k], ALU.subtract, ALU.mult)
            dve.wait_ge(xs.h, xs.n)
            dve.tensor_tensor(ya[:, k, :], yb[:], xt[:, k, :], ALU.add)
            xt_free[k] = ev.n
            sp.wait_ge(ev.h, ev.n)
            sp.dma_start(out=io["out"][ti * 128:(ti + 1) * 128, :],
                         in_=ya[:, k, :]).then_inc(os_.h, 16)
            os_.n += 16
            ya_free[k] = os_.n

    sp.wait_ge(b.out.h, b.out.n)
    sp.wait_ge(b.out2.h, b.out2.n)


def hostprep_d(inputs):
    return dict(wT_d=np.ascontiguousarray(np.asarray(inputs["L1_ffn_weight"]).T),
                ident=np.eye(128, dtype=BF))


D_IO = [("wT_d", [D, D], F32), ("ident", [128, 128], BF16)]


def build_d_program():
    b = B()
    io = {n: b.dram(n, s, d, "ExternalInput") for n, s, d in D_IO}
    io["x3"] = b.dram("x3", [T, D], F32, "ExternalInput")
    io["out"] = b.dram("out", [T, D], F32, "ExternalOutput")
    with b.ctx:
        blk = b.ctx.enter_context(b.nc.Block())

        @blk.gpsimd
        def _(g):
            build_stage_d(b, io)
    return b.nc


_PROG_CACHE = {}


def _get_prog(name, builder):
    if name not in _PROG_CACHE:
        _PROG_CACHE[name] = builder()
    return _PROG_CACHE[name]


def kernel(**inputs):
    return kernel_fused(**inputs)


def kernel_unfused(**inputs):
    x = np.asarray(inputs["L0_attn_x"])
    ncores = 8
    hpa = hostprep_a(inputs)
    hpb = hostprep_b(inputs)
    hpc = hostprep_c(inputs)
    hpd = hostprep_d(inputs)

    nc_a = _get_prog("a", build_a_program)
    in_maps = [dict(hpa, x0=np.ascontiguousarray(x[c])) for c in range(ncores)]
    res = run_bass_kernel_spmd(nc_a, in_maps, core_ids=list(range(ncores)))
    x1 = [np.asarray(res.results[c]["x1"]) for c in range(ncores)]

    nc_b = _get_prog("b", build_b_program)
    in_maps = [dict(hpb, x1=x1[c]) for c in range(ncores)]
    res = run_bass_kernel_spmd(nc_b, in_maps, core_ids=list(range(ncores)))
    x2 = [np.asarray(res.results[c]["x2"]) for c in range(ncores)]

    nc_c = _get_prog("c", build_c_program)
    in_maps = [dict(hpc, x2=x2[c]) for c in range(ncores)]
    res = run_bass_kernel_spmd(nc_c, in_maps, core_ids=list(range(ncores)))
    x3 = [np.asarray(res.results[c]["x3"]) for c in range(ncores)]

    nc_d = _get_prog("d", build_d_program)
    in_maps = [dict(hpd, x3=x3[c]) for c in range(ncores)]
    res = run_bass_kernel_spmd(nc_d, in_maps, core_ids=list(range(ncores)))
    out = np.stack([np.asarray(res.results[c]["out"]) for c in range(ncores)])
    return out



def _stage_barrier(b):
    sems = [b.mm, b.ea, b.ev, b.ep, b.out, b.out2, b.cdma, b.pdma,
            b.xdma, b.xdma2, b.wd[0], b.wd[1], b.hd[0], b.hd[1]]
    engs = [b.pe, b.nc.scalar, b.nc.vector, b.nc.gpsimd, b.sp]
    for e in engs:
        for sm in sems:
            if sm.n:
                e.wait_ge(sm.h, sm.n)


def build_fused_program(n_cores=8):
    b = B()
    io = {}
    for n, sh, dt in A_IO + B_IO + C_IO + D_IO:
        if n not in io:
            io[n] = b.dram(n, sh, dt, "ExternalInput")
    io["out"] = b.dram("out", [T, D], F32, "ExternalOutput")
    for n in ["x1", "x2", "x3"]:
        io[n] = b.dram(n, [T, D], F32, "Internal")
    io["qT_h"] = b.dram("qT_h", [D, T], BF16, "Internal")
    io["kT_h"] = b.dram("kT_h", [D, T], BF16, "Internal")
    io["v_h"] = b.dram("v_h", [T, D], BF16, "Internal")
    io["cc_in"] = b.dram("cc_in", [1, 1], F32, "Internal")
    io["cc_out"] = b.dram("cc_out", [1, 1], F32, "Internal")
    with b.ctx:
        blk = b.ctx.enter_context(b.nc.Block())

        @blk.gpsimd
        def _(g):
            for fn in [build_stage_a, build_stage_b,
                       lambda bb, ii: build_stage_c(bb, ii, n_cores=n_cores),
                       build_stage_d]:
                with ExitStack() as sc:
                    b.cur = sc
                    fn(b, io)
                b.cur = None
                _stage_barrier(b)
    return b.nc


def kernel_fused(**inputs):
    x = np.asarray(inputs["L0_attn_x"])
    ncores = 8
    hp = {}
    hp.update(hostprep_a(inputs))
    hp.update(hostprep_b(inputs))
    hp.update(hostprep_c(inputs))
    hp.update(hostprep_d(inputs))
    nc = _get_prog("fused", build_fused_program)
    in_maps = [dict(hp, x0=np.ascontiguousarray(x[c])) for c in range(ncores)]
    res = run_bass_kernel_spmd(nc, in_maps, core_ids=list(range(ncores)))
    return np.stack([np.asarray(res.results[c]["out"]) for c in range(ncores)])
